# version 06
# baseline (speedup 1.0000x reference)
import numpy as np
import ml_dtypes
from contextlib import ExitStack

import concourse.bass as bass
import concourse.mybir as mybir
from concourse.bass_utils import run_bass_kernel_spmd

F32 = mybir.dt.float32
BF16 = mybir.dt.bfloat16
ALU = mybir.AluOpType
AF = mybir.ActivationFunctionType
AX = mybir.AxisListType

NEG = -1.0e30
D = 1024
DC = D // 128
TFULL = 4096

ENGS = ("sync", "vector", "scalar", "gpsimd", "tensor")
DMA_LANES = {"sync": ("qs0", "qs1", "qs2", "qs3"),
             "gpsimd": ("qg0", "qg1", "qg2", "qg3"),
             "vector": ("qv0",), "scalar": ("qc0",)}
SEM_NAMES = ("qs0", "qs1", "qs2", "qs3", "qg0", "qg1", "qg2", "qg3",
             "qv0", "qc0", "cv", "cs", "cg", "ct")
CMP_SEM = {"vector": "cv", "scalar": "cs", "gpsimd": "cg", "tensor": "ct"}


class G:
    __slots__ = ("w", "r", "name")

    def __init__(self, name=""):
        self.w = None
        self.r = []
        self.name = name


class Rec:
    def __init__(self):
        self.ops = {e: [] for e in ENGS}
        self.cnt = {s: 0 for s in SEM_NAMES}
        self.waited = {e: {s: 0 for s in SEM_NAMES} for e in ENGS}
        self.lane_i = {e: 0 for e in DMA_LANES}
        self.sems = None
        self.nops = 0
        self.fence = []

    def _wait(self, eng, tok, cur_stream, enforce_same_queue=True):
        if tok is None:
            return
        sem, val, stream = tok
        if stream == cur_stream:
            if stream.startswith("c:") or not enforce_same_queue:
                return
        if self.waited[eng][sem] >= val:
            return
        self.waited[eng][sem] = val

        def fn(e, sem=sem, val=val):
            e.wait_ge(self.sems[sem], val)
        self.ops[eng].append(fn)

    PARANOID_COMPUTE = False

    def op(self, eng, fn, reads=(), writes=(), dma=False):
        if dma:
            lanes = DMA_LANES[eng]
            sem = lanes[self.lane_i[eng] % len(lanes)]
            self.lane_i[eng] += 1
            amt = 16
            stream = "d:" + eng
            prev = self.cnt[sem]
            if prev > 0 and self.waited[eng][sem] < prev:
                self.waited[eng][sem] = prev
                self.ops[eng].append(
                    lambda e, sem=sem, prev=prev: e.wait_ge(self.sems[sem], prev))
        else:
            sem = CMP_SEM[eng]
            amt = 1
            stream = "c:" + eng
        if Rec.PARANOID_COMPUTE and not dma:
            for s2 in SEM_NAMES:
                val = self.cnt[s2]
                if val > 0 and self.waited[eng][s2] < val:
                    self.waited[eng][s2] = val
                    self.ops[eng].append(
                        lambda e, s2=s2, val=val: e.wait_ge(self.sems[s2], val))
        for b in reads:
            self._wait(eng, b.w, stream)
        for b in writes:
            if b.w is None and not b.r:
                for t in self.fence:
                    self._wait(eng, t, stream)
            self._wait(eng, b.w, stream, enforce_same_queue=False)
            for t in b.r:
                self._wait(eng, t, stream)
        self.cnt[sem] += amt
        tok = (sem, self.cnt[sem], stream)

        def wrapped(e, fn=fn, sem=sem, amt=amt):
            fn(e).then_inc(self.sems[sem], amt)
        self.ops[eng].append(wrapped)
        self.nops += 1
        for b in writes:
            b.w = tok
            b.r = []
        for b in reads:
            b.r.append(tok)
        return tok

    PARANOID_DMA = False

    def dma(self, eng, out, in_, reads=(), writes=(), **kw):
        if Rec.PARANOID_DMA:
            s = DMA_SEM[eng]
            val = self.cnt[s]
            if val > 0 and self.waited[eng][s] < val:
                self.waited[eng][s] = val
                self.ops[eng].append(
                    lambda e, s=s, val=val: e.wait_ge(self.sems[s], val))
        return self.op(eng, lambda e, out=out, in_=in_, kw=kw: e.dma_start(out=out, in_=in_, **kw),
                       reads=reads, writes=writes, dma=True)

    def barrier_token(self):
        return [(s, self.cnt[s], "x") for s in SEM_NAMES if self.cnt[s] > 0]

    def final_drain(self, eng="sync"):
        for s in SEM_NAMES:
            if self.cnt[s] > 0:
                val = self.cnt[s]
                if self.waited[eng][s] >= val:
                    continue
                self.waited[eng][s] = val

                def fn(e, s=s, val=val):
                    e.wait_ge(self.sems[s], val)
                self.ops[eng].append(fn)

    def build(self, nc, stack):
        self.sems = {s: stack.enter_context(nc.semaphore(s)) for s in SEM_NAMES}
        block = stack.enter_context(nc.Block())
        for eng in ENGS:
            ops = self.ops[eng]
            if not ops:
                continue

            def replay(e, ops=ops):
                for fn in ops:
                    fn(e)
            getattr(block, eng)(replay)


class Ctx:

    def __init__(self, nc, stack, T=TFULL):
        self.nc = nc
        self.stack = stack
        self.r = Rec()
        self.T = T
        self.NTG = T // 512
        self.NQT = T // 128
        self._sb = {}
        self._g = {}
        self._scopes = []
        self.pb = stack.enter_context(nc.psum_tensor("psum_all", [128, 4096], F32))
        self.consts = None
        self.hbm = {}

    def push(self):
        self._scopes.append((ExitStack(), []))

    def pop(self):
        st, names = self._scopes.pop()
        for n in names:
            del self._sb[n]
            del self._g["sb:" + n]
        st.close()
        self.r.fence = self.r.barrier_token()

    def sb(self, name, shape, dtype):
        if name not in self._sb:
            if self._scopes:
                st, names = self._scopes[-1]
                names.append(name)
            else:
                st = self.stack
            t = st.enter_context(self.nc.sbuf_tensor(name, shape, dtype))
            self._sb[name] = t
            self._g["sb:" + name] = G("sb:" + name)
        return self._sb[name]

    def g(self, name):
        return self._g["sb:" + name]

    def bankg(self, i):
        k = "pb:%d" % i
        if k not in self._g:
            self._g[k] = G(k)
        return self._g[k]

    def bank(self, i, n=1):
        return self.pb[:, 512 * i:512 * (i + n)]

    def dram(self, name, shape, dtype, kind="Internal"):
        if name not in self.hbm:
            ap = self.nc.dram_tensor(name, shape, dtype, kind=kind).ap()
            self.hbm[name] = (ap, G("hbm:" + name))
        return self.hbm[name]

    def finish(self):
        self.r.final_drain("sync")
        self.r.build(self.nc, self.stack)



def _pm(vec):
    v = np.asarray(vec)
    n = v.shape[-1] // 128
    return np.ascontiguousarray(v.reshape(n, 128).T)


def host_consts(T=TFULL):
    c = {}
    c["ones_col_f"] = np.ones((128, 1), np.float32)
    c["ones_row_f"] = np.ones((1, 128), np.float32)
    c["ones_row_b"] = np.ones((1, 128), ml_dtypes.bfloat16)
    c["ones_col_b"] = np.ones((128, 1), ml_dtypes.bfloat16)
    f = np.arange(512)[None, :]
    p = np.arange(128)[:, None]
    masks = [np.where(f <= o + p, 0.0, NEG).astype(np.float32) for o in (0, 128, 256, 384)]
    c["cmask"] = np.stack(masks)
    qt = np.arange(T // 128)[None, :]
    c["countdown"] = ((T - 1) - (qt * 128 + p)).astype(np.float32)
    seg = np.zeros((128, 2), np.float32)
    seg[:64, 0] = 1.0
    seg[64:, 1] = 1.0
    c["seg_lhsT"] = seg.astype(ml_dtypes.bfloat16)
    c["ind_lhsT"] = np.ascontiguousarray(seg.T).astype(ml_dtypes.bfloat16)
    c["ind_f"] = np.ascontiguousarray(seg.T)
    c["ident_f"] = np.eye(128, dtype=np.float32)
    kv = np.zeros((128, 8), np.float32)
    kv[:, 0] = 1e-12
    c["kvals"] = kv
    return c



def host_inputs(core, inputs, T=TFULL):
    g = lambda k: np.asarray(inputs[k])
    m = {}
    x = g("L0_attn_X")[core]
    m["xT"] = np.ascontiguousarray(x.T).astype(np.float32)
    m["w_q"] = g("L0_attn_Wq").astype(np.float32)
    m["w_k"] = g("L0_attn_Wk").astype(np.float32)
    m["w_v"] = g("L0_attn_Wv").astype(np.float32)
    m["b_qkv0"] = np.concatenate(
        [_pm(g("L0_attn_bq")), _pm(g("L0_attn_bk")), _pm(g("L0_attn_bv"))],
        axis=1).astype(np.float32)
    rwt = np.zeros((1024, 64), np.float32)
    rwt[:, :32] = g("L0_ffn_router_w").T
    m["router_wT"] = rwt
    rbp = np.zeros((1, 64), np.float32)
    rbp[0, :32] = g("L0_ffn_router_bias")
    m["router_b"] = rbp
    m["gate_wT"] = np.ascontiguousarray(g("L0_ffn_gate_w").transpose(0, 2, 1)).astype(np.float32)
    m["up_wT"] = np.ascontiguousarray(g("L0_ffn_up_w").transpose(0, 2, 1)).astype(np.float32)
    m["down_wT"] = np.ascontiguousarray(g("L0_ffn_down_w").transpose(0, 2, 1)).astype(np.float32)
    m["sgate_wT"] = np.ascontiguousarray(g("L0_ffn_sgate_w").T).astype(np.float32)
    m["sup_wT"] = np.ascontiguousarray(g("L0_ffn_sup_w").T).astype(np.float32)
    m["sdown_wT"] = np.ascontiguousarray(g("L0_ffn_sdown_w").T).astype(np.float32)
    m["wqkv1T"] = np.ascontiguousarray(g("L1_attn_W_qkv").T).astype(np.float32)
    m["b_qkv1"] = _pm(g("L1_attn_b_qkv")).astype(np.float32)
    m["l1f_We"] = g("L1_ffn_We").astype(np.float32)
    m["l1f_be"] = np.ascontiguousarray(
        g("L1_ffn_be").reshape(20, 8, 128).transpose(0, 2, 1)).astype(np.float32)
    m["l1f_Wq"] = g("L1_ffn_Wq").astype(np.float32)
    m["l1f_Wk"] = g("L1_ffn_Wk").astype(np.float32)
    m["l1f_Wv"] = g("L1_ffn_Wv").astype(np.float32)
    m["l1f_Wo"] = g("L1_ffn_Wo").astype(np.float32)
    m["l1f_lng"] = _pm(g("L1_ffn_ln_g")).astype(np.float32)
    m["l1f_lnb"] = _pm(g("L1_ffn_ln_b")).astype(np.float32)
    m["l1f_ew"] = np.ascontiguousarray(g("L1_ffn_ew").reshape(1, 18)).astype(np.float32)
    m["l1f_actw"] = np.ascontiguousarray(g("L1_ffn_act_w")[None, :]).astype(np.float32)
    return m


CONST_SPECS = {
    "ones_col_f": ([128, 1], F32),
    "ones_row_f": ([1, 128], F32),
    "ones_row_b": ([1, 128], BF16),
    "ones_col_b": ([128, 1], BF16),
    "cmask": ([4, 128, 512], F32),
    "countdown": (None, F32),
    "seg_lhsT": ([128, 2], BF16),
    "ind_lhsT": ([2, 128], BF16),
    "ind_f": ([2, 128], F32),
    "ident_f": ([128, 128], F32),
    "kvals": ([128, 8], F32),
}


def declare_consts(C, only=None):
    nc, r = C.nc, C.r
    c = {}
    for name, (shape, dt) in CONST_SPECS.items():
        if only is not None and name not in only:
            continue
        if name == "countdown":
            shape = [128, C.T // 128]
        ap = nc.dram_tensor(name, shape, dt, kind="ExternalInput").ap()
        if name == "cmask":
            t = C.sb("k_cmask", [128, 4 * 512], F32)
            for gi in range(4):
                r.dma("sync", t[:, gi * 512:(gi + 1) * 512], ap[gi], writes=[C.g("k_cmask")])
        else:
            t = C.sb("k_" + name, shape, dt)
            r.dma("sync", t[:], ap, writes=[C.g("k_" + name)])
        c[name] = t
    C.consts = c
    return c



def emit_ln(C, x_hbm, xg, xn_hbm, xng, eps, gamma_beta=None, tag="ln",
            in_dtype=F32, out_dtype=BF16):
    r = C.r
    xb = C.sb(tag + "_x", [128, DC * 512], F32)
    if in_dtype != F32:
        xbr = C.sb(tag + "_xr", [128, DC * 512], in_dtype)
        gxbr = C.g(tag + "_xr")
    sq = C.sb(tag + "_sq", [128, DC * 512], F32)
    stats = C.sb(tag + "_st", [1, 1024], F32)
    ab = C.sb(tag + "_ab", [128, 1024], F32)
    xnb = C.sb(tag + "_xn", [128, DC * 512], out_dtype)
    gxb, gsq, gst, gab, gxn = (C.g(tag + "_x"), C.g(tag + "_sq"), C.g(tag + "_st"),
                               C.g(tag + "_ab"), C.g(tag + "_xn"))
    if gamma_beta is not None:
        gb = C.sb(tag + "_gb", [128, 16], F32)
        ggb = C.g(tag + "_gb")
        r.dma("sync", gb[:, 0:8], gamma_beta[0], writes=[ggb])
        r.dma("sync", gb[:, 8:16], gamma_beta[1], writes=[ggb])
    ones_f = C.consts["ones_col_f"]
    ones_row_f = C.consts["ones_row_f"]
    konesg = C.g("k_ones_col_f")
    inv_d = 1.0 / D
    for tg in range(C.NTG):
        if in_dtype == F32:
            for ic in range(DC):
                r.dma("sync", xb[:, ic * 512:(ic + 1) * 512],
                      x_hbm[ic * 128:(ic + 1) * 128, tg * 512:(tg + 1) * 512],
                      reads=[xg], writes=[gxb])
        else:
            for ic in range(DC):
                r.dma("sync", xbr[:, ic * 512:(ic + 1) * 512],
                      x_hbm[ic * 128:(ic + 1) * 128, tg * 512:(tg + 1) * 512],
                      reads=[xg], writes=[gxbr])
            for ic in range(DC):
                sl = slice(ic * 512, (ic + 1) * 512)
                r.op("vector", lambda e, sl=sl: e.tensor_copy(out=xb[:, sl], in_=xbr[:, sl]),
                     reads=[gxbr], writes=[gxb])
        for ic in range(DC):
            sl = slice(ic * 512, (ic + 1) * 512)
            r.op("vector", lambda e, sl=sl: e.tensor_mul(out=sq[:, sl], in0=xb[:, sl], in1=xb[:, sl]),
                 reads=[gxb], writes=[gsq])
        for ic in range(DC):
            sl = slice(ic * 512, (ic + 1) * 512)
            r.op("tensor", lambda e, sl=sl, ic=ic: e.matmul(
                C.bank(0)[0:1, :], ones_f[:], xb[:, sl], start=(ic == 0), stop=(ic == DC - 1)),
                reads=[gxb, konesg], writes=[C.bankg(0)])
        for ic in range(DC):
            sl = slice(ic * 512, (ic + 1) * 512)
            r.op("tensor", lambda e, sl=sl, ic=ic: e.matmul(
                C.bank(1)[0:1, :], ones_f[:], sq[:, sl], start=(ic == 0), stop=(ic == DC - 1)),
                reads=[gsq, konesg], writes=[C.bankg(1)])
        r.op("vector", lambda e: e.tensor_scalar_mul(out=stats[0:1, 0:512], in0=C.bank(0)[0:1, :], scalar1=inv_d),
             reads=[C.bankg(0)], writes=[gst])
        r.op("vector", lambda e: e.tensor_scalar_mul(out=stats[0:1, 512:1024], in0=C.bank(1)[0:1, :], scalar1=inv_d),
             reads=[C.bankg(1)], writes=[gst])
        r.op("vector", lambda e: e.tensor_mul(out=sq[0:1, 0:512], in0=stats[0:1, 0:512], in1=stats[0:1, 0:512]),
             reads=[gst], writes=[gsq])
        r.op("vector", lambda e: e.tensor_sub(out=stats[0:1, 512:1024], in0=stats[0:1, 512:1024], in1=sq[0:1, 0:512]),
             reads=[gsq], writes=[gst])
        r.op("vector", lambda e: e.tensor_scalar_add(out=stats[0:1, 512:1024], in0=stats[0:1, 512:1024], scalar1=float(eps)),
             reads=[gst], writes=[gst])
        r.op("scalar", lambda e: e.activation(out=stats[0:1, 512:1024], in_=stats[0:1, 512:1024], func=AF.Ln),
             reads=[gst], writes=[gst])
        r.op("scalar", lambda e: e.activation(out=stats[0:1, 512:1024], in_=stats[0:1, 512:1024], func=AF.Exp, scale=-0.5),
             reads=[gst], writes=[gst])
        r.op("tensor", lambda e: e.matmul(C.bank(2), ones_row_f[:], stats[0:1, 0:512], start=True, stop=True),
             reads=[gst, C.g("k_ones_row_f")], writes=[C.bankg(2)])
        r.op("tensor", lambda e: e.matmul(C.bank(3), ones_row_f[:], stats[0:1, 512:1024], start=True, stop=True),
             reads=[gst, C.g("k_ones_row_f")], writes=[C.bankg(3)])
        r.op("vector", lambda e: e.tensor_copy(out=ab[:, 0:512], in_=C.bank(2)),
             reads=[C.bankg(2)], writes=[gab])
        r.op("vector", lambda e: e.tensor_copy(out=ab[:, 512:1024], in_=C.bank(3)),
             reads=[C.bankg(3)], writes=[gab])
        for ic in range(DC):
            sl = slice(ic * 512, (ic + 1) * 512)
            r.op("vector", lambda e, sl=sl: e.tensor_sub(out=sq[:, sl], in0=xb[:, sl], in1=ab[:, 0:512]),
                 reads=[gxb, gab], writes=[gsq])
            if gamma_beta is None:
                r.op("vector", lambda e, sl=sl: e.tensor_mul(out=xnb[:, sl], in0=sq[:, sl], in1=ab[:, 512:1024]),
                     reads=[gsq, gab], writes=[gxn])
            else:
                r.op("vector", lambda e, sl=sl: e.tensor_mul(out=sq[:, sl], in0=sq[:, sl], in1=ab[:, 512:1024]),
                     reads=[gsq, gab], writes=[gsq])
                r.op("vector", lambda e, sl=sl, ic=ic: e.tensor_scalar(
                    out=xnb[:, sl], in0=sq[:, sl], scalar1=gb[:, ic:ic + 1],
                    scalar2=gb[:, 8 + ic:9 + ic], op0=ALU.mult, op1=ALU.add),
                    reads=[gsq, ggb], writes=[gxn])
        for ic in range(DC):
            sl = slice(ic * 512, (ic + 1) * 512)
            r.dma("sync", xn_hbm[ic * 128:(ic + 1) * 128, tg * 512:(tg + 1) * 512], xnb[:, sl],
                  reads=[gxn], writes=[xng])
    return


def load_slab(C, name, hbm_ap, hg, nchunk, T, dtype=BF16, queue="sync"):
    t = C.sb(name, [128, nchunk * T], dtype)
    g = C.g(name)
    for ic in range(nchunk):
        C.r.dma(queue, t[:, ic * T:(ic + 1) * T], hbm_ap[ic * 128:(ic + 1) * 128, :],
                reads=[hg], writes=[g])
    return t, g


def load_w(C, name, w_hbm, wg, in_chunks, out_cols):
    t = C.sb(name, [128, in_chunks * out_cols], BF16)
    g = C.g(name)
    for ic in range(in_chunks):
        C.r.dma("gpsimd", t[:, ic * out_cols:(ic + 1) * out_cols],
                w_hbm[ic * 128:(ic + 1) * 128, :], reads=[wg], writes=[g])
    return t, g


def wsl(C, wname, in_chunks, out_cols, ic, oc):
    return C._sb[wname][:, ic * out_cols + oc * 128: ic * out_cols + (oc + 1) * 128]


def emit_proj(C, slab_name, w_name, in_chunks, out_cols, epi, banks=(0, 1), tgs=None):
    r = C.r
    slab = C._sb[slab_name]
    sg = C.g(slab_name)
    wg = C.g(w_name)
    T = C.T
    if tgs is None:
        tgs = range(C.NTG)
    nb = len(banks)
    i = 0
    for oc in range(out_cols // 128):
        for tg in tgs:
            b = banks[i % nb]
            i += 1
            for ic in range(in_chunks):
                lh = wsl(C, w_name, in_chunks, out_cols, ic, oc)
                rh = slab[:, ic * T + tg * 512: ic * T + tg * 512 + 512]
                r.op("tensor", lambda e, lh=lh, rh=rh, ic=ic, b=b: e.matmul(
                    C.bank(b), lh, rh,
                    start=(ic == 0), stop=(ic == in_chunks - 1)),
                    reads=[sg, wg], writes=[C.bankg(b)])
            epi(oc, tg, C.bank(b), C.bankg(b))


def emit_l0attn(C, xn_hbm, xng, s_hbm, sg_out, wq_ap, wqg, wk_ap, wkg, wv_ap, wvg,
                b0_ap, b0g):
    r, T, NTG, NQT = C.r, C.T, C.NTG, C.NQT
    nc = C.nc
    q_hbm, qg = C.dram("sc_q0T", [D, T], BF16)
    bqkv = C.sb("l0a_b", [128, 24], F32)
    r.dma("sync", bqkv[:], b0_ap, reads=[b0g], writes=[C.g("l0a_b")])
    bg = C.g("l0a_b")

    xnslab, xnsg = load_slab(C, "l0a_xn", xn_hbm, xng, DC, T)
    load_w(C, "l0a_w", wq_ap, wqg, DC, D)

    def epi_q(oc, tg, bank, bankg):
        tmp = C.sb("l0a_ev", [128, 512], BF16)
        r.op("vector", lambda e, oc=oc, bank=bank: e.tensor_scalar_add(
            out=tmp[:], in0=bank, scalar1=bqkv[:, oc:oc + 1]),
            reads=[bankg, bg], writes=[C.g("l0a_ev")])
        r.dma("sync", q_hbm[oc * 128:(oc + 1) * 128, tg * 512:(tg + 1) * 512], tmp[:],
              reads=[C.g("l0a_ev")], writes=[qg])
    emit_proj(C, "l0a_xn", "l0a_w", DC, D, epi_q, banks=(0, 1))

    kslab = C.sb("l0a_k", [128, DC * T], BF16)
    kg = C.g("l0a_k")
    load_w(C, "l0a_w", wk_ap, wkg, DC, D)

    def epi_k(oc, tg, bank, bankg):
        r.op("vector", lambda e, oc=oc, tg=tg, bank=bank: e.tensor_scalar_add(
            out=kslab[:, oc * T + tg * 512: oc * T + tg * 512 + 512], in0=bank,
            scalar1=bqkv[:, 8 + oc:9 + oc]),
            reads=[bankg, bg], writes=[kg])
    emit_proj(C, "l0a_xn", "l0a_w", DC, D, epi_k, banks=(2, 3))

    qt_b = C.sb("l0a_qt", [128, DC * 128], BF16)
    qtg = C.g("l0a_qt")
    srow = C.sb("l0a_srow", [128, T], F32)
    srg = C.g("l0a_srow")
    pexp = C.sb("l0a_p", [128, T], BF16)
    pg = C.g("l0a_p")
    cols = C.sb("l0a_cols", [128, 8], F32)
    colg = C.g("l0a_cols")
    ratio = C.sb("l0a_ratio", [128, NQT], F32)
    ratg = C.g("l0a_ratio")
    cmask = C.consts["cmask"]
    cmg = C.g("k_cmask")
    cd = C.consts["countdown"]
    cdg = C.g("k_countdown")
    scale = 1.0 / 32.0
    for qt in range(NQT):
        ngrp = qt // 4 + 1
        for ic in range(DC):
            r.dma("sync", qt_b[:, ic * 128:(ic + 1) * 128],
                  q_hbm[ic * 128:(ic + 1) * 128, qt * 128:(qt + 1) * 128],
                  reads=[qg], writes=[qtg])
        for kg_i in range(ngrp):
            b = kg_i % 2
            for ic in range(DC):
                r.op("tensor", lambda e, ic=ic, kg_i=kg_i, b=b: e.matmul(
                    C.bank(b), qt_b[:, ic * 128:(ic + 1) * 128],
                    kslab[:, ic * T + kg_i * 512: ic * T + kg_i * 512 + 512],
                    start=(ic == 0), stop=(ic == DC - 1)),
                    reads=[qtg, kg], writes=[C.bankg(b)])
            if kg_i == ngrp - 1:
                r.op("vector", lambda e, kg_i=kg_i, b=b, m=qt % 4: e.tensor_add(
                    out=srow[:, kg_i * 512:(kg_i + 1) * 512], in0=C.bank(b),
                    in1=cmask[:, m * 512:(m + 1) * 512]),
                    reads=[C.bankg(b), cmg], writes=[srg])
            else:
                r.op("vector", lambda e, kg_i=kg_i, b=b: e.tensor_copy(
                    out=srow[:, kg_i * 512:(kg_i + 1) * 512], in_=C.bank(b)),
                    reads=[C.bankg(b)], writes=[srg])
        kw = ngrp * 512
        r.op("vector", lambda e, kw=kw: e.reduce_max(out=cols[:, 0:1], in_=srow[:, 0:kw], axis=AX.X),
             reads=[srg], writes=[colg])
        r.op("scalar", lambda e: e.activation(out=cols[:, 1:2], in_=cols[:, 0:1], func=AF.Relu, scale=scale),
             reads=[colg], writes=[colg])
        r.op("scalar", lambda e: e.mul(out=cols[:, 2:3], in_=cols[:, 1:2], mul=-1.0),
             reads=[colg], writes=[colg])
        r.op("scalar", lambda e, kw=kw: e.activation(
            out=pexp[:, 0:kw], in_=srow[:, 0:kw], func=AF.Exp, bias=cols[:, 2:3],
            scale=scale, accum_out=cols[:, 3:4]),
            reads=[srg, colg], writes=[pg, colg])
        r.op("scalar", lambda e: e.activation(out=cols[:, 4:5], in_=cols[:, 2:3], func=AF.Exp),
             reads=[colg], writes=[colg])
        r.op("scalar", lambda e, qt=qt: e.activation(out=cols[:, 5:6], in_=cd[:, qt:qt + 1],
                                                     func=AF.Identity, scale=cols[:, 4:5]),
             reads=[colg, cdg], writes=[colg])
        r.op("scalar", lambda e: e.activation(out=cols[:, 6:7], in_=cols[:, 3:4],
                                              func=AF.Identity, bias=cols[:, 5:6]),
             reads=[colg], writes=[colg])
        r.op("vector", lambda e: e.reciprocal(out=cols[:, 7:8], in_=cols[:, 6:7]),
             reads=[colg], writes=[colg])
        r.op("scalar", lambda e, qt=qt: e.activation(out=ratio[:, qt:qt + 1], in_=cols[:, 7:8],
                                                     func=AF.Identity, scale=cols[:, 6:7]),
             reads=[colg], writes=[ratg])

    rrows = C.sb("l0a_rr", [NQT, 128], F32)
    rrg = C.g("l0a_rr")
    rflat = C.sb("l0a_rf", [1, T], F32)
    rfg = C.g("l0a_rf")
    ident = C.consts["ident_f"]
    idg = C.g("k_ident_f")
    r.op("tensor", lambda e: e.transpose(C.bank(4)[0:NQT, 0:128], ratio[:], ident[:]),
         reads=[ratg, idg], writes=[C.bankg(4)])
    r.op("vector", lambda e: e.tensor_copy(out=rrows[:], in_=C.bank(4)[0:NQT, 0:128]),
         reads=[C.bankg(4)], writes=[rrg])
    rdram, rdg = C.dram("sc_l0a_r", [NQT, 128], F32)
    r.dma("sync", rdram, rrows[:], reads=[rrg], writes=[rdg])
    r.dma("sync", rflat[:], rdram.rearrange("q p -> (q p)")[None, :], reads=[rdg], writes=[rfg])

    load_w(C, "l0a_w", wv_ap, wvg, DC, D)
    rb = C.sb("l0a_rb", [128, 512], F32)
    rbg = C.g("l0a_rb")
    sev = C.sb("l0a_sev", [128, 512], F32)
    sevg = C.g("l0a_sev")
    ones_row_f = C.consts["ones_row_f"]
    org = C.g("k_ones_row_f")

    def epi_v(oc, tg, bank, bankg):
        if oc == 0:
            r.op("tensor", lambda e, tg=tg: e.matmul(
                C.bank(5), ones_row_f[:], rflat[0:1, tg * 512:(tg + 1) * 512],
                start=True, stop=True),
                reads=[rfg, org], writes=[C.bankg(5)])
            r.op("vector", lambda e: e.tensor_copy(out=rb[:], in_=C.bank(5)),
                 reads=[C.bankg(5)], writes=[rbg])
        r.op("vector", lambda e, oc=oc, bank=bank: e.scalar_tensor_tensor(
            out=sev[:], in0=bank, scalar=bqkv[:, 16 + oc:17 + oc], in1=rb[:],
            op0=ALU.add, op1=ALU.mult),
            reads=[bankg, bg, rbg], writes=[sevg])
        r.dma("sync", s_hbm[oc * 128:(oc + 1) * 128, tg * 512:(tg + 1) * 512], sev[:],
              reads=[sevg], writes=[sg_out])
    emit_proj(C, "l0a_xn", "l0a_w", DC, D, epi_v, banks=(6, 7))

def emit_l0ffn(C, xn_hbm, xng, x_hbm, xg, s_hbm, sg_in, x1_hbm, x1g,
               rw_ap, rwg, rb_ap, rbg_in, gw_ap, gwg, uw_ap, uwg, dw_ap, dwg,
               sga_ap, sgag, sua_ap, suag, sda_ap, sdag):
    r, T, NTG = C.r, C.T, C.NTG
    E = 32
    xnslab, xnsg = load_slab(C, "l0f_xn", xn_hbm, xng, DC, T)
    rwsl, rwsg = load_w(C, "l0f_rw", rw_ap, rwg, DC, 64)
    rbrow = C.sb("l0f_rb", [1, 64], F32)
    r.dma("sync", rbrow[:], rb_ap, reads=[rbg_in], writes=[C.g("l0f_rb")])
    rbb = C.sb("l0f_rbb", [128, 64], F32)
    r.op("tensor", lambda e: e.matmul(C.bank(0)[:, 0:64], C.consts["ones_row_f"][:], rbrow[:],
                                      start=True, stop=True),
         reads=[C.g("l0f_rb"), C.g("k_ones_row_f")], writes=[C.bankg(0)])
    r.op("vector", lambda e: e.tensor_copy(out=rbb[:], in_=C.bank(0)[:, 0:64]),
         reads=[C.bankg(0)], writes=[C.g("l0f_rbb")])

    z = C.sb("l0f_z", [128, 64], F32)
    zw = C.sb("l0f_zw", [128, 64], F32)
    t32 = C.sb("l0f_t32", [128, 64], F32)
    wtsf = C.sb("l0f_wts", [128, 64], F32)
    cols = C.sb("l0f_c", [128, 8], F32)
    wrows = C.sb("l0f_wr", [32, T], BF16)
    gz, gzw, gt32, gwt, gc, gwr = (C.g("l0f_z"), C.g("l0f_zw"), C.g("l0f_t32"),
                                   C.g("l0f_wts"), C.g("l0f_c"), C.g("l0f_wr"))
    ident = C.consts["ident_f"]
    idg = C.g("k_ident_f")
    for tt in range(T // 128):
        for ic in range(DC):
            r.op("tensor", lambda e, ic=ic, tt=tt: e.matmul(
                C.bank(1)[:, 0:64], xnslab[:, ic * T + tt * 128: ic * T + tt * 128 + 128],
                rwsl[:, ic * 64:(ic + 1) * 64], start=(ic == 0), stop=(ic == DC - 1)),
                reads=[xnsg, rwsg], writes=[C.bankg(1)])
        r.op("vector", lambda e: e.tensor_add(out=z[:], in0=C.bank(1)[:, 0:64], in1=rbb[:]),
             reads=[C.bankg(1), C.g("l0f_rbb")], writes=[gz])
        r.op("vector", lambda e: e.reduce_max(out=cols[:, 0:1], in_=z[:, 0:E], axis=AX.X),
             reads=[gz], writes=[gc])
        r.op("scalar", lambda e: e.mul(out=cols[:, 1:2], in_=cols[:, 0:1], mul=-1.0),
             reads=[gc], writes=[gc])
        r.op("scalar", lambda e: e.activation(out=z[:, 0:E], in_=z[:, 0:E], func=AF.Exp, bias=cols[:, 1:2]),
             reads=[gz, gc], writes=[gz])
        r.op("vector", lambda e: e.tensor_copy(out=zw[:], in_=z[:]), reads=[gz], writes=[gzw])
        for it in range(4):
            r.op("vector", lambda e: e.reduce_max(out=cols[:, 2:3], in_=zw[:, 0:E], axis=AX.X),
                 reads=[gzw], writes=[gc])
            r.op("scalar", lambda e: e.copy(out=cols[:, 3:4], in_=cols[:, 2:3]),
                 reads=[gc], writes=[gc])
            if it < 3:
                r.op("vector", lambda e: e.tensor_single_scalar(
                    out=t32[:], in_=zw[:], scalar=cols[:, 3:4], op=ALU.is_ge),
                    reads=[gzw, gc], writes=[gt32])
                r.op("vector", lambda e: e.scalar_tensor_tensor(
                    out=zw[:], in0=t32[:], scalar=NEG, in1=zw[:], op0=ALU.mult, op1=ALU.add),
                    reads=[gt32, gzw], writes=[gzw])
        r.op("vector", lambda e: e.scalar_tensor_tensor(
            out=t32[:], in0=z[:], scalar=cols[:, 3:4], in1=z[:], op0=ALU.is_ge,
            op1=ALU.mult, accum_out=cols[:, 4:5]),
            reads=[gz, gc], writes=[gt32, gc])
        r.op("scalar", lambda e: e.copy(out=cols[:, 5:6], in_=cols[:, 4:5]),
             reads=[gc], writes=[gc])
        r.op("vector", lambda e: e.reciprocal(out=cols[:, 6:7], in_=cols[:, 5:6]),
             reads=[gc], writes=[gc])
        r.op("scalar", lambda e: e.copy(out=cols[:, 7:8], in_=cols[:, 6:7]),
             reads=[gc], writes=[gc])
        r.op("vector", lambda e: e.tensor_scalar_mul(out=wtsf[:], in0=t32[:], scalar1=cols[:, 7:8]),
             reads=[gt32, gc], writes=[gwt])
        r.op("tensor", lambda e: e.transpose(C.bank(2)[0:32, 0:128], wtsf[:, 0:32], ident[:]),
             reads=[gwt, idg], writes=[C.bankg(2)])
        r.op("vector", lambda e, tt=tt: e.tensor_copy(out=wrows[0:32, tt * 128:(tt + 1) * 128],
                                                      in_=C.bank(2)[0:32, 0:128]),
             reads=[C.bankg(2)], writes=[gwr])
    wr_dram, gwrd = C.dram("sc_l0f_wr", [32, T], BF16)
    r.dma("sync", wr_dram, wrows[:], reads=[gwr], writes=[gwrd])
    wflat = C.sb("l0f_wf", [1, E * 512], BF16)
    gwf = C.g("l0f_wf")

    gwe = C.sb("l0f_gwe", [128, DC * 128], BF16)
    uwe = C.sb("l0f_uwe", [128, DC * 128], BF16)
    dwe = C.sb("l0f_dwe", [128, E * 128], BF16)
    ggw, guw, gdw = C.g("l0f_gwe"), C.g("l0f_uwe"), C.g("l0f_dwe")
    sgw, _ = load_w(C, "l0f_sgw", sga_ap, sgag, DC, 512)
    suw, _ = load_w(C, "l0f_suw", sua_ap, suag, DC, 512)
    sdw, _ = load_w(C, "l0f_sdw", sda_ap, sdag, 4, D)
    gsgw, gsuw, gsdw = C.g("l0f_sgw"), C.g("l0f_suw"), C.g("l0f_sdw")
    sgb = C.sb("l0f_sgb", [128, 512], BF16)
    sub = C.sb("l0f_sub", [128, 512], BF16)
    zsh = C.sb("l0f_zsh", [128, 4 * 512], BF16)
    gsgb, gsub, gzsh = C.g("l0f_sgb"), C.g("l0f_sub"), C.g("l0f_zsh")
    hslab = C.sb("l0f_h", [128, E * 512], BF16)
    gh = C.g("l0f_h")
    wb = C.sb("l0f_wb", [128, 512], BF16)
    gwb = C.g("l0f_wb")
    sg_t = C.sb("l0f_sg", [128, 512], BF16)
    su_t = C.sb("l0f_su", [128, 512], BF16)
    gsg, gsu = C.g("l0f_sg"), C.g("l0f_su")
    xch = C.sb("l0f_xch", [128, 512], F32)
    sch = C.sb("l0f_sch", [128, 512], F32)
    gxc, gsc = C.g("l0f_xch"), C.g("l0f_sch")
    out_ev = C.sb("l0f_oev", [128, 512], F32)
    goev = C.g("l0f_oev")
    onesb = C.consts["ones_row_b"]
    gob = C.g("k_ones_row_b")

    for tg in range(NTG):
        tsl = slice(tg * 512, (tg + 1) * 512)
        for e_i in range(E):
            r.dma("sync", wflat[0:1, e_i * 512:(e_i + 1) * 512], wr_dram[e_i:e_i + 1, tsl],
                  reads=[gwrd], writes=[gwf])
        for hc in range(4):
            for ic in range(DC):
                r.op("tensor", lambda e, ic=ic, hc=hc, tg=tg: e.matmul(
                    C.bank(0), sgw[:, ic * 512 + hc * 128: ic * 512 + (hc + 1) * 128],
                    xnslab[:, ic * T + tg * 512: ic * T + tg * 512 + 512],
                    start=(ic == 0), stop=(ic == DC - 1)),
                    reads=[xnsg, gsgw], writes=[C.bankg(0)])
            for ic in range(DC):
                r.op("tensor", lambda e, ic=ic, hc=hc, tg=tg: e.matmul(
                    C.bank(1), suw[:, ic * 512 + hc * 128: ic * 512 + (hc + 1) * 128],
                    xnslab[:, ic * T + tg * 512: ic * T + tg * 512 + 512],
                    start=(ic == 0), stop=(ic == DC - 1)),
                    reads=[xnsg, gsuw], writes=[C.bankg(1)])
            r.op("scalar", lambda e: e.activation(out=sgb[:], in_=C.bank(0), func=AF.Silu),
                 reads=[C.bankg(0)], writes=[gsgb])
            r.op("vector", lambda e: e.tensor_copy(out=sub[:], in_=C.bank(1)),
                 reads=[C.bankg(1)], writes=[gsub])
            r.op("vector", lambda e, hc=hc: e.tensor_mul(
                out=zsh[:, hc * 512:(hc + 1) * 512], in0=sgb[:], in1=sub[:]),
                reads=[gsgb, gsub], writes=[gzsh])
        for e_i in range(E):
            for ic in range(DC):
                r.dma("gpsimd", gwe[:, ic * 128:(ic + 1) * 128],
                      gw_ap[e_i, ic * 128:(ic + 1) * 128, :], reads=[gwg], writes=[ggw])
                r.dma("gpsimd", uwe[:, ic * 128:(ic + 1) * 128],
                      uw_ap[e_i, ic * 128:(ic + 1) * 128, :], reads=[uwg], writes=[guw])
            for ic in range(DC):
                r.op("tensor", lambda e, ic=ic, tg=tg: e.matmul(
                    C.bank(2), gwe[:, ic * 128:(ic + 1) * 128],
                    xnslab[:, ic * T + tg * 512: ic * T + tg * 512 + 512],
                    start=(ic == 0), stop=(ic == DC - 1)),
                    reads=[xnsg, ggw], writes=[C.bankg(2)])
            for ic in range(DC):
                r.op("tensor", lambda e, ic=ic, tg=tg: e.matmul(
                    C.bank(3), uwe[:, ic * 128:(ic + 1) * 128],
                    xnslab[:, ic * T + tg * 512: ic * T + tg * 512 + 512],
                    start=(ic == 0), stop=(ic == DC - 1)),
                    reads=[xnsg, guw], writes=[C.bankg(3)])
            r.op("tensor", lambda e, e_i=e_i: e.matmul(
                C.bank(4), onesb[:], wflat[0:1, e_i * 512:(e_i + 1) * 512],
                start=True, stop=True),
                reads=[gwf, gob], writes=[C.bankg(4)])
            r.op("scalar", lambda e: e.activation(out=sg_t[:], in_=C.bank(2), func=AF.Silu),
                 reads=[C.bankg(2)], writes=[gsg])
            r.op("vector", lambda e: e.tensor_copy(out=su_t[:], in_=C.bank(3)),
                 reads=[C.bankg(3)], writes=[gsu])
            r.op("vector", lambda e: e.tensor_copy(out=wb[:], in_=C.bank(4)),
                 reads=[C.bankg(4)], writes=[gwb])
            r.op("vector", lambda e: e.tensor_mul(out=sg_t[:], in0=sg_t[:], in1=su_t[:]),
                 reads=[gsu], writes=[gsg])
            r.op("vector", lambda e, e_i=e_i: e.tensor_mul(
                out=hslab[:, e_i * 512:(e_i + 1) * 512], in0=sg_t[:], in1=wb[:]),
                reads=[gsg, gwb], writes=[gh])
        for oc in range(DC):
            for e_i in range(E):
                r.dma("gpsimd", dwe[:, e_i * 128:(e_i + 1) * 128],
                      dw_ap[e_i, :, oc * 128:(oc + 1) * 128], reads=[dwg], writes=[gdw])
            for e_i in range(E):
                r.op("tensor", lambda e, e_i=e_i: e.matmul(
                    C.bank(5), dwe[:, e_i * 128:(e_i + 1) * 128],
                    hslab[:, e_i * 512:(e_i + 1) * 512],
                    start=(e_i == 0), stop=False),
                    reads=[gh, gdw], writes=[C.bankg(5)])
            for hc in range(4):
                r.op("tensor", lambda e, hc=hc, oc=oc: e.matmul(
                    C.bank(5), sdw[:, hc * D + oc * 128: hc * D + (oc + 1) * 128],
                    zsh[:, hc * 512:(hc + 1) * 512],
                    start=False, stop=(hc == 3)),
                    reads=[gzsh, gsdw], writes=[C.bankg(5)])
            r.dma("sync", xch[:], x_hbm[oc * 128:(oc + 1) * 128, tsl], reads=[xg], writes=[gxc])
            r.dma("sync", sch[:], s_hbm[oc * 128:(oc + 1) * 128, tsl], reads=[sg_in], writes=[gsc])
            r.op("scalar", lambda e: e.activation(out=out_ev[:], in_=C.bank(5), func=AF.Silu),
                 reads=[C.bankg(5)], writes=[goev])
            r.op("vector", lambda e: e.tensor_mul(out=out_ev[:], in0=out_ev[:], in1=sch[:]),
                 reads=[gsc], writes=[goev])
            r.op("vector", lambda e: e.tensor_add(out=out_ev[:], in0=out_ev[:], in1=xch[:]),
                 reads=[gxc], writes=[goev])
            r.dma("sync", x1_hbm[oc * 128:(oc + 1) * 128, tsl], out_ev[:],
                  reads=[goev], writes=[x1g])


def emit_l1qkv(C, xn_hbm, xng, w_ap, wg_in, b_ap, bg_in, qkv_hbm, qkvg):
    r, T = C.r, C.T
    load_slab(C, "l1q_xn", xn_hbm, xng, DC, T)
    load_w(C, "l1q_w", w_ap, wg_in, DC, 3072)
    bb = C.sb("l1q_b", [128, 24], F32)
    r.dma("sync", bb[:], b_ap, reads=[bg_in], writes=[C.g("l1q_b")])
    bg = C.g("l1q_b")
    ev = C.sb("l1q_ev", [128, 512], BF16)
    gev = C.g("l1q_ev")

    def epi(oc, tg, bank, bankg):
        r.op("vector", lambda e, oc=oc, bank=bank: e.tensor_scalar_add(
            out=ev[:], in0=bank, scalar1=bb[:, oc:oc + 1]),
            reads=[bankg, bg], writes=[gev])
        r.dma("sync", qkv_hbm[oc * 128:(oc + 1) * 128, tg * 512:(tg + 1) * 512], ev[:],
              reads=[gev], writes=[qkvg])
    emit_proj(C, "l1q_xn", "l1q_w", DC, 3072, epi, banks=(0, 1, 2, 3))


def emit_l1mix(C, qkv_hbm, qkvg, all_hbm, allg, s1_hbm, s1g):
    r, T = C.r, C.T
    q2 = C.sb("l1m_q", [128, T], BF16)
    k2 = C.sb("l1m_k", [128, 512], BF16)
    v2 = C.sb("l1m_v", [128, 512], BF16)
    Et = C.sb("l1m_e", [128, 512], BF16)
    scb = C.sb("l1m_sc", [2, 512], F32)
    pb = C.sb("l1m_pb", [128, 512], F32)
    cz = C.sb("l1m_cz", [128, 512], F32)
    pv = C.sb("l1m_pv", [128, 512], F32)
    gq, gk, gv, gE, gsc, gpb, gcz, gpv = (C.g("l1m_q"), C.g("l1m_k"), C.g("l1m_v"),
                                          C.g("l1m_e"), C.g("l1m_sc"), C.g("l1m_pb"),
                                          C.g("l1m_cz"), C.g("l1m_pv"))
    seg = C.consts["seg_lhsT"]
    gseg = C.g("k_seg_lhsT")
    indf = C.consts["ind_f"]
    gind = C.g("k_ind_f")
    cnorm = (1.0 / 8.0) * (8.0 ** -0.5)
    for hp in range(8):
        base = 384 * hp
        r.dma("sync", q2[0:64, :], qkv_hbm[base:base + 64, :], reads=[qkvg], writes=[gq])
        r.dma("sync", q2[64:128, :], qkv_hbm[base + 192:base + 256, :], reads=[qkvg], writes=[gq])
        for tg in range(C.NTG):
            tsl = slice(tg * 512, (tg + 1) * 512)
            for t in range(8):
                r.dma("sync", k2[0:64, :], all_hbm[t, base + 64:base + 128, tsl],
                      reads=[allg], writes=[gk])
                r.dma("sync", k2[64:128, :], all_hbm[t, base + 256:base + 320, tsl],
                      reads=[allg], writes=[gk])
                r.dma("sync", v2[0:64, :], all_hbm[t, base + 128:base + 192, tsl],
                      reads=[allg], writes=[gv])
                r.dma("sync", v2[64:128, :], all_hbm[t, base + 320:base + 384, tsl],
                      reads=[allg], writes=[gv])
                r.op("vector", lambda e, tsl=tsl: e.tensor_mul(out=Et[:], in0=q2[:, tsl], in1=k2[:]),
                     reads=[gq, gk], writes=[gE])
                r.op("tensor", lambda e: e.matmul(
                    C.bank(0)[0:2, :], seg[:], Et[:], start=True, stop=True),
                    reads=[gE, gseg], writes=[C.bankg(0)])
                if t == 0:
                    r.op("vector", lambda e: e.tensor_scalar_mul(out=scb[:], in0=C.bank(0)[0:2, :], scalar1=cnorm),
                         reads=[C.bankg(0)], writes=[gsc])
                else:
                    r.op("vector", lambda e: e.scalar_tensor_tensor(
                        out=scb[:], in0=C.bank(0)[0:2, :], scalar=cnorm, in1=scb[:],
                        op0=ALU.mult, op1=ALU.add),
                        reads=[C.bankg(0)], writes=[gsc])
                r.op("tensor", lambda e: e.matmul(C.bank(1), indf[:], scb[:], start=True, stop=True),
                     reads=[gsc, gind], writes=[C.bankg(1)])
                r.op("vector", lambda e: e.tensor_copy(out=pb[:], in_=C.bank(1)),
                     reads=[C.bankg(1)], writes=[gpb])
                if t == 0:
                    r.op("vector", lambda e: e.tensor_mul(out=cz[:], in0=pb[:], in1=v2[:]),
                         reads=[gpb, gv], writes=[gcz])
                else:
                    r.op("vector", lambda e: e.tensor_mul(out=pv[:], in0=pb[:], in1=v2[:]),
                         reads=[gpb, gv], writes=[gpv])
                    r.op("vector", lambda e: e.tensor_add(out=cz[:], in0=cz[:], in1=pv[:]),
                         reads=[gpv], writes=[gcz])
            r.dma("sync", s1_hbm[hp * 128:(hp + 1) * 128, tsl], cz[:],
                  reads=[gcz], writes=[s1g])


NODE_SPEC = {
    0: {"edges": [(0, -1)], "q": 0, "k": 0, "v": 0},
    1: {"edges": [(2, 0), (1, -1)], "q": 2, "k": 1, "v": 2},
    2: {"edges": [(5, 1), (3, -1), (4, 0)], "q": 5, "k": 3, "v": 4},
    3: {"edges": [(9, 2), (6, -1), (8, 1)], "q": 9, "k": 6, "v": 8},
    4: {"edges": [(14, 3), (10, -1), (12, 1)], "q": 14, "k": 10, "v": 12},
    5: {"edges": [(19, 4), (15, 0), (17, 2)], "q": 19, "k": 15, "v": 17},
}
LAM = 4.5


def emit_l1f_prep(C, ew_ap, ewg, actw_ap, actwg):
    r = C.r
    row = C.sb("l1f_row", [1, 64], F32)
    gr = C.g("l1f_row")
    r.op("vector", lambda e: e.memset(row[:], 0.0), writes=[gr])
    r.dma("sync", row[0:1, 0:18], ew_ap, reads=[ewg], writes=[gr])
    r.dma("sync", row[0:1, 18:24], actw_ap, reads=[actwg], writes=[gr])
    cols = C.sb("l1f_sc", [128, 64], F32)
    gc = C.g("l1f_sc")
    r.op("tensor", lambda e: e.matmul(C.bank(7)[:, 0:64], C.consts["ones_row_f"][:], row[:],
                                      start=True, stop=True),
         reads=[gr, C.g("k_ones_row_f")], writes=[C.bankg(7)])
    r.op("vector", lambda e: e.tensor_copy(out=cols[:], in_=C.bank(7)[:, 0:64]),
         reads=[C.bankg(7)], writes=[gc])
    return cols, gc


def _edge_proj(C, in_hbm, ing, we_ap, weg, e_i, be_ap, beg, out_hbm, outg):
    r, T = C.r, C.T
    load_slab(C, "l1n_act", in_hbm, ing, DC, T)
    load_w(C, "l1n_w", we_ap[e_i], weg, DC, D)
    bb = C.sb("l1n_be", [128, 8], F32)
    r.dma("sync", bb[:], be_ap[e_i], reads=[beg], writes=[C.g("l1n_be")])
    bg = C.g("l1n_be")
    ev = C.sb("l1n_ev", [128, 512], BF16)
    gev = C.g("l1n_ev")

    def epi(oc, tg, bank, bankg):
        r.op("vector", lambda e, oc=oc, bank=bank: e.tensor_scalar_add(
            out=ev[:], in0=bank, scalar1=bb[:, oc:oc + 1]),
            reads=[bankg, bg], writes=[gev])
        r.dma("sync", out_hbm[oc * 128:(oc + 1) * 128, tg * 512:(tg + 1) * 512], ev[:],
              reads=[gev], writes=[outg])
    emit_proj(C, "l1n_act", "l1n_w", DC, D, epi, banks=(0, 1))


def _head_proj(C, in_hbm, ing, w_ap, wg_, scol, scolg, out_hbm, outg):
    r, T = C.r, C.T
    load_slab(C, "l1n_act", in_hbm, ing, DC, T)
    load_w(C, "l1n_w", w_ap, wg_, DC, D)
    ev = C.sb("l1n_ev", [128, 512], BF16)
    gev = C.g("l1n_ev")

    def epi(oc, tg, bank, bankg):
        r.op("vector", lambda e, bank=bank: e.tensor_scalar_mul(
            out=ev[:], in0=bank, scalar1=scol),
            reads=[bankg, scolg], writes=[gev])
        r.dma("sync", out_hbm[oc * 128:(oc + 1) * 128, tg * 512:(tg + 1) * 512], ev[:],
              reads=[gev], writes=[outg])
    emit_proj(C, "l1n_act", "l1n_w", DC, D, epi, banks=(0, 1))


def _vh_proj(C, in_hbm, ing, w_ap, wg_, scol, scolg, out_hbm, outg):
    r, T = C.r, C.T
    slab = C._sb["l1n_act"]
    sg_ = C.g("l1n_act")
    load_w(C, "l1n_w", w_ap, wg_, DC, D)
    wg2 = C.g("l1n_w")
    ev = C.sb("l1n_ev", [128, 512], BF16)
    gev = C.g("l1n_ev")
    wt = C._sb["l1n_w"]
    for tt in range(T // 128):
        for og in range(2):
            b = og
            for ic in range(DC):
                lh = slab[:, ic * T + tt * 128: ic * T + tt * 128 + 128]
                rh = wt[:, ic * D + og * 512: ic * D + (og + 1) * 512]
                r.op("tensor", lambda e, lh=lh, rh=rh, ic=ic, b=b: e.matmul(
                    C.bank(b), lh, rh,
                    start=(ic == 0), stop=(ic == DC - 1)),
                    reads=[sg_, wg2], writes=[C.bankg(b)])
            r.op("vector", lambda e, b=b: e.tensor_scalar_mul(
                out=ev[:], in0=C.bank(b), scalar1=scol),
                reads=[C.bankg(b), scolg], writes=[gev])
            r.dma("sync", out_hbm[tt * 128:(tt + 1) * 128, og * 512:(og + 1) * 512], ev[:],
                  reads=[gev], writes=[outg])


def emit_l1ffn_node(C, c, node_hbm, we_ap, weg, be_ap, beg,
                    wq_ap, wqg, wk_ap, wkg, wv_ap, wvg, wo_ap, wog,
                    ewcols, ewg, out_hbm, outg):
    r, T = C.r, C.T
    spec = NODE_SPEC[c]
    sc_p = {}
    for e_i, n_i in spec["edges"]:
        ap, g_in = node_hbm[n_i]
        name = "sc_p%d" % e_i
        p_ap, p_g = C.dram(name, [D, T], BF16)
        _edge_proj(C, ap, g_in, we_ap, weg, e_i, be_ap, beg, p_ap, p_g)
        sc_p[e_i] = (p_ap, p_g)
    qh_ap, qh_g = C.dram("sc_qh", [D, T], BF16)
    kh_ap, kh_g = C.dram("sc_kh", [D, T], BF16)
    vh_ap, vh_g = C.dram("sc_vh", [T, D], BF16)
    pq_ap, pq_g = sc_p[spec["q"]]
    _head_proj(C, pq_ap, pq_g, wq_ap[c], wqg, ewcols[:, 3 * c:3 * c + 1], ewg, qh_ap, qh_g)
    pk_ap, pk_g = sc_p[spec["k"]]
    _head_proj(C, pk_ap, pk_g, wk_ap[c], wkg, ewcols[:, 3 * c + 1:3 * c + 2], ewg, kh_ap, kh_g)
    pv_ap, pv_g = sc_p[spec["v"]]
    load_slab(C, "l1n_act", pv_ap, pv_g, DC, T)
    _vh_proj(C, pv_ap, pv_g, wv_ap[c], wvg, ewcols[:, 3 * c + 2:3 * c + 3], ewg, vh_ap, vh_g)

    NKC = T // 128
    kh_s = C.sb("l1a_k", [128, T], BF16)
    qh_s = C.sb("l1a_q", [128, 2 * 512], BF16)
    vh_s = C.sb("l1a_v", [128, NKC * 128], BF16)
    psl = C.sb("l1a_p", [128, NKC * 512], BF16)
    Rw = C.sb("l1a_R", [128, 512], F32)
    c0 = C.sb("l1n_oev", [128, 512], F32)
    sqs = C.sb("l1a_sq", [128, 512], F32)
    strow = C.sb("l1a_st", [1, 8], F32)
    bcol = C.sb("l1a_b", [128, 1], F32)
    rrow = C.sb("l1a_rr", [1, 512], F32)
    rb = C.sb("l1a_rb", [128, 512], F32)
    ctxs = C.sb("l1a_ctx", [128, DC * T], BF16)
    gk, gq, gv, gp, gR, gc0, gsq, gst, gb, grr, grb, gctx = (
        C.g("l1a_k"), C.g("l1a_q"), C.g("l1a_v"), C.g("l1a_p"), C.g("l1a_R"),
        C.g("l1n_oev"), C.g("l1a_sq"), C.g("l1a_st"), C.g("l1a_b"), C.g("l1a_rr"),
        C.g("l1a_rb"), C.g("l1a_ctx"))
    ones_f = C.consts["ones_col_f"]
    ones_row_f = C.consts["ones_row_f"]
    g1c, g1r = C.g("k_ones_col_f"), C.g("k_ones_row_f")
    iscale = 128.0 ** -0.5
    inv_n = 1.0 / (128 * 512)
    for h in range(DC):
        hsl = slice(h * 128, (h + 1) * 128)
        r.dma("sync", kh_s[:], kh_ap[hsl, :], reads=[kh_g], writes=[gk])
        r.dma("sync", qh_s[:, 0:512], qh_ap[hsl, 0:512], reads=[qh_g], writes=[gq])
        r.dma("sync", vh_s[:].rearrange("p (kc d) -> p kc d", kc=NKC),
              vh_ap[:, hsl].rearrange("(kc p) d -> p kc d", p=128),
              reads=[vh_g], writes=[gv])
        for x in range(1):
            r.op("tensor", lambda e: e.matmul(
                C.bank(5), kh_s[:, 0:128], qh_s[:, 0:512], start=True, stop=True),
                reads=[gk, gq], writes=[C.bankg(5)])
            r.op("vector", lambda e: e.tensor_copy(out=c0[:], in_=C.bank(5)),
                 reads=[C.bankg(5)], writes=[gc0])
            r.op("vector", lambda e: e.tensor_mul(out=sqs[:], in0=c0[:], in1=c0[:]),
                 reads=[gc0], writes=[gsq])
            r.op("tensor", lambda e: e.matmul(C.bank(6)[0:1, :], ones_f[:], c0[:],
                                              start=True, stop=True),
                 reads=[gc0, g1c], writes=[C.bankg(6)])
            r.op("tensor", lambda e: e.matmul(C.bank(7)[0:1, :], ones_f[:], sqs[:],
                                              start=True, stop=True),
                 reads=[gsq, g1c], writes=[C.bankg(7)])
            r.op("vector", lambda e: e.reduce_sum(out=strow[0:1, 0:1], in_=C.bank(6)[0:1, :], axis=AX.X),
                 reads=[C.bankg(6)], writes=[gst])
            r.op("vector", lambda e: e.reduce_sum(out=strow[0:1, 1:2], in_=C.bank(7)[0:1, :], axis=AX.X),
                 reads=[C.bankg(7)], writes=[gst])
            kv = C.consts["kvals"]
            kvg = C.g("k_kvals")
            r.op("scalar", lambda e: e.mul(out=strow[0:1, 2:3], in_=strow[0:1, 0:1], mul=inv_n),
                 reads=[gst], writes=[gst])
            r.op("scalar", lambda e: e.mul(out=strow[0:1, 3:4], in_=strow[0:1, 1:2], mul=inv_n),
                 reads=[gst], writes=[gst])
            r.op("scalar", lambda e: e.activation(out=strow[0:1, 4:5], in_=strow[0:1, 2:3], func=AF.Square),
                 reads=[gst], writes=[gst])
            r.op("scalar", lambda e: e.mul(out=strow[0:1, 4:5], in_=strow[0:1, 4:5], mul=-1.0),
                 reads=[gst], writes=[gst])
            r.op("scalar", lambda e: e.activation(out=strow[0:1, 5:6], in_=strow[0:1, 3:4],
                                                  func=AF.Identity, bias=strow[0:1, 4:5]),
                 reads=[gst], writes=[gst])
            r.op("scalar", lambda e, kv=kv: e.activation(out=strow[0:1, 5:6], in_=strow[0:1, 5:6],
                                                         func=AF.Ln, bias=kv[0:1, 0:1]),
                 reads=[gst, kvg], writes=[gst])
            r.op("scalar", lambda e: e.activation(out=strow[0:1, 5:6], in_=strow[0:1, 5:6],
                                                  func=AF.Exp, scale=0.5),
                 reads=[gst], writes=[gst])
            r.op("scalar", lambda e: e.mul(out=strow[0:1, 6:7], in_=strow[0:1, 5:6], mul=LAM),
                 reads=[gst], writes=[gst])
            r.op("scalar", lambda e: e.activation(out=strow[0:1, 3:4], in_=strow[0:1, 2:3],
                                                  func=AF.Identity, bias=strow[0:1, 6:7]),
                 reads=[gst], writes=[gst])
            r.op("scalar", lambda e: e.mul(out=strow[0:1, 7:8], in_=strow[0:1, 3:4], mul=-iscale),
                 reads=[gst], writes=[gst])
            r.op("tensor", lambda e: e.matmul(C.bank(5)[:, 0:1], ones_row_f[:], strow[0:1, 7:8],
                                              start=True, stop=True),
                 reads=[gst, g1r], writes=[C.bankg(5)])
            r.op("vector", lambda e: e.tensor_copy(out=bcol[:], in_=C.bank(5)[:, 0:1]),
                 reads=[C.bankg(5)], writes=[gb])
        for qg in range(C.NTG):
            qpar = qg % 2
            qbuf = qh_s[:, qpar * 512:(qpar + 1) * 512]
            if qg > 0:
                r.dma("sync", qbuf, qh_ap[hsl, qg * 512:(qg + 1) * 512],
                      reads=[qh_g], writes=[gq])
            for kc in range(NKC):
                b = kc % 2
                r.op("tensor", lambda e, kc=kc, b=b, qbuf=qbuf: e.matmul(
                    C.bank(b), kh_s[:, kc * 128:(kc + 1) * 128], qbuf,
                    start=True, stop=True),
                    reads=[gk, gq], writes=[C.bankg(b)])
                r.op("scalar", lambda e, kc=kc, b=b: e.activation(
                    out=psl[:, kc * 512:(kc + 1) * 512], in_=C.bank(b), func=AF.Exp,
                    bias=bcol[:, 0:1], scale=iscale),
                    reads=[C.bankg(b), gb], writes=[gp])
                if kc == 0:
                    r.op("vector", lambda e, kc=kc: e.tensor_scalar_mul(
                        out=Rw[:], in0=psl[:, kc * 512:(kc + 1) * 512], scalar1=1.0),
                        reads=[gp], writes=[gR])
                else:
                    r.op("vector", lambda e, kc=kc: e.scalar_tensor_tensor(
                        out=Rw[:], in0=psl[:, kc * 512:(kc + 1) * 512], scalar=1.0,
                        in1=Rw[:], op0=ALU.mult, op1=ALU.add),
                        reads=[gp], writes=[gR])
            for kc in range(NKC):
                r.op("tensor", lambda e, kc=kc: e.matmul(
                    C.bank(2), vh_s[:, kc * 128:(kc + 1) * 128],
                    psl[:, kc * 512:(kc + 1) * 512],
                    start=(kc == 0), stop=(kc == NKC - 1)),
                    reads=[gv, gp], writes=[C.bankg(2)])
            r.op("tensor", lambda e: e.matmul(C.bank(3)[0:1, :], ones_f[:], Rw[:],
                                              start=True, stop=True),
                 reads=[gR, g1c], writes=[C.bankg(3)])
            r.op("vector", lambda e: e.reciprocal(out=rrow[:], in_=C.bank(3)[0:1, :]),
                 reads=[C.bankg(3)], writes=[grr])
            r.op("tensor", lambda e: e.matmul(C.bank(4), ones_row_f[:], rrow[:],
                                              start=True, stop=True),
                 reads=[grr, g1r], writes=[C.bankg(4)])
            r.op("vector", lambda e: e.tensor_copy(out=rb[:], in_=C.bank(4)),
                 reads=[C.bankg(4)], writes=[grb])
            r.op("vector", lambda e, h=h, qg=qg: e.tensor_mul(
                out=ctxs[:, h * T + qg * 512: h * T + qg * 512 + 512],
                in0=C.bank(2), in1=rb[:]),
                reads=[C.bankg(2), grb], writes=[gctx])

    load_w(C, "l1n_w", wo_ap[c], wog, DC, D)
    pqt = C.sb("l1n_pqt", [128, 512], BF16)
    gpqt = C.g("l1n_pqt")
    oev = C.sb("l1n_oev", [128, 512], F32)
    goev = C.g("l1n_oev")
    nev = C.sb("l1n_ev", [128, 512], BF16)
    gnev = C.g("l1n_ev")

    def epi(oc, tg, bank, bankg):
        tsl = slice(tg * 512, (tg + 1) * 512)
        r.dma("sync", pqt[:], pq_ap[oc * 128:(oc + 1) * 128, tsl], reads=[pq_g], writes=[gpqt])
        r.op("vector", lambda e, bank=bank, c=c: e.tensor_scalar_mul(
            out=oev[:], in0=bank, scalar1=ewcols[:, 18 + c:19 + c]),
            reads=[bankg, ewg], writes=[goev])
        r.op("vector", lambda e, c=c: e.scalar_tensor_tensor(
            out=nev[:], in0=pqt[:], scalar=ewcols[:, 3 * c:3 * c + 1], in1=oev[:],
            op0=ALU.mult, op1=ALU.add),
            reads=[gpqt, ewg, goev], writes=[gnev])
        r.dma("sync", out_hbm[oc * 128:(oc + 1) * 128, tsl], nev[:],
              reads=[gnev], writes=[outg])
    emit_proj(C, "l1a_ctx", "l1n_w", DC, D, epi, banks=(0, 1))


def emit_final(C, n5_hbm, n5g, lng_ap, lngg, lnb_ap, lnbg, s1_hbm, s1g,
               x1_hbm, x1g, out_hbm, outg):
    r, T = C.r, C.T
    ln5_ap, ln5g = C.dram("sc_ln5", [D, T], F32)
    emit_ln(C, n5_hbm, n5g, ln5_ap, ln5g, eps=1e-6, gamma_beta=(lng_ap, lnb_ap),
            tag="ln5", in_dtype=BF16, out_dtype=F32)
    lch = C.sb("fin_l", [128, 512], F32)
    sch = C.sb("fin_s", [128, 512], F32)
    xch = C.sb("fin_x", [128, 512], F32)
    glc, gsc, gxc = C.g("fin_l"), C.g("fin_s"), C.g("fin_x")
    for tg in range(C.NTG):
        tsl = slice(tg * 512, (tg + 1) * 512)
        for oc in range(DC):
            csl = slice(oc * 128, (oc + 1) * 128)
            r.dma("sync", lch[:], ln5_ap[csl, tsl], reads=[ln5g], writes=[glc])
            r.dma("sync", sch[:], s1_hbm[csl, tsl], reads=[s1g], writes=[gsc])
            r.dma("sync", xch[:], x1_hbm[csl, tsl], reads=[x1g], writes=[gxc])
            r.op("scalar", lambda e: e.activation(out=lch[:], in_=lch[:], func=AF.Silu),
                 reads=[glc], writes=[glc])
            r.op("vector", lambda e: e.tensor_mul(out=lch[:], in0=lch[:], in1=sch[:]),
                 reads=[gsc], writes=[glc])
            r.op("vector", lambda e: e.tensor_add(out=lch[:], in0=lch[:], in1=xch[:]),
                 reads=[gxc], writes=[glc])
            r.dma("sync", out_hbm[csl, tsl], lch[:], reads=[glc], writes=[outg])



def _new(T, consts=None):
    nc = bass.Bass("TRN2", target_bir_lowering=False, debug=False,
                   detect_race_conditions=False)
    stack = ExitStack()
    stack.__enter__()
    C = Ctx(nc, stack, T=T)
    declare_consts(C, only=consts)
    return nc, stack, C


def _ein(C, name, shape, dtype):
    ap = C.nc.dram_tensor(name, shape, dtype, kind="ExternalInput").ap()
    return ap, G(name)


def _eout(C, name, shape, dtype):
    ap = C.nc.dram_tensor(name, shape, dtype, kind="ExternalOutput").ap()
    return ap, G(name)


def build_ln0(T):
    nc, stack, C = _new(T)
    x, xg = _ein(C, "xT", [D, T], F32)
    xn, xng = _eout(C, "xn0T", [D, T], BF16)
    C.push(); emit_ln(C, x, xg, xn, xng, eps=1e-5, tag="ln0"); C.pop()
    C.finish(); stack.__exit__(None, None, None)
    return nc, ["xn0T"]


def build_l0attn(T):
    nc, stack, C = _new(T)
    xn, xng = _ein(C, "xn0T", [D, T], BF16)
    wq, wqg = _ein(C, "w_q", [D, D], F32)
    wk, wkg = _ein(C, "w_k", [D, D], F32)
    wv, wvg = _ein(C, "w_v", [D, D], F32)
    b0, b0g = _ein(C, "b_qkv0", [128, 24], F32)
    s, sg_ = _eout(C, "s0T", [D, T], F32)
    C.push(); emit_l0attn(C, xn, xng, s, sg_, wq, wqg, wk, wkg, wv, wvg, b0, b0g); C.pop()
    C.finish(); stack.__exit__(None, None, None)
    return nc, ["s0T"]


def build_l0ffn(T):
    nc, stack, C = _new(T)
    xn, xng = _ein(C, "xn0T", [D, T], BF16)
    x, xg = _ein(C, "xT", [D, T], F32)
    s, sg_ = _ein(C, "s0T", [D, T], F32)
    rw, rwg = _ein(C, "router_wT", [D, 64], F32)
    rb, rbg = _ein(C, "router_b", [1, 64], F32)
    gw, gwg = _ein(C, "gate_wT", [32, D, 128], F32)
    uw, uwg = _ein(C, "up_wT", [32, D, 128], F32)
    dw, dwg = _ein(C, "down_wT", [32, 128, D], F32)
    sga, sgag = _ein(C, "sgate_wT", [D, 512], F32)
    sua, suag = _ein(C, "sup_wT", [D, 512], F32)
    sda, sdag = _ein(C, "sdown_wT", [512, D], F32)
    x1, x1g = _eout(C, "x1T", [D, T], F32)
    C.push()
    emit_l0ffn(C, xn, xng, x, xg, s, sg_, x1, x1g, rw, rwg, rb, rbg,
               gw, gwg, uw, uwg, dw, dwg, sga, sgag, sua, suag, sda, sdag)
    C.pop(); C.finish(); stack.__exit__(None, None, None)
    return nc, ["x1T"]


def build_ln1(T):
    nc, stack, C = _new(T)
    x, xg = _ein(C, "x1T", [D, T], F32)
    xn, xng = _eout(C, "xn1T", [D, T], BF16)
    C.push(); emit_ln(C, x, xg, xn, xng, eps=1e-5, tag="ln1"); C.pop()
    C.finish(); stack.__exit__(None, None, None)
    return nc, ["xn1T"]


def build_l1qkv(T):
    nc, stack, C = _new(T)
    xn, xng = _ein(C, "xn1T", [D, T], BF16)
    w, wg_ = _ein(C, "wqkv1T", [D, 3072], F32)
    b, bg_ = _ein(C, "b_qkv1", [128, 24], F32)
    qkv, qkvg = _eout(C, "qkv1T", [3072, T], BF16)
    C.push(); emit_l1qkv(C, xn, xng, w, wg_, b, bg_, qkv, qkvg); C.pop()
    C.finish(); stack.__exit__(None, None, None)
    return nc, ["qkv1T"]


def build_l1mix(T):
    nc, stack, C = _new(T)
    qkv, qkvg = _ein(C, "qkv1T", [3072, T], BF16)
    alla, allg = _ein(C, "qkv_all", [8, 3072, T], BF16)
    s1, s1g = _eout(C, "s1T", [D, T], F32)
    C.push(); emit_l1mix(C, qkv, qkvg, alla, allg, s1, s1g); C.pop()
    C.finish(); stack.__exit__(None, None, None)
    return nc, ["s1T"]


def build_node(T, c):
    nc, stack, C = _new(T, consts={"ones_col_f", "ones_row_f", "kvals"})
    spec = NODE_SPEC[c]
    needed = sorted({n_i for (_, n_i) in spec["edges"]})
    node_hbm = {}
    for n_i in needed:
        nm = "xn1T" if n_i == -1 else "n%dT" % n_i
        ap, g_ = _ein(C, nm, [D, T], BF16)
        node_hbm[n_i] = (ap, g_)
    we, weg = _ein(C, "l1f_We", [20, D, D], F32)
    be, beg = _ein(C, "l1f_be", [20, 128, 8], F32)
    wq, wqg = _ein(C, "l1f_Wq", [6, D, D], F32)
    wk, wkg = _ein(C, "l1f_Wk", [6, D, D], F32)
    wv, wvg = _ein(C, "l1f_Wv", [6, D, D], F32)
    wo, wog = _ein(C, "l1f_Wo", [6, D, D], F32)
    ew, ewg_ = _ein(C, "l1f_ew", [1, 18], F32)
    aw, awg = _ein(C, "l1f_actw", [1, 6], F32)
    out, outg = _eout(C, "n%dT" % c, [D, T], BF16)
    C.push()
    ewcols, ewg = emit_l1f_prep(C, ew, ewg_, aw, awg)
    emit_l1ffn_node(C, c, node_hbm, we, weg, be, beg, wq, wqg, wk, wkg,
                    wv, wvg, wo, wog, ewcols, ewg, out, outg)
    C.pop(); C.finish(); stack.__exit__(None, None, None)
    return nc, ["n%dT" % c]


def build_final(T):
    nc, stack, C = _new(T)
    n5, n5g = _ein(C, "n5T", [D, T], BF16)
    lng, lngg = _ein(C, "l1f_lng", [128, 8], F32)
    lnb, lnbg = _ein(C, "l1f_lnb", [128, 8], F32)
    s1, s1g = _ein(C, "s1T", [D, T], F32)
    x1, x1g = _ein(C, "x1T", [D, T], F32)
    out, outg = _eout(C, "outT", [D, T], F32)
    C.push()
    emit_final(C, n5, n5g, lng, lngg, lnb, lnbg, s1, s1g, x1, x1g, out, outg)
    C.pop(); C.finish(); stack.__exit__(None, None, None)
    return nc, ["outT"]


NODE_INPUT_NAMES = {
    c: sorted({"xn1T" if n_i == -1 else "n%dT" % n_i for (_, n_i) in NODE_SPEC[c]["edges"]})
    for c in range(6)
}

PHASES = [
    ("ln0", build_ln0, ["xT"], ["xn0T"]),
    ("l0attn", build_l0attn, ["xn0T", "w_q", "w_k", "w_v", "b_qkv0"], ["s0T"]),
    ("l0ffn", build_l0ffn, ["xn0T", "xT", "s0T", "router_wT", "router_b", "gate_wT",
                            "up_wT", "down_wT", "sgate_wT", "sup_wT", "sdown_wT"], ["x1T"]),
    ("ln1", build_ln1, ["x1T"], ["xn1T"]),
    ("l1qkv", build_l1qkv, ["xn1T", "wqkv1T", "b_qkv1"], ["qkv1T"]),
    ("l1mix", build_l1mix, ["qkv1T", "qkv_all"], ["s1T"]),
] + [
    ("node%d" % c, (lambda T, c=c: build_node(T, c)),
     NODE_INPUT_NAMES[c] + ["l1f_We", "l1f_be", "l1f_Wq", "l1f_Wk", "l1f_Wv",
                            "l1f_Wo", "l1f_ew", "l1f_actw"], ["n%dT" % c])
    for c in range(6)
] + [
    ("final", build_final, ["n5T", "l1f_lng", "l1f_lnb", "s1T", "x1T"], ["outT"]),
]


def kernel(**inputs):
    T = TFULL
    ncore = 8
    consts = host_consts(T)
    per_core = [host_inputs(i, inputs, T) for i in range(ncore)]
    state = [dict(consts, **pc) for pc in per_core]
    for name, build_fn, in_names, out_names in PHASES:
        nc, outs = build_fn(T)
        cn = set(consts.keys())
        in_maps = []
        for i in range(ncore):
            m = {k: state[i][k] for k in in_names}
            m.update({k: consts[k] for k in cn})
            in_maps.append(m)
        res = run_bass_kernel_spmd(nc, in_maps, core_ids=list(range(ncore)))
        for i in range(ncore):
            for k in out_names:
                state[i][k] = np.asarray(res.results[i][k])
        if name == "l1qkv":
            alln = np.stack([state[i]["qkv1T"] for i in range(ncore)])
            for i in range(ncore):
                state[i]["qkv_all"] = alln
    out = np.stack([np.ascontiguousarray(state[i]["outT"].T) for i in range(ncore)])
    return out


# version 09
# speedup vs baseline: 1.0049x; 1.0049x over previous
import numpy as np
import ml_dtypes
from contextlib import ExitStack

import concourse.bass as bass
import concourse.mybir as mybir
from concourse.bass_utils import run_bass_kernel_spmd

F32 = mybir.dt.float32
BF16 = mybir.dt.bfloat16
ALU = mybir.AluOpType
AF = mybir.ActivationFunctionType
AX = mybir.AxisListType

NEG = -1.0e30
D = 1024
DC = D // 128
TFULL = 4096

ENGS = ("sync", "vector", "scalar", "gpsimd", "tensor")
DMA_LANES = {"sync": ("qs0", "qs1", "qs2", "qs3"),
             "gpsimd": ("qg0", "qg1", "qg2", "qg3"),
             "vector": ("qv0",), "scalar": ("qc0",)}
SEM_NAMES = ("qs0", "qs1", "qs2", "qs3", "qg0", "qg1", "qg2", "qg3",
             "qv0", "qc0", "cv", "cs", "cg", "ct")
CMP_SEM = {"vector": "cv", "scalar": "cs", "gpsimd": "cg", "tensor": "ct"}


class G:
    __slots__ = ("w", "r", "name")

    def __init__(self, name=""):
        self.w = None
        self.r = []
        self.name = name


class Rec:
    def __init__(self):
        self.ops = {e: [] for e in ENGS}
        self.cnt = {s: 0 for s in SEM_NAMES}
        self.waited = {e: {s: 0 for s in SEM_NAMES} for e in ENGS}
        self.lane_i = {e: 0 for e in DMA_LANES}
        self.sems = None
        self.nops = 0
        self.fence = []

    def _wait(self, eng, tok, cur_stream, enforce_same_queue=True):
        if tok is None:
            return
        sem, val, stream = tok
        if stream == cur_stream:
            if stream.startswith("c:") or not enforce_same_queue:
                return
        if self.waited[eng][sem] >= val:
            return
        self.waited[eng][sem] = val

        def fn(e, sem=sem, val=val):
            e.wait_ge(self.sems[sem], val)
        self.ops[eng].append(fn)

    PARANOID_COMPUTE = False

    def op(self, eng, fn, reads=(), writes=(), dma=False):
        if dma:
            lanes = DMA_LANES[eng]
            sem = lanes[self.lane_i[eng] % len(lanes)]
            self.lane_i[eng] += 1
            amt = 16
            stream = "d:" + eng
            prev = self.cnt[sem]
            if prev > 0 and self.waited[eng][sem] < prev:
                self.waited[eng][sem] = prev
                self.ops[eng].append(
                    lambda e, sem=sem, prev=prev: e.wait_ge(self.sems[sem], prev))
        else:
            sem = CMP_SEM[eng]
            amt = 1
            stream = "c:" + eng
        if Rec.PARANOID_COMPUTE and not dma:
            for s2 in SEM_NAMES:
                val = self.cnt[s2]
                if val > 0 and self.waited[eng][s2] < val:
                    self.waited[eng][s2] = val
                    self.ops[eng].append(
                        lambda e, s2=s2, val=val: e.wait_ge(self.sems[s2], val))
        for b in reads:
            self._wait(eng, b.w, stream)
        for b in writes:
            if b.w is None and not b.r:
                for t in self.fence:
                    self._wait(eng, t, stream)
            self._wait(eng, b.w, stream, enforce_same_queue=False)
            for t in b.r:
                self._wait(eng, t, stream)
        self.cnt[sem] += amt
        tok = (sem, self.cnt[sem], stream)

        def wrapped(e, fn=fn, sem=sem, amt=amt):
            fn(e).then_inc(self.sems[sem], amt)
        self.ops[eng].append(wrapped)
        self.nops += 1
        for b in writes:
            b.w = tok
            b.r = []
        for b in reads:
            b.r.append(tok)
        return tok

    PARANOID_DMA = False

    def dma(self, eng, out, in_, reads=(), writes=(), **kw):
        if Rec.PARANOID_DMA:
            s = DMA_SEM[eng]
            val = self.cnt[s]
            if val > 0 and self.waited[eng][s] < val:
                self.waited[eng][s] = val
                self.ops[eng].append(
                    lambda e, s=s, val=val: e.wait_ge(self.sems[s], val))
        return self.op(eng, lambda e, out=out, in_=in_, kw=kw: e.dma_start(out=out, in_=in_, **kw),
                       reads=reads, writes=writes, dma=True)

    def barrier_token(self):
        return [(s, self.cnt[s], "x") for s in SEM_NAMES if self.cnt[s] > 0]

    def final_drain(self, eng="sync"):
        for s in SEM_NAMES:
            if self.cnt[s] > 0:
                val = self.cnt[s]
                if self.waited[eng][s] >= val:
                    continue
                self.waited[eng][s] = val

                def fn(e, s=s, val=val):
                    e.wait_ge(self.sems[s], val)
                self.ops[eng].append(fn)

    def build(self, nc, stack):
        self.sems = {s: stack.enter_context(nc.semaphore(s)) for s in SEM_NAMES}
        block = stack.enter_context(nc.Block())
        for eng in ENGS:
            ops = self.ops[eng]
            if not ops:
                continue

            def replay(e, ops=ops):
                for fn in ops:
                    fn(e)
            getattr(block, eng)(replay)


class Ctx:

    def __init__(self, nc, stack, T=TFULL):
        self.nc = nc
        self.stack = stack
        self.r = Rec()
        self.T = T
        self.NTG = T // 512
        self.NQT = T // 128
        self._sb = {}
        self._g = {}
        self._scopes = []
        self.pb = stack.enter_context(nc.psum_tensor("psum_all", [128, 4096], F32))
        self.consts = None
        self.hbm = {}

    def push(self):
        self._scopes.append((ExitStack(), []))

    def pop(self):
        st, names = self._scopes.pop()
        for n in names:
            del self._sb[n]
            del self._g["sb:" + n]
        st.close()
        self.r.fence = self.r.barrier_token()

    _uniq = 0

    def sb(self, name, shape, dtype):
        if name not in self._sb:
            if self._scopes:
                st, names = self._scopes[-1]
                names.append(name)
            else:
                st = self.stack
            Ctx._uniq += 1
            t = st.enter_context(self.nc.sbuf_tensor(
                "%s__%d" % (name, Ctx._uniq), shape, dtype))
            self._sb[name] = t
            self._g["sb:" + name] = G("sb:" + name)
        return self._sb[name]

    def g(self, name):
        return self._g["sb:" + name]

    def bankg(self, i):
        k = "pb:%d" % i
        if k not in self._g:
            self._g[k] = G(k)
        return self._g[k]

    def bank(self, i, n=1):
        return self.pb[:, 512 * i:512 * (i + n)]

    def dram(self, name, shape, dtype, kind="Internal"):
        if name not in self.hbm:
            ap = self.nc.dram_tensor(name, shape, dtype, kind=kind).ap()
            self.hbm[name] = (ap, G("hbm:" + name))
        return self.hbm[name]

    def finish(self):
        self.r.final_drain("sync")
        self.r.build(self.nc, self.stack)



def _pm(vec):
    v = np.asarray(vec)
    n = v.shape[-1] // 128
    return np.ascontiguousarray(v.reshape(n, 128).T)


def host_consts(T=TFULL):
    c = {}
    c["ones_col_f"] = np.ones((128, 1), np.float32)
    c["ones_row_f"] = np.ones((1, 128), np.float32)
    c["ones_row_b"] = np.ones((1, 128), ml_dtypes.bfloat16)
    c["ones_col_b"] = np.ones((128, 1), ml_dtypes.bfloat16)
    f = np.arange(512)[None, :]
    p = np.arange(128)[:, None]
    masks = [np.where(f <= o + p, 0.0, NEG).astype(np.float32) for o in (0, 128, 256, 384)]
    c["cmask"] = np.stack(masks)
    qt = np.arange(T // 128)[None, :]
    c["countdown"] = ((T - 1) - (qt * 128 + p)).astype(np.float32)
    seg = np.zeros((128, 2), np.float32)
    seg[:64, 0] = 1.0
    seg[64:, 1] = 1.0
    c["seg_lhsT"] = seg.astype(ml_dtypes.bfloat16)
    c["ind_lhsT"] = np.ascontiguousarray(seg.T).astype(ml_dtypes.bfloat16)
    c["ind_f"] = np.ascontiguousarray(seg.T)
    c["ident_f"] = np.eye(128, dtype=np.float32)
    kv = np.zeros((128, 8), np.float32)
    kv[:, 0] = 1e-12
    c["kvals"] = kv
    return c



def host_inputs(core, inputs, T=TFULL):
    g = lambda k: np.asarray(inputs[k])
    m = {}
    x = g("L0_attn_X")[core]
    m["xT"] = np.ascontiguousarray(x.T).astype(np.float32)
    m["w_q"] = g("L0_attn_Wq").astype(np.float32)
    m["w_k"] = g("L0_attn_Wk").astype(np.float32)
    m["w_v"] = g("L0_attn_Wv").astype(np.float32)
    m["b_qkv0"] = np.concatenate(
        [_pm(g("L0_attn_bq")), _pm(g("L0_attn_bk")), _pm(g("L0_attn_bv"))],
        axis=1).astype(np.float32)
    rwt = np.zeros((1024, 64), np.float32)
    rwt[:, :32] = g("L0_ffn_router_w").T
    m["router_wT"] = rwt
    rbp = np.zeros((1, 64), np.float32)
    rbp[0, :32] = g("L0_ffn_router_bias")
    m["router_b"] = rbp
    m["gate_wT"] = np.ascontiguousarray(g("L0_ffn_gate_w").transpose(0, 2, 1)).astype(np.float32)
    m["up_wT"] = np.ascontiguousarray(g("L0_ffn_up_w").transpose(0, 2, 1)).astype(np.float32)
    m["down_wT"] = np.ascontiguousarray(g("L0_ffn_down_w").transpose(0, 2, 1)).astype(np.float32)
    m["sgate_wT"] = np.ascontiguousarray(g("L0_ffn_sgate_w").T).astype(np.float32)
    m["sup_wT"] = np.ascontiguousarray(g("L0_ffn_sup_w").T).astype(np.float32)
    m["sdown_wT"] = np.ascontiguousarray(g("L0_ffn_sdown_w").T).astype(np.float32)
    m["wqkv1T"] = np.ascontiguousarray(g("L1_attn_W_qkv").T).astype(np.float32)
    m["b_qkv1"] = _pm(g("L1_attn_b_qkv")).astype(np.float32)
    m["l1f_We"] = g("L1_ffn_We").astype(np.float32)
    m["l1f_be"] = np.ascontiguousarray(
        g("L1_ffn_be").reshape(20, 8, 128).transpose(0, 2, 1)).astype(np.float32)
    m["l1f_Wq"] = g("L1_ffn_Wq").astype(np.float32)
    m["l1f_Wk"] = g("L1_ffn_Wk").astype(np.float32)
    m["l1f_Wv"] = g("L1_ffn_Wv").astype(np.float32)
    m["l1f_Wo"] = g("L1_ffn_Wo").astype(np.float32)
    m["l1f_lng"] = _pm(g("L1_ffn_ln_g")).astype(np.float32)
    m["l1f_lnb"] = _pm(g("L1_ffn_ln_b")).astype(np.float32)
    m["l1f_ew"] = np.ascontiguousarray(g("L1_ffn_ew").reshape(1, 18)).astype(np.float32)
    m["l1f_actw"] = np.ascontiguousarray(g("L1_ffn_act_w")[None, :]).astype(np.float32)
    return m


CONST_SPECS = {
    "ones_col_f": ([128, 1], F32),
    "ones_row_f": ([1, 128], F32),
    "ones_row_b": ([1, 128], BF16),
    "ones_col_b": ([128, 1], BF16),
    "cmask": ([4, 128, 512], F32),
    "countdown": (None, F32),
    "seg_lhsT": ([128, 2], BF16),
    "ind_lhsT": ([2, 128], BF16),
    "ind_f": ([2, 128], F32),
    "ident_f": ([128, 128], F32),
    "kvals": ([128, 8], F32),
}


def declare_consts(C, only=None):
    nc, r = C.nc, C.r
    c = {}
    for name, (shape, dt) in CONST_SPECS.items():
        if only is not None and name not in only:
            continue
        if name == "countdown":
            shape = [128, C.T // 128]
        ap = nc.dram_tensor(name, shape, dt, kind="ExternalInput").ap()
        if name == "cmask":
            t = C.sb("k_cmask", [128, 4 * 512], F32)
            for gi in range(4):
                r.dma("sync", t[:, gi * 512:(gi + 1) * 512], ap[gi], writes=[C.g("k_cmask")])
        else:
            t = C.sb("k_" + name, shape, dt)
            r.dma("sync", t[:], ap, writes=[C.g("k_" + name)])
        c[name] = t
    C.consts = c
    return c



def emit_ln(C, x_hbm, xg, xn_hbm, xng, eps, gamma_beta=None, tag="ln",
            in_dtype=F32, out_dtype=BF16):
    r = C.r
    xb = C.sb(tag + "_x", [128, DC * 512], F32)
    if in_dtype != F32:
        xbr = C.sb(tag + "_xr", [128, DC * 512], in_dtype)
        gxbr = C.g(tag + "_xr")
    sq = C.sb(tag + "_sq", [128, DC * 512], F32)
    stats = C.sb(tag + "_st", [1, 1024], F32)
    ab = C.sb(tag + "_ab", [128, 1024], F32)
    xnb = C.sb(tag + "_xn", [128, DC * 512], out_dtype)
    gxb, gsq, gst, gab, gxn = (C.g(tag + "_x"), C.g(tag + "_sq"), C.g(tag + "_st"),
                               C.g(tag + "_ab"), C.g(tag + "_xn"))
    if gamma_beta is not None:
        gb = C.sb(tag + "_gb", [128, 16], F32)
        ggb = C.g(tag + "_gb")
        r.dma("sync", gb[:, 0:8], gamma_beta[0], writes=[ggb])
        r.dma("sync", gb[:, 8:16], gamma_beta[1], writes=[ggb])
    ones_f = C.consts["ones_col_f"]
    ones_row_f = C.consts["ones_row_f"]
    konesg = C.g("k_ones_col_f")
    inv_d = 1.0 / D
    for tg in range(C.NTG):
        if in_dtype == F32:
            for ic in range(DC):
                r.dma("sync", xb[:, ic * 512:(ic + 1) * 512],
                      x_hbm[ic * 128:(ic + 1) * 128, tg * 512:(tg + 1) * 512],
                      reads=[xg], writes=[gxb])
        else:
            for ic in range(DC):
                r.dma("sync", xbr[:, ic * 512:(ic + 1) * 512],
                      x_hbm[ic * 128:(ic + 1) * 128, tg * 512:(tg + 1) * 512],
                      reads=[xg], writes=[gxbr])
            for ic in range(DC):
                sl = slice(ic * 512, (ic + 1) * 512)
                r.op("vector", lambda e, sl=sl: e.tensor_copy(out=xb[:, sl], in_=xbr[:, sl]),
                     reads=[gxbr], writes=[gxb])
        for ic in range(DC):
            sl = slice(ic * 512, (ic + 1) * 512)
            r.op("vector", lambda e, sl=sl: e.tensor_mul(out=sq[:, sl], in0=xb[:, sl], in1=xb[:, sl]),
                 reads=[gxb], writes=[gsq])
        for ic in range(DC):
            sl = slice(ic * 512, (ic + 1) * 512)
            r.op("tensor", lambda e, sl=sl, ic=ic: e.matmul(
                C.bank(0)[0:1, :], ones_f[:], xb[:, sl], start=(ic == 0), stop=(ic == DC - 1)),
                reads=[gxb, konesg], writes=[C.bankg(0)])
        for ic in range(DC):
            sl = slice(ic * 512, (ic + 1) * 512)
            r.op("tensor", lambda e, sl=sl, ic=ic: e.matmul(
                C.bank(1)[0:1, :], ones_f[:], sq[:, sl], start=(ic == 0), stop=(ic == DC - 1)),
                reads=[gsq, konesg], writes=[C.bankg(1)])
        r.op("vector", lambda e: e.tensor_scalar_mul(out=stats[0:1, 0:512], in0=C.bank(0)[0:1, :], scalar1=inv_d),
             reads=[C.bankg(0)], writes=[gst])
        r.op("vector", lambda e: e.tensor_scalar_mul(out=stats[0:1, 512:1024], in0=C.bank(1)[0:1, :], scalar1=inv_d),
             reads=[C.bankg(1)], writes=[gst])
        r.op("vector", lambda e: e.tensor_mul(out=sq[0:1, 0:512], in0=stats[0:1, 0:512], in1=stats[0:1, 0:512]),
             reads=[gst], writes=[gsq])
        r.op("vector", lambda e: e.tensor_sub(out=stats[0:1, 512:1024], in0=stats[0:1, 512:1024], in1=sq[0:1, 0:512]),
             reads=[gsq], writes=[gst])
        r.op("vector", lambda e: e.tensor_scalar_add(out=stats[0:1, 512:1024], in0=stats[0:1, 512:1024], scalar1=float(eps)),
             reads=[gst], writes=[gst])
        r.op("scalar", lambda e: e.activation(out=stats[0:1, 512:1024], in_=stats[0:1, 512:1024], func=AF.Ln),
             reads=[gst], writes=[gst])
        r.op("scalar", lambda e: e.activation(out=stats[0:1, 512:1024], in_=stats[0:1, 512:1024], func=AF.Exp, scale=-0.5),
             reads=[gst], writes=[gst])
        r.op("tensor", lambda e: e.matmul(C.bank(2), ones_row_f[:], stats[0:1, 0:512], start=True, stop=True),
             reads=[gst, C.g("k_ones_row_f")], writes=[C.bankg(2)])
        r.op("tensor", lambda e: e.matmul(C.bank(3), ones_row_f[:], stats[0:1, 512:1024], start=True, stop=True),
             reads=[gst, C.g("k_ones_row_f")], writes=[C.bankg(3)])
        r.op("vector", lambda e: e.tensor_copy(out=ab[:, 0:512], in_=C.bank(2)),
             reads=[C.bankg(2)], writes=[gab])
        r.op("vector", lambda e: e.tensor_copy(out=ab[:, 512:1024], in_=C.bank(3)),
             reads=[C.bankg(3)], writes=[gab])
        for ic in range(DC):
            sl = slice(ic * 512, (ic + 1) * 512)
            r.op("vector", lambda e, sl=sl: e.tensor_sub(out=sq[:, sl], in0=xb[:, sl], in1=ab[:, 0:512]),
                 reads=[gxb, gab], writes=[gsq])
            if gamma_beta is None:
                r.op("vector", lambda e, sl=sl: e.tensor_mul(out=xnb[:, sl], in0=sq[:, sl], in1=ab[:, 512:1024]),
                     reads=[gsq, gab], writes=[gxn])
            else:
                r.op("vector", lambda e, sl=sl: e.tensor_mul(out=sq[:, sl], in0=sq[:, sl], in1=ab[:, 512:1024]),
                     reads=[gsq, gab], writes=[gsq])
                r.op("vector", lambda e, sl=sl, ic=ic: e.tensor_scalar(
                    out=xnb[:, sl], in0=sq[:, sl], scalar1=gb[:, ic:ic + 1],
                    scalar2=gb[:, 8 + ic:9 + ic], op0=ALU.mult, op1=ALU.add),
                    reads=[gsq, ggb], writes=[gxn])
        for ic in range(DC):
            sl = slice(ic * 512, (ic + 1) * 512)
            r.dma("sync", xn_hbm[ic * 128:(ic + 1) * 128, tg * 512:(tg + 1) * 512], xnb[:, sl],
                  reads=[gxn], writes=[xng])
    return


def load_slab(C, name, hbm_ap, hg, nchunk, T, dtype=BF16, queue="sync"):
    t = C.sb(name, [128, nchunk * T], dtype)
    g = C.g(name)
    for ic in range(nchunk):
        C.r.dma(queue, t[:, ic * T:(ic + 1) * T], hbm_ap[ic * 128:(ic + 1) * 128, :],
                reads=[hg], writes=[g])
    return t, g


def load_w(C, name, w_hbm, wg, in_chunks, out_cols):
    t = C.sb(name, [128, in_chunks * out_cols], BF16)
    g = C.g(name)
    for ic in range(in_chunks):
        C.r.dma("gpsimd", t[:, ic * out_cols:(ic + 1) * out_cols],
                w_hbm[ic * 128:(ic + 1) * 128, :], reads=[wg], writes=[g])
    return t, g


def wsl(C, wname, in_chunks, out_cols, ic, oc):
    return C._sb[wname][:, ic * out_cols + oc * 128: ic * out_cols + (oc + 1) * 128]


def emit_proj(C, slab_name, w_name, in_chunks, out_cols, epi, banks=(0, 1), tgs=None):
    r = C.r
    slab = C._sb[slab_name]
    sg = C.g(slab_name)
    wg = C.g(w_name)
    T = C.T
    if tgs is None:
        tgs = range(C.NTG)
    nb = len(banks)
    i = 0
    for oc in range(out_cols // 128):
        for tg in tgs:
            b = banks[i % nb]
            i += 1
            for ic in range(in_chunks):
                lh = wsl(C, w_name, in_chunks, out_cols, ic, oc)
                rh = slab[:, ic * T + tg * 512: ic * T + tg * 512 + 512]
                r.op("tensor", lambda e, lh=lh, rh=rh, ic=ic, b=b: e.matmul(
                    C.bank(b), lh, rh,
                    start=(ic == 0), stop=(ic == in_chunks - 1)),
                    reads=[sg, wg], writes=[C.bankg(b)])
            epi(oc, tg, C.bank(b), C.bankg(b))


def emit_l0attn(C, xn_hbm, xng, s_hbm, sg_out, wq_ap, wqg, wk_ap, wkg, wv_ap, wvg,
                b0_ap, b0g):
    r, T, NTG, NQT = C.r, C.T, C.NTG, C.NQT
    nc = C.nc
    q_hbm, qg = C.dram("sc_q0T", [D, T], BF16)
    bqkv = C.sb("l0a_b", [128, 24], F32)
    r.dma("sync", bqkv[:], b0_ap, reads=[b0g], writes=[C.g("l0a_b")])
    bg = C.g("l0a_b")

    xnslab, xnsg = load_slab(C, "l0a_xn", xn_hbm, xng, DC, T)
    load_w(C, "l0a_w", wq_ap, wqg, DC, D)

    def epi_q(oc, tg, bank, bankg):
        tmp = C.sb("l0a_ev", [128, 512], BF16)
        r.op("vector", lambda e, oc=oc, bank=bank: e.tensor_scalar_add(
            out=tmp[:], in0=bank, scalar1=bqkv[:, oc:oc + 1]),
            reads=[bankg, bg], writes=[C.g("l0a_ev")])
        r.dma("sync", q_hbm[oc * 128:(oc + 1) * 128, tg * 512:(tg + 1) * 512], tmp[:],
              reads=[C.g("l0a_ev")], writes=[qg])
    emit_proj(C, "l0a_xn", "l0a_w", DC, D, epi_q, banks=(0, 1))

    kslab = C.sb("l0a_k", [128, DC * T], BF16)
    kg = C.g("l0a_k")
    load_w(C, "l0a_w", wk_ap, wkg, DC, D)

    def epi_k(oc, tg, bank, bankg):
        r.op("vector", lambda e, oc=oc, tg=tg, bank=bank: e.tensor_scalar_add(
            out=kslab[:, oc * T + tg * 512: oc * T + tg * 512 + 512], in0=bank,
            scalar1=bqkv[:, 8 + oc:9 + oc]),
            reads=[bankg, bg], writes=[kg])
    emit_proj(C, "l0a_xn", "l0a_w", DC, D, epi_k, banks=(2, 3))

    qt_b = C.sb("l0a_qt", [128, DC * 128], BF16)
    qtg = C.g("l0a_qt")
    srow = C.sb("l0a_srow", [128, T], F32)
    srg = C.g("l0a_srow")
    pexp = C.sb("l0a_p", [128, T], BF16)
    pg = C.g("l0a_p")
    cols = C.sb("l0a_cols", [128, 8], F32)
    colg = C.g("l0a_cols")
    ratio = C.sb("l0a_ratio", [128, NQT], F32)
    ratg = C.g("l0a_ratio")
    cmask = C.consts["cmask"]
    cmg = C.g("k_cmask")
    cd = C.consts["countdown"]
    cdg = C.g("k_countdown")
    scale = 1.0 / 32.0
    for qt in range(NQT):
        ngrp = qt // 4 + 1
        for ic in range(DC):
            r.dma("sync", qt_b[:, ic * 128:(ic + 1) * 128],
                  q_hbm[ic * 128:(ic + 1) * 128, qt * 128:(qt + 1) * 128],
                  reads=[qg], writes=[qtg])
        for kg_i in range(ngrp):
            b = kg_i % 2
            for ic in range(DC):
                r.op("tensor", lambda e, ic=ic, kg_i=kg_i, b=b: e.matmul(
                    C.bank(b), qt_b[:, ic * 128:(ic + 1) * 128],
                    kslab[:, ic * T + kg_i * 512: ic * T + kg_i * 512 + 512],
                    start=(ic == 0), stop=(ic == DC - 1)),
                    reads=[qtg, kg], writes=[C.bankg(b)])
            if kg_i == ngrp - 1:
                r.op("vector", lambda e, kg_i=kg_i, b=b, m=qt % 4: e.tensor_add(
                    out=srow[:, kg_i * 512:(kg_i + 1) * 512], in0=C.bank(b),
                    in1=cmask[:, m * 512:(m + 1) * 512]),
                    reads=[C.bankg(b), cmg], writes=[srg])
            else:
                r.op("vector", lambda e, kg_i=kg_i, b=b: e.tensor_copy(
                    out=srow[:, kg_i * 512:(kg_i + 1) * 512], in_=C.bank(b)),
                    reads=[C.bankg(b)], writes=[srg])
        kw = ngrp * 512
        r.op("vector", lambda e, kw=kw: e.reduce_max(out=cols[:, 0:1], in_=srow[:, 0:kw], axis=AX.X),
             reads=[srg], writes=[colg])
        r.op("scalar", lambda e: e.activation(out=cols[:, 1:2], in_=cols[:, 0:1], func=AF.Relu, scale=scale),
             reads=[colg], writes=[colg])
        r.op("scalar", lambda e: e.mul(out=cols[:, 2:3], in_=cols[:, 1:2], mul=-1.0),
             reads=[colg], writes=[colg])
        r.op("scalar", lambda e, kw=kw: e.activation(
            out=pexp[:, 0:kw], in_=srow[:, 0:kw], func=AF.Exp, bias=cols[:, 2:3],
            scale=scale, accum_out=cols[:, 3:4]),
            reads=[srg, colg], writes=[pg, colg])
        r.op("scalar", lambda e: e.activation(out=cols[:, 4:5], in_=cols[:, 2:3], func=AF.Exp),
             reads=[colg], writes=[colg])
        r.op("scalar", lambda e, qt=qt: e.activation(out=cols[:, 5:6], in_=cd[:, qt:qt + 1],
                                                     func=AF.Identity, scale=cols[:, 4:5]),
             reads=[colg, cdg], writes=[colg])
        r.op("scalar", lambda e: e.activation(out=cols[:, 6:7], in_=cols[:, 3:4],
                                              func=AF.Identity, bias=cols[:, 5:6]),
             reads=[colg], writes=[colg])
        r.op("vector", lambda e: e.reciprocal(out=cols[:, 7:8], in_=cols[:, 6:7]),
             reads=[colg], writes=[colg])
        r.op("scalar", lambda e, qt=qt: e.activation(out=ratio[:, qt:qt + 1], in_=cols[:, 7:8],
                                                     func=AF.Identity, scale=cols[:, 6:7]),
             reads=[colg], writes=[ratg])

    rrows = C.sb("l0a_rr", [NQT, 128], F32)
    rrg = C.g("l0a_rr")
    rflat = C.sb("l0a_rf", [1, T], F32)
    rfg = C.g("l0a_rf")
    ident = C.consts["ident_f"]
    idg = C.g("k_ident_f")
    r.op("tensor", lambda e: e.transpose(C.bank(4)[0:NQT, 0:128], ratio[:], ident[:]),
         reads=[ratg, idg], writes=[C.bankg(4)])
    r.op("vector", lambda e: e.tensor_copy(out=rrows[:], in_=C.bank(4)[0:NQT, 0:128]),
         reads=[C.bankg(4)], writes=[rrg])
    rdram, rdg = C.dram("sc_l0a_r", [NQT, 128], F32)
    r.dma("sync", rdram, rrows[:], reads=[rrg], writes=[rdg])
    r.dma("sync", rflat[:], rdram.rearrange("q p -> (q p)")[None, :], reads=[rdg], writes=[rfg])

    load_w(C, "l0a_w", wv_ap, wvg, DC, D)
    rb = C.sb("l0a_rb", [128, 512], F32)
    rbg = C.g("l0a_rb")
    sev = C.sb("l0a_sev", [128, 512], F32)
    sevg = C.g("l0a_sev")
    ones_row_f = C.consts["ones_row_f"]
    org = C.g("k_ones_row_f")

    def epi_v(oc, tg, bank, bankg):
        if oc == 0:
            r.op("tensor", lambda e, tg=tg: e.matmul(
                C.bank(5), ones_row_f[:], rflat[0:1, tg * 512:(tg + 1) * 512],
                start=True, stop=True),
                reads=[rfg, org], writes=[C.bankg(5)])
            r.op("vector", lambda e: e.tensor_copy(out=rb[:], in_=C.bank(5)),
                 reads=[C.bankg(5)], writes=[rbg])
        r.op("vector", lambda e, oc=oc, bank=bank: e.scalar_tensor_tensor(
            out=sev[:], in0=bank, scalar=bqkv[:, 16 + oc:17 + oc], in1=rb[:],
            op0=ALU.add, op1=ALU.mult),
            reads=[bankg, bg, rbg], writes=[sevg])
        r.dma("sync", s_hbm[oc * 128:(oc + 1) * 128, tg * 512:(tg + 1) * 512], sev[:],
              reads=[sevg], writes=[sg_out])
    emit_proj(C, "l0a_xn", "l0a_w", DC, D, epi_v, banks=(6, 7))

def emit_l0ffn(C, xn_hbm, xng, x_hbm, xg, s_hbm, sg_in, x1_hbm, x1g,
               rw_ap, rwg, rb_ap, rbg_in, gw_ap, gwg, uw_ap, uwg, dw_ap, dwg,
               sga_ap, sgag, sua_ap, suag, sda_ap, sdag):
    r, T, NTG = C.r, C.T, C.NTG
    E = 32
    xnslab, xnsg = load_slab(C, "l0f_xn", xn_hbm, xng, DC, T)
    rwsl, rwsg = load_w(C, "l0f_rw", rw_ap, rwg, DC, 64)
    rbrow = C.sb("l0f_rb", [1, 64], F32)
    r.dma("sync", rbrow[:], rb_ap, reads=[rbg_in], writes=[C.g("l0f_rb")])
    rbb = C.sb("l0f_rbb", [128, 64], F32)
    r.op("tensor", lambda e: e.matmul(C.bank(0)[:, 0:64], C.consts["ones_row_f"][:], rbrow[:],
                                      start=True, stop=True),
         reads=[C.g("l0f_rb"), C.g("k_ones_row_f")], writes=[C.bankg(0)])
    r.op("vector", lambda e: e.tensor_copy(out=rbb[:], in_=C.bank(0)[:, 0:64]),
         reads=[C.bankg(0)], writes=[C.g("l0f_rbb")])

    z = C.sb("l0f_z", [128, 64], F32)
    zw = C.sb("l0f_zw", [128, 64], F32)
    t32 = C.sb("l0f_t32", [128, 64], F32)
    wtsf = C.sb("l0f_wts", [128, 64], F32)
    cols = C.sb("l0f_c", [128, 8], F32)
    wrows = C.sb("l0f_wr", [32, T], BF16)
    gz, gzw, gt32, gwt, gc, gwr = (C.g("l0f_z"), C.g("l0f_zw"), C.g("l0f_t32"),
                                   C.g("l0f_wts"), C.g("l0f_c"), C.g("l0f_wr"))
    ident = C.consts["ident_f"]
    idg = C.g("k_ident_f")
    for tt in range(T // 128):
        for ic in range(DC):
            r.op("tensor", lambda e, ic=ic, tt=tt: e.matmul(
                C.bank(1)[:, 0:64], xnslab[:, ic * T + tt * 128: ic * T + tt * 128 + 128],
                rwsl[:, ic * 64:(ic + 1) * 64], start=(ic == 0), stop=(ic == DC - 1)),
                reads=[xnsg, rwsg], writes=[C.bankg(1)])
        r.op("vector", lambda e: e.tensor_add(out=z[:], in0=C.bank(1)[:, 0:64], in1=rbb[:]),
             reads=[C.bankg(1), C.g("l0f_rbb")], writes=[gz])
        r.op("vector", lambda e: e.reduce_max(out=cols[:, 0:1], in_=z[:, 0:E], axis=AX.X),
             reads=[gz], writes=[gc])
        r.op("scalar", lambda e: e.mul(out=cols[:, 1:2], in_=cols[:, 0:1], mul=-1.0),
             reads=[gc], writes=[gc])
        r.op("scalar", lambda e: e.activation(out=z[:, 0:E], in_=z[:, 0:E], func=AF.Exp, bias=cols[:, 1:2]),
             reads=[gz, gc], writes=[gz])
        r.op("vector", lambda e: e.tensor_copy(out=zw[:], in_=z[:]), reads=[gz], writes=[gzw])
        for it in range(4):
            r.op("vector", lambda e: e.reduce_max(out=cols[:, 2:3], in_=zw[:, 0:E], axis=AX.X),
                 reads=[gzw], writes=[gc])
            r.op("scalar", lambda e: e.copy(out=cols[:, 3:4], in_=cols[:, 2:3]),
                 reads=[gc], writes=[gc])
            if it < 3:
                r.op("vector", lambda e: e.tensor_single_scalar(
                    out=t32[:], in_=zw[:], scalar=cols[:, 3:4], op=ALU.is_ge),
                    reads=[gzw, gc], writes=[gt32])
                r.op("vector", lambda e: e.scalar_tensor_tensor(
                    out=zw[:], in0=t32[:], scalar=NEG, in1=zw[:], op0=ALU.mult, op1=ALU.add),
                    reads=[gt32, gzw], writes=[gzw])
        r.op("vector", lambda e: e.scalar_tensor_tensor(
            out=t32[:], in0=z[:], scalar=cols[:, 3:4], in1=z[:], op0=ALU.is_ge,
            op1=ALU.mult, accum_out=cols[:, 4:5]),
            reads=[gz, gc], writes=[gt32, gc])
        r.op("scalar", lambda e: e.copy(out=cols[:, 5:6], in_=cols[:, 4:5]),
             reads=[gc], writes=[gc])
        r.op("vector", lambda e: e.reciprocal(out=cols[:, 6:7], in_=cols[:, 5:6]),
             reads=[gc], writes=[gc])
        r.op("scalar", lambda e: e.copy(out=cols[:, 7:8], in_=cols[:, 6:7]),
             reads=[gc], writes=[gc])
        r.op("vector", lambda e: e.tensor_scalar_mul(out=wtsf[:], in0=t32[:], scalar1=cols[:, 7:8]),
             reads=[gt32, gc], writes=[gwt])
        r.op("tensor", lambda e: e.transpose(C.bank(2)[0:32, 0:128], wtsf[:, 0:32], ident[:]),
             reads=[gwt, idg], writes=[C.bankg(2)])
        r.op("vector", lambda e, tt=tt: e.tensor_copy(out=wrows[0:32, tt * 128:(tt + 1) * 128],
                                                      in_=C.bank(2)[0:32, 0:128]),
             reads=[C.bankg(2)], writes=[gwr])
    wr_dram, gwrd = C.dram("sc_l0f_wr", [32, T], BF16)
    r.dma("sync", wr_dram, wrows[:], reads=[gwr], writes=[gwrd])
    wflat = C.sb("l0f_wf", [1, E * 512], BF16)
    gwf = C.g("l0f_wf")

    gwe = C.sb("l0f_gwe", [128, DC * 128], BF16)
    uwe = C.sb("l0f_uwe", [128, DC * 128], BF16)
    dwe = C.sb("l0f_dwe", [128, E * 128], BF16)
    ggw, guw, gdw = C.g("l0f_gwe"), C.g("l0f_uwe"), C.g("l0f_dwe")
    sgw, _ = load_w(C, "l0f_sgw", sga_ap, sgag, DC, 512)
    suw, _ = load_w(C, "l0f_suw", sua_ap, suag, DC, 512)
    sdw, _ = load_w(C, "l0f_sdw", sda_ap, sdag, 4, D)
    gsgw, gsuw, gsdw = C.g("l0f_sgw"), C.g("l0f_suw"), C.g("l0f_sdw")
    sgb = C.sb("l0f_sgb", [128, 512], BF16)
    sub = C.sb("l0f_sub", [128, 512], BF16)
    zsh = C.sb("l0f_zsh", [128, 4 * 512], BF16)
    gsgb, gsub, gzsh = C.g("l0f_sgb"), C.g("l0f_sub"), C.g("l0f_zsh")
    hslab = C.sb("l0f_h", [128, E * 512], BF16)
    gh = C.g("l0f_h")
    wb = C.sb("l0f_wb", [128, 512], BF16)
    gwb = C.g("l0f_wb")
    sg_t = C.sb("l0f_sg", [128, 512], BF16)
    su_t = C.sb("l0f_su", [128, 512], BF16)
    gsg, gsu = C.g("l0f_sg"), C.g("l0f_su")
    xch = C.sb("l0f_xch", [128, 512], F32)
    sch = C.sb("l0f_sch", [128, 512], F32)
    gxc, gsc = C.g("l0f_xch"), C.g("l0f_sch")
    out_ev = C.sb("l0f_oev", [128, 512], F32)
    goev = C.g("l0f_oev")
    onesb = C.consts["ones_row_b"]
    gob = C.g("k_ones_row_b")

    for tg in range(NTG):
        tsl = slice(tg * 512, (tg + 1) * 512)
        for e_i in range(E):
            r.dma("sync", wflat[0:1, e_i * 512:(e_i + 1) * 512], wr_dram[e_i:e_i + 1, tsl],
                  reads=[gwrd], writes=[gwf])
        for hc in range(4):
            for ic in range(DC):
                r.op("tensor", lambda e, ic=ic, hc=hc, tg=tg: e.matmul(
                    C.bank(0), sgw[:, ic * 512 + hc * 128: ic * 512 + (hc + 1) * 128],
                    xnslab[:, ic * T + tg * 512: ic * T + tg * 512 + 512],
                    start=(ic == 0), stop=(ic == DC - 1)),
                    reads=[xnsg, gsgw], writes=[C.bankg(0)])
            for ic in range(DC):
                r.op("tensor", lambda e, ic=ic, hc=hc, tg=tg: e.matmul(
                    C.bank(1), suw[:, ic * 512 + hc * 128: ic * 512 + (hc + 1) * 128],
                    xnslab[:, ic * T + tg * 512: ic * T + tg * 512 + 512],
                    start=(ic == 0), stop=(ic == DC - 1)),
                    reads=[xnsg, gsuw], writes=[C.bankg(1)])
            r.op("scalar", lambda e: e.activation(out=sgb[:], in_=C.bank(0), func=AF.Silu),
                 reads=[C.bankg(0)], writes=[gsgb])
            r.op("vector", lambda e: e.tensor_copy(out=sub[:], in_=C.bank(1)),
                 reads=[C.bankg(1)], writes=[gsub])
            r.op("vector", lambda e, hc=hc: e.tensor_mul(
                out=zsh[:, hc * 512:(hc + 1) * 512], in0=sgb[:], in1=sub[:]),
                reads=[gsgb, gsub], writes=[gzsh])
        for e_i in range(E):
            for ic in range(DC):
                r.dma("gpsimd", gwe[:, ic * 128:(ic + 1) * 128],
                      gw_ap[e_i, ic * 128:(ic + 1) * 128, :], reads=[gwg], writes=[ggw])
                r.dma("gpsimd", uwe[:, ic * 128:(ic + 1) * 128],
                      uw_ap[e_i, ic * 128:(ic + 1) * 128, :], reads=[uwg], writes=[guw])
            for ic in range(DC):
                r.op("tensor", lambda e, ic=ic, tg=tg: e.matmul(
                    C.bank(2), gwe[:, ic * 128:(ic + 1) * 128],
                    xnslab[:, ic * T + tg * 512: ic * T + tg * 512 + 512],
                    start=(ic == 0), stop=(ic == DC - 1)),
                    reads=[xnsg, ggw], writes=[C.bankg(2)])
            for ic in range(DC):
                r.op("tensor", lambda e, ic=ic, tg=tg: e.matmul(
                    C.bank(3), uwe[:, ic * 128:(ic + 1) * 128],
                    xnslab[:, ic * T + tg * 512: ic * T + tg * 512 + 512],
                    start=(ic == 0), stop=(ic == DC - 1)),
                    reads=[xnsg, guw], writes=[C.bankg(3)])
            r.op("tensor", lambda e, e_i=e_i: e.matmul(
                C.bank(4), onesb[:], wflat[0:1, e_i * 512:(e_i + 1) * 512],
                start=True, stop=True),
                reads=[gwf, gob], writes=[C.bankg(4)])
            r.op("scalar", lambda e: e.activation(out=sg_t[:], in_=C.bank(2), func=AF.Silu),
                 reads=[C.bankg(2)], writes=[gsg])
            r.op("vector", lambda e: e.tensor_copy(out=su_t[:], in_=C.bank(3)),
                 reads=[C.bankg(3)], writes=[gsu])
            r.op("vector", lambda e: e.tensor_copy(out=wb[:], in_=C.bank(4)),
                 reads=[C.bankg(4)], writes=[gwb])
            r.op("vector", lambda e: e.tensor_mul(out=sg_t[:], in0=sg_t[:], in1=su_t[:]),
                 reads=[gsu], writes=[gsg])
            r.op("vector", lambda e, e_i=e_i: e.tensor_mul(
                out=hslab[:, e_i * 512:(e_i + 1) * 512], in0=sg_t[:], in1=wb[:]),
                reads=[gsg, gwb], writes=[gh])
        for oc in range(DC):
            for e_i in range(E):
                r.dma("gpsimd", dwe[:, e_i * 128:(e_i + 1) * 128],
                      dw_ap[e_i, :, oc * 128:(oc + 1) * 128], reads=[dwg], writes=[gdw])
            for e_i in range(E):
                r.op("tensor", lambda e, e_i=e_i: e.matmul(
                    C.bank(5), dwe[:, e_i * 128:(e_i + 1) * 128],
                    hslab[:, e_i * 512:(e_i + 1) * 512],
                    start=(e_i == 0), stop=False),
                    reads=[gh, gdw], writes=[C.bankg(5)])
            for hc in range(4):
                r.op("tensor", lambda e, hc=hc, oc=oc: e.matmul(
                    C.bank(5), sdw[:, hc * D + oc * 128: hc * D + (oc + 1) * 128],
                    zsh[:, hc * 512:(hc + 1) * 512],
                    start=False, stop=(hc == 3)),
                    reads=[gzsh, gsdw], writes=[C.bankg(5)])
            r.dma("sync", xch[:], x_hbm[oc * 128:(oc + 1) * 128, tsl], reads=[xg], writes=[gxc])
            r.dma("sync", sch[:], s_hbm[oc * 128:(oc + 1) * 128, tsl], reads=[sg_in], writes=[gsc])
            r.op("scalar", lambda e: e.activation(out=out_ev[:], in_=C.bank(5), func=AF.Silu),
                 reads=[C.bankg(5)], writes=[goev])
            r.op("vector", lambda e: e.tensor_mul(out=out_ev[:], in0=out_ev[:], in1=sch[:]),
                 reads=[gsc], writes=[goev])
            r.op("vector", lambda e: e.tensor_add(out=out_ev[:], in0=out_ev[:], in1=xch[:]),
                 reads=[gxc], writes=[goev])
            r.dma("sync", x1_hbm[oc * 128:(oc + 1) * 128, tsl], out_ev[:],
                  reads=[goev], writes=[x1g])


def emit_l1qkv(C, xn_hbm, xng, w_ap, wg_in, b_ap, bg_in, qkv_hbm, qkvg):
    r, T = C.r, C.T
    load_slab(C, "l1q_xn", xn_hbm, xng, DC, T)
    load_w(C, "l1q_w", w_ap, wg_in, DC, 3072)
    bb = C.sb("l1q_b", [128, 24], F32)
    r.dma("sync", bb[:], b_ap, reads=[bg_in], writes=[C.g("l1q_b")])
    bg = C.g("l1q_b")
    ev = C.sb("l1q_ev", [128, 512], BF16)
    gev = C.g("l1q_ev")

    def epi(oc, tg, bank, bankg):
        r.op("vector", lambda e, oc=oc, bank=bank: e.tensor_scalar_add(
            out=ev[:], in0=bank, scalar1=bb[:, oc:oc + 1]),
            reads=[bankg, bg], writes=[gev])
        r.dma("sync", qkv_hbm[oc * 128:(oc + 1) * 128, tg * 512:(tg + 1) * 512], ev[:],
              reads=[gev], writes=[qkvg])
    emit_proj(C, "l1q_xn", "l1q_w", DC, 3072, epi, banks=(0, 1, 2, 3))


def emit_l1mix(C, qkv_hbm, qkvg, all_hbm, allg, s1_hbm, s1g):
    r, T = C.r, C.T
    q2 = C.sb("l1m_q", [128, T], BF16)
    k2 = C.sb("l1m_k", [128, 512], BF16)
    v2 = C.sb("l1m_v", [128, 512], BF16)
    Et = C.sb("l1m_e", [128, 512], BF16)
    scb = C.sb("l1m_sc", [2, 512], F32)
    pb = C.sb("l1m_pb", [128, 512], F32)
    cz = C.sb("l1m_cz", [128, 512], F32)
    pv = C.sb("l1m_pv", [128, 512], F32)
    gq, gk, gv, gE, gsc, gpb, gcz, gpv = (C.g("l1m_q"), C.g("l1m_k"), C.g("l1m_v"),
                                          C.g("l1m_e"), C.g("l1m_sc"), C.g("l1m_pb"),
                                          C.g("l1m_cz"), C.g("l1m_pv"))
    seg = C.consts["seg_lhsT"]
    gseg = C.g("k_seg_lhsT")
    indf = C.consts["ind_f"]
    gind = C.g("k_ind_f")
    cnorm = (1.0 / 8.0) * (8.0 ** -0.5)
    for hp in range(8):
        base = 384 * hp
        r.dma("sync", q2[0:64, :], qkv_hbm[base:base + 64, :], reads=[qkvg], writes=[gq])
        r.dma("sync", q2[64:128, :], qkv_hbm[base + 192:base + 256, :], reads=[qkvg], writes=[gq])
        for tg in range(C.NTG):
            tsl = slice(tg * 512, (tg + 1) * 512)
            for t in range(8):
                r.dma("sync", k2[0:64, :], all_hbm[t, base + 64:base + 128, tsl],
                      reads=[allg], writes=[gk])
                r.dma("sync", k2[64:128, :], all_hbm[t, base + 256:base + 320, tsl],
                      reads=[allg], writes=[gk])
                r.dma("sync", v2[0:64, :], all_hbm[t, base + 128:base + 192, tsl],
                      reads=[allg], writes=[gv])
                r.dma("sync", v2[64:128, :], all_hbm[t, base + 320:base + 384, tsl],
                      reads=[allg], writes=[gv])
                r.op("vector", lambda e, tsl=tsl: e.tensor_mul(out=Et[:], in0=q2[:, tsl], in1=k2[:]),
                     reads=[gq, gk], writes=[gE])
                r.op("tensor", lambda e: e.matmul(
                    C.bank(0)[0:2, :], seg[:], Et[:], start=True, stop=True),
                    reads=[gE, gseg], writes=[C.bankg(0)])
                if t == 0:
                    r.op("vector", lambda e: e.tensor_scalar_mul(out=scb[:], in0=C.bank(0)[0:2, :], scalar1=cnorm),
                         reads=[C.bankg(0)], writes=[gsc])
                else:
                    r.op("vector", lambda e: e.scalar_tensor_tensor(
                        out=scb[:], in0=C.bank(0)[0:2, :], scalar=cnorm, in1=scb[:],
                        op0=ALU.mult, op1=ALU.add),
                        reads=[C.bankg(0)], writes=[gsc])
                r.op("tensor", lambda e: e.matmul(C.bank(1), indf[:], scb[:], start=True, stop=True),
                     reads=[gsc, gind], writes=[C.bankg(1)])
                r.op("vector", lambda e: e.tensor_copy(out=pb[:], in_=C.bank(1)),
                     reads=[C.bankg(1)], writes=[gpb])
                if t == 0:
                    r.op("vector", lambda e: e.tensor_mul(out=cz[:], in0=pb[:], in1=v2[:]),
                         reads=[gpb, gv], writes=[gcz])
                else:
                    r.op("vector", lambda e: e.tensor_mul(out=pv[:], in0=pb[:], in1=v2[:]),
                         reads=[gpb, gv], writes=[gpv])
                    r.op("vector", lambda e: e.tensor_add(out=cz[:], in0=cz[:], in1=pv[:]),
                         reads=[gpv], writes=[gcz])
            r.dma("sync", s1_hbm[hp * 128:(hp + 1) * 128, tsl], cz[:],
                  reads=[gcz], writes=[s1g])


NODE_SPEC = {
    0: {"edges": [(0, -1)], "q": 0, "k": 0, "v": 0},
    1: {"edges": [(2, 0), (1, -1)], "q": 2, "k": 1, "v": 2},
    2: {"edges": [(5, 1), (3, -1), (4, 0)], "q": 5, "k": 3, "v": 4},
    3: {"edges": [(9, 2), (6, -1), (8, 1)], "q": 9, "k": 6, "v": 8},
    4: {"edges": [(14, 3), (10, -1), (12, 1)], "q": 14, "k": 10, "v": 12},
    5: {"edges": [(19, 4), (15, 0), (17, 2)], "q": 19, "k": 15, "v": 17},
}
LAM = 4.5


def emit_l1f_prep(C, ew_ap, ewg, actw_ap, actwg):
    r = C.r
    row = C.sb("l1f_row", [1, 64], F32)
    gr = C.g("l1f_row")
    r.op("vector", lambda e: e.memset(row[:], 0.0), writes=[gr])
    r.dma("sync", row[0:1, 0:18], ew_ap, reads=[ewg], writes=[gr])
    r.dma("sync", row[0:1, 18:24], actw_ap, reads=[actwg], writes=[gr])
    cols = C.sb("l1f_sc", [128, 64], F32)
    gc = C.g("l1f_sc")
    r.op("tensor", lambda e: e.matmul(C.bank(7)[:, 0:64], C.consts["ones_row_f"][:], row[:],
                                      start=True, stop=True),
         reads=[gr, C.g("k_ones_row_f")], writes=[C.bankg(7)])
    r.op("vector", lambda e: e.tensor_copy(out=cols[:], in_=C.bank(7)[:, 0:64]),
         reads=[C.bankg(7)], writes=[gc])
    return cols, gc


def _edge_proj(C, in_hbm, ing, we_ap, weg, e_i, be_ap, beg, out_hbm, outg):
    r, T = C.r, C.T
    load_slab(C, "l1n_act", in_hbm, ing, DC, T)
    load_w(C, "l1n_w", we_ap[e_i], weg, DC, D)
    bb = C.sb("l1n_be", [128, 8], F32)
    r.dma("sync", bb[:], be_ap[e_i], reads=[beg], writes=[C.g("l1n_be")])
    bg = C.g("l1n_be")
    ev = C.sb("l1n_ev", [128, 512], BF16)
    gev = C.g("l1n_ev")

    def epi(oc, tg, bank, bankg):
        r.op("vector", lambda e, oc=oc, bank=bank: e.tensor_scalar_add(
            out=ev[:], in0=bank, scalar1=bb[:, oc:oc + 1]),
            reads=[bankg, bg], writes=[gev])
        r.dma("sync", out_hbm[oc * 128:(oc + 1) * 128, tg * 512:(tg + 1) * 512], ev[:],
              reads=[gev], writes=[outg])
    emit_proj(C, "l1n_act", "l1n_w", DC, D, epi, banks=(0, 1))


def _head_proj(C, in_hbm, ing, w_ap, wg_, scol, scolg, out_hbm, outg):
    r, T = C.r, C.T
    load_slab(C, "l1n_act", in_hbm, ing, DC, T)
    load_w(C, "l1n_w", w_ap, wg_, DC, D)
    ev = C.sb("l1n_ev", [128, 512], BF16)
    gev = C.g("l1n_ev")

    def epi(oc, tg, bank, bankg):
        r.op("vector", lambda e, bank=bank: e.tensor_scalar_mul(
            out=ev[:], in0=bank, scalar1=scol),
            reads=[bankg, scolg], writes=[gev])
        r.dma("sync", out_hbm[oc * 128:(oc + 1) * 128, tg * 512:(tg + 1) * 512], ev[:],
              reads=[gev], writes=[outg])
    emit_proj(C, "l1n_act", "l1n_w", DC, D, epi, banks=(0, 1))


def _vh_proj(C, in_hbm, ing, w_ap, wg_, scol, scolg, out_hbm, outg):
    r, T = C.r, C.T
    slab = C._sb["l1n_act"]
    sg_ = C.g("l1n_act")
    load_w(C, "l1n_w", w_ap, wg_, DC, D)
    wg2 = C.g("l1n_w")
    ev = C.sb("l1n_ev", [128, 512], BF16)
    gev = C.g("l1n_ev")
    wt = C._sb["l1n_w"]
    for tt in range(T // 128):
        for og in range(2):
            b = og
            for ic in range(DC):
                lh = slab[:, ic * T + tt * 128: ic * T + tt * 128 + 128]
                rh = wt[:, ic * D + og * 512: ic * D + (og + 1) * 512]
                r.op("tensor", lambda e, lh=lh, rh=rh, ic=ic, b=b: e.matmul(
                    C.bank(b), lh, rh,
                    start=(ic == 0), stop=(ic == DC - 1)),
                    reads=[sg_, wg2], writes=[C.bankg(b)])
            r.op("vector", lambda e, b=b: e.tensor_scalar_mul(
                out=ev[:], in0=C.bank(b), scalar1=scol),
                reads=[C.bankg(b), scolg], writes=[gev])
            r.dma("sync", out_hbm[tt * 128:(tt + 1) * 128, og * 512:(og + 1) * 512], ev[:],
                  reads=[gev], writes=[outg])


def emit_l1ffn_node(C, c, node_hbm, we_ap, weg, be_ap, beg,
                    wq_ap, wqg, wk_ap, wkg, wv_ap, wvg, wo_ap, wog,
                    ewcols, ewg, out_hbm, outg):
    r, T = C.r, C.T
    spec = NODE_SPEC[c]
    C.push()
    sc_p = {}
    for e_i, n_i in spec["edges"]:
        ap, g_in = node_hbm[n_i]
        name = "sc_p%d" % e_i
        p_ap, p_g = C.dram(name, [D, T], BF16)
        _edge_proj(C, ap, g_in, we_ap, weg, e_i, be_ap, beg, p_ap, p_g)
        sc_p[e_i] = (p_ap, p_g)
    qh_ap, qh_g = C.dram("sc_qh", [D, T], BF16)
    kh_ap, kh_g = C.dram("sc_kh", [D, T], BF16)
    vh_ap, vh_g = C.dram("sc_vh", [T, D], BF16)
    pq_ap, pq_g = sc_p[spec["q"]]
    _head_proj(C, pq_ap, pq_g, wq_ap[c], wqg, ewcols[:, 3 * c:3 * c + 1], ewg, qh_ap, qh_g)
    pk_ap, pk_g = sc_p[spec["k"]]
    _head_proj(C, pk_ap, pk_g, wk_ap[c], wkg, ewcols[:, 3 * c + 1:3 * c + 2], ewg, kh_ap, kh_g)
    pv_ap, pv_g = sc_p[spec["v"]]
    load_slab(C, "l1n_act", pv_ap, pv_g, DC, T)
    _vh_proj(C, pv_ap, pv_g, wv_ap[c], wvg, ewcols[:, 3 * c + 2:3 * c + 3], ewg, vh_ap, vh_g)
    C.pop()

    NKC = T // 128
    ctxs = C.sb("l1a_ctx", [128, DC * T], BF16)
    C.push()
    kh_s = C.sb("l1a_k", [128, T], BF16)
    qh_s = C.sb("l1a_q", [128, 2 * 512], BF16)
    vh_s = C.sb("l1a_v", [128, NKC * 128], BF16)
    psl = C.sb("l1a_p", [128, NKC * 512], BF16)
    Rw = C.sb("l1a_R", [128, 512], F32)
    c0 = C.sb("l1n_oev", [128, 512], F32)
    sqs = C.sb("l1a_sq", [128, 512], F32)
    strow = C.sb("l1a_st", [1, 8], F32)
    bcol = C.sb("l1a_b", [128, 1], F32)
    rrow = C.sb("l1a_rr", [1, 512], F32)
    rb = C.sb("l1a_rb", [128, 512], F32)
    gk, gq, gv, gp, gR, gc0, gsq, gst, gb, grr, grb, gctx = (
        C.g("l1a_k"), C.g("l1a_q"), C.g("l1a_v"), C.g("l1a_p"), C.g("l1a_R"),
        C.g("l1n_oev"), C.g("l1a_sq"), C.g("l1a_st"), C.g("l1a_b"), C.g("l1a_rr"),
        C.g("l1a_rb"), C.g("l1a_ctx"))
    ones_f = C.consts["ones_col_f"]
    ones_row_f = C.consts["ones_row_f"]
    g1c, g1r = C.g("k_ones_col_f"), C.g("k_ones_row_f")
    iscale = 128.0 ** -0.5
    inv_n = 1.0 / (128 * 512)
    for h in range(DC):
        hsl = slice(h * 128, (h + 1) * 128)
        r.dma("sync", kh_s[:], kh_ap[hsl, :], reads=[kh_g], writes=[gk])
        r.dma("sync", qh_s[:, 0:512], qh_ap[hsl, 0:512], reads=[qh_g], writes=[gq])
        r.dma("sync", vh_s[:].rearrange("p (kc d) -> p kc d", kc=NKC),
              vh_ap[:, hsl].rearrange("(kc p) d -> p kc d", p=128),
              reads=[vh_g], writes=[gv])
        for x in range(1):
            r.op("tensor", lambda e: e.matmul(
                C.bank(5), kh_s[:, 0:128], qh_s[:, 0:512], start=True, stop=True),
                reads=[gk, gq], writes=[C.bankg(5)])
            r.op("vector", lambda e: e.tensor_copy(out=c0[:], in_=C.bank(5)),
                 reads=[C.bankg(5)], writes=[gc0])
            r.op("vector", lambda e: e.tensor_mul(out=sqs[:], in0=c0[:], in1=c0[:]),
                 reads=[gc0], writes=[gsq])
            r.op("tensor", lambda e: e.matmul(C.bank(6)[0:1, :], ones_f[:], c0[:],
                                              start=True, stop=True),
                 reads=[gc0, g1c], writes=[C.bankg(6)])
            r.op("tensor", lambda e: e.matmul(C.bank(7)[0:1, :], ones_f[:], sqs[:],
                                              start=True, stop=True),
                 reads=[gsq, g1c], writes=[C.bankg(7)])
            r.op("vector", lambda e: e.reduce_sum(out=strow[0:1, 0:1], in_=C.bank(6)[0:1, :], axis=AX.X),
                 reads=[C.bankg(6)], writes=[gst])
            r.op("vector", lambda e: e.reduce_sum(out=strow[0:1, 1:2], in_=C.bank(7)[0:1, :], axis=AX.X),
                 reads=[C.bankg(7)], writes=[gst])
            kv = C.consts["kvals"]
            kvg = C.g("k_kvals")
            r.op("scalar", lambda e: e.mul(out=strow[0:1, 2:3], in_=strow[0:1, 0:1], mul=inv_n),
                 reads=[gst], writes=[gst])
            r.op("scalar", lambda e: e.mul(out=strow[0:1, 3:4], in_=strow[0:1, 1:2], mul=inv_n),
                 reads=[gst], writes=[gst])
            r.op("scalar", lambda e: e.activation(out=strow[0:1, 4:5], in_=strow[0:1, 2:3], func=AF.Square),
                 reads=[gst], writes=[gst])
            r.op("scalar", lambda e: e.mul(out=strow[0:1, 4:5], in_=strow[0:1, 4:5], mul=-1.0),
                 reads=[gst], writes=[gst])
            r.op("scalar", lambda e: e.activation(out=strow[0:1, 5:6], in_=strow[0:1, 3:4],
                                                  func=AF.Identity, bias=strow[0:1, 4:5]),
                 reads=[gst], writes=[gst])
            r.op("scalar", lambda e, kv=kv: e.activation(out=strow[0:1, 5:6], in_=strow[0:1, 5:6],
                                                         func=AF.Ln, bias=kv[0:1, 0:1]),
                 reads=[gst, kvg], writes=[gst])
            r.op("scalar", lambda e: e.activation(out=strow[0:1, 5:6], in_=strow[0:1, 5:6],
                                                  func=AF.Exp, scale=0.5),
                 reads=[gst], writes=[gst])
            r.op("scalar", lambda e: e.mul(out=strow[0:1, 6:7], in_=strow[0:1, 5:6], mul=LAM),
                 reads=[gst], writes=[gst])
            r.op("scalar", lambda e: e.activation(out=strow[0:1, 3:4], in_=strow[0:1, 2:3],
                                                  func=AF.Identity, bias=strow[0:1, 6:7]),
                 reads=[gst], writes=[gst])
            r.op("scalar", lambda e: e.mul(out=strow[0:1, 7:8], in_=strow[0:1, 3:4], mul=-iscale),
                 reads=[gst], writes=[gst])
            r.op("tensor", lambda e: e.matmul(C.bank(5)[:, 0:1], ones_row_f[:], strow[0:1, 7:8],
                                              start=True, stop=True),
                 reads=[gst, g1r], writes=[C.bankg(5)])
            r.op("vector", lambda e: e.tensor_copy(out=bcol[:], in_=C.bank(5)[:, 0:1]),
                 reads=[C.bankg(5)], writes=[gb])
        for qg in range(C.NTG):
            qpar = qg % 2
            qbuf = qh_s[:, qpar * 512:(qpar + 1) * 512]
            if qg > 0:
                r.dma("sync", qbuf, qh_ap[hsl, qg * 512:(qg + 1) * 512],
                      reads=[qh_g], writes=[gq])
            for kc in range(NKC):
                b = kc % 2
                r.op("tensor", lambda e, kc=kc, b=b, qbuf=qbuf: e.matmul(
                    C.bank(b), kh_s[:, kc * 128:(kc + 1) * 128], qbuf,
                    start=True, stop=True),
                    reads=[gk, gq], writes=[C.bankg(b)])
                r.op("scalar", lambda e, kc=kc, b=b: e.activation(
                    out=psl[:, kc * 512:(kc + 1) * 512], in_=C.bank(b), func=AF.Exp,
                    bias=bcol[:, 0:1], scale=iscale),
                    reads=[C.bankg(b), gb], writes=[gp])
                if kc == 0:
                    r.op("vector", lambda e, kc=kc: e.tensor_scalar_mul(
                        out=Rw[:], in0=psl[:, kc * 512:(kc + 1) * 512], scalar1=1.0),
                        reads=[gp], writes=[gR])
                else:
                    r.op("vector", lambda e, kc=kc: e.scalar_tensor_tensor(
                        out=Rw[:], in0=psl[:, kc * 512:(kc + 1) * 512], scalar=1.0,
                        in1=Rw[:], op0=ALU.mult, op1=ALU.add),
                        reads=[gp], writes=[gR])
            for kc in range(NKC):
                r.op("tensor", lambda e, kc=kc: e.matmul(
                    C.bank(2), vh_s[:, kc * 128:(kc + 1) * 128],
                    psl[:, kc * 512:(kc + 1) * 512],
                    start=(kc == 0), stop=(kc == NKC - 1)),
                    reads=[gv, gp], writes=[C.bankg(2)])
            r.op("tensor", lambda e: e.matmul(C.bank(3)[0:1, :], ones_f[:], Rw[:],
                                              start=True, stop=True),
                 reads=[gR, g1c], writes=[C.bankg(3)])
            r.op("vector", lambda e: e.reciprocal(out=rrow[:], in_=C.bank(3)[0:1, :]),
                 reads=[C.bankg(3)], writes=[grr])
            r.op("tensor", lambda e: e.matmul(C.bank(4), ones_row_f[:], rrow[:],
                                              start=True, stop=True),
                 reads=[grr, g1r], writes=[C.bankg(4)])
            r.op("vector", lambda e: e.tensor_copy(out=rb[:], in_=C.bank(4)),
                 reads=[C.bankg(4)], writes=[grb])
            r.op("vector", lambda e, h=h, qg=qg: e.tensor_mul(
                out=ctxs[:, h * T + qg * 512: h * T + qg * 512 + 512],
                in0=C.bank(2), in1=rb[:]),
                reads=[C.bankg(2), grb], writes=[gctx])

    C.pop()
    C.push()
    load_w(C, "l1n_w", wo_ap[c], wog, DC, D)
    pqt = C.sb("l1n_pqt", [128, 512], BF16)
    gpqt = C.g("l1n_pqt")
    oev = C.sb("l1n_oev", [128, 512], F32)
    goev = C.g("l1n_oev")
    nev = C.sb("l1n_ev", [128, 512], BF16)
    gnev = C.g("l1n_ev")

    def epi(oc, tg, bank, bankg):
        tsl = slice(tg * 512, (tg + 1) * 512)
        r.dma("sync", pqt[:], pq_ap[oc * 128:(oc + 1) * 128, tsl], reads=[pq_g], writes=[gpqt])
        r.op("vector", lambda e, bank=bank, c=c: e.tensor_scalar_mul(
            out=oev[:], in0=bank, scalar1=ewcols[:, 18 + c:19 + c]),
            reads=[bankg, ewg], writes=[goev])
        r.op("vector", lambda e, c=c: e.scalar_tensor_tensor(
            out=nev[:], in0=pqt[:], scalar=ewcols[:, 3 * c:3 * c + 1], in1=oev[:],
            op0=ALU.mult, op1=ALU.add),
            reads=[gpqt, ewg, goev], writes=[gnev])
        r.dma("sync", out_hbm[oc * 128:(oc + 1) * 128, tsl], nev[:],
              reads=[gnev], writes=[outg])
    emit_proj(C, "l1a_ctx", "l1n_w", DC, D, epi, banks=(0, 1))
    C.pop()


def emit_final(C, n5_hbm, n5g, lng_ap, lngg, lnb_ap, lnbg, s1_hbm, s1g,
               x1_hbm, x1g, out_hbm, outg):
    r, T = C.r, C.T
    ln5_ap, ln5g = C.dram("sc_ln5", [D, T], F32)
    emit_ln(C, n5_hbm, n5g, ln5_ap, ln5g, eps=1e-6, gamma_beta=(lng_ap, lnb_ap),
            tag="ln5", in_dtype=BF16, out_dtype=F32)
    lch = C.sb("fin_l", [128, 512], F32)
    sch = C.sb("fin_s", [128, 512], F32)
    xch = C.sb("fin_x", [128, 512], F32)
    glc, gsc, gxc = C.g("fin_l"), C.g("fin_s"), C.g("fin_x")
    for tg in range(C.NTG):
        tsl = slice(tg * 512, (tg + 1) * 512)
        for oc in range(DC):
            csl = slice(oc * 128, (oc + 1) * 128)
            r.dma("sync", lch[:], ln5_ap[csl, tsl], reads=[ln5g], writes=[glc])
            r.dma("sync", sch[:], s1_hbm[csl, tsl], reads=[s1g], writes=[gsc])
            r.dma("sync", xch[:], x1_hbm[csl, tsl], reads=[x1g], writes=[gxc])
            r.op("scalar", lambda e: e.activation(out=lch[:], in_=lch[:], func=AF.Silu),
                 reads=[glc], writes=[glc])
            r.op("vector", lambda e: e.tensor_mul(out=lch[:], in0=lch[:], in1=sch[:]),
                 reads=[gsc], writes=[glc])
            r.op("vector", lambda e: e.tensor_add(out=lch[:], in0=lch[:], in1=xch[:]),
                 reads=[gxc], writes=[glc])
            r.dma("sync", out_hbm[csl, tsl], lch[:], reads=[glc], writes=[outg])



def _new(T, consts=None):
    nc = bass.Bass("TRN2", target_bir_lowering=False, debug=False,
                   detect_race_conditions=False)
    stack = ExitStack()
    stack.__enter__()
    C = Ctx(nc, stack, T=T)
    declare_consts(C, only=consts)
    return nc, stack, C


def _ein(C, name, shape, dtype):
    ap = C.nc.dram_tensor(name, shape, dtype, kind="ExternalInput").ap()
    return ap, G(name)


def _eout(C, name, shape, dtype):
    ap = C.nc.dram_tensor(name, shape, dtype, kind="ExternalOutput").ap()
    return ap, G(name)


def build_ln0(T):
    nc, stack, C = _new(T)
    x, xg = _ein(C, "xT", [D, T], F32)
    xn, xng = _eout(C, "xn0T", [D, T], BF16)
    C.push(); emit_ln(C, x, xg, xn, xng, eps=1e-5, tag="ln0"); C.pop()
    C.finish(); stack.__exit__(None, None, None)
    return nc, ["xn0T"]


def build_l0attn(T):
    nc, stack, C = _new(T)
    xn, xng = _ein(C, "xn0T", [D, T], BF16)
    wq, wqg = _ein(C, "w_q", [D, D], F32)
    wk, wkg = _ein(C, "w_k", [D, D], F32)
    wv, wvg = _ein(C, "w_v", [D, D], F32)
    b0, b0g = _ein(C, "b_qkv0", [128, 24], F32)
    s, sg_ = _eout(C, "s0T", [D, T], F32)
    C.push(); emit_l0attn(C, xn, xng, s, sg_, wq, wqg, wk, wkg, wv, wvg, b0, b0g); C.pop()
    C.finish(); stack.__exit__(None, None, None)
    return nc, ["s0T"]


def build_l0ffn(T):
    nc, stack, C = _new(T)
    xn, xng = _ein(C, "xn0T", [D, T], BF16)
    x, xg = _ein(C, "xT", [D, T], F32)
    s, sg_ = _ein(C, "s0T", [D, T], F32)
    rw, rwg = _ein(C, "router_wT", [D, 64], F32)
    rb, rbg = _ein(C, "router_b", [1, 64], F32)
    gw, gwg = _ein(C, "gate_wT", [32, D, 128], F32)
    uw, uwg = _ein(C, "up_wT", [32, D, 128], F32)
    dw, dwg = _ein(C, "down_wT", [32, 128, D], F32)
    sga, sgag = _ein(C, "sgate_wT", [D, 512], F32)
    sua, suag = _ein(C, "sup_wT", [D, 512], F32)
    sda, sdag = _ein(C, "sdown_wT", [512, D], F32)
    x1, x1g = _eout(C, "x1T", [D, T], F32)
    C.push()
    emit_l0ffn(C, xn, xng, x, xg, s, sg_, x1, x1g, rw, rwg, rb, rbg,
               gw, gwg, uw, uwg, dw, dwg, sga, sgag, sua, suag, sda, sdag)
    C.pop(); C.finish(); stack.__exit__(None, None, None)
    return nc, ["x1T"]


def build_ln1(T):
    nc, stack, C = _new(T)
    x, xg = _ein(C, "x1T", [D, T], F32)
    xn, xng = _eout(C, "xn1T", [D, T], BF16)
    C.push(); emit_ln(C, x, xg, xn, xng, eps=1e-5, tag="ln1"); C.pop()
    C.finish(); stack.__exit__(None, None, None)
    return nc, ["xn1T"]


def build_l1qkv(T):
    nc, stack, C = _new(T)
    xn, xng = _ein(C, "xn1T", [D, T], BF16)
    w, wg_ = _ein(C, "wqkv1T", [D, 3072], F32)
    b, bg_ = _ein(C, "b_qkv1", [128, 24], F32)
    qkv, qkvg = _eout(C, "qkv1T", [3072, T], BF16)
    C.push(); emit_l1qkv(C, xn, xng, w, wg_, b, bg_, qkv, qkvg); C.pop()
    C.finish(); stack.__exit__(None, None, None)
    return nc, ["qkv1T"]


def build_l1mix(T):
    nc, stack, C = _new(T)
    qkv, qkvg = _ein(C, "qkv1T", [3072, T], BF16)
    alla, allg = _ein(C, "qkv_all", [8, 3072, T], BF16)
    s1, s1g = _eout(C, "s1T", [D, T], F32)
    C.push(); emit_l1mix(C, qkv, qkvg, alla, allg, s1, s1g); C.pop()
    C.finish(); stack.__exit__(None, None, None)
    return nc, ["s1T"]


def build_node(T, c):
    nc, stack, C = _new(T, consts={"ones_col_f", "ones_row_f", "kvals"})
    spec = NODE_SPEC[c]
    needed = sorted({n_i for (_, n_i) in spec["edges"]})
    node_hbm = {}
    for n_i in needed:
        nm = "xn1T" if n_i == -1 else "n%dT" % n_i
        ap, g_ = _ein(C, nm, [D, T], BF16)
        node_hbm[n_i] = (ap, g_)
    we, weg = _ein(C, "l1f_We", [20, D, D], F32)
    be, beg = _ein(C, "l1f_be", [20, 128, 8], F32)
    wq, wqg = _ein(C, "l1f_Wq", [6, D, D], F32)
    wk, wkg = _ein(C, "l1f_Wk", [6, D, D], F32)
    wv, wvg = _ein(C, "l1f_Wv", [6, D, D], F32)
    wo, wog = _ein(C, "l1f_Wo", [6, D, D], F32)
    ew, ewg_ = _ein(C, "l1f_ew", [1, 18], F32)
    aw, awg = _ein(C, "l1f_actw", [1, 6], F32)
    out, outg = _eout(C, "n%dT" % c, [D, T], BF16)
    C.push()
    ewcols, ewg = emit_l1f_prep(C, ew, ewg_, aw, awg)
    emit_l1ffn_node(C, c, node_hbm, we, weg, be, beg, wq, wqg, wk, wkg,
                    wv, wvg, wo, wog, ewcols, ewg, out, outg)
    C.pop(); C.finish(); stack.__exit__(None, None, None)
    return nc, ["n%dT" % c]


def build_final(T):
    nc, stack, C = _new(T)
    n5, n5g = _ein(C, "n5T", [D, T], BF16)
    lng, lngg = _ein(C, "l1f_lng", [128, 8], F32)
    lnb, lnbg = _ein(C, "l1f_lnb", [128, 8], F32)
    s1, s1g = _ein(C, "s1T", [D, T], F32)
    x1, x1g = _ein(C, "x1T", [D, T], F32)
    out, outg = _eout(C, "outT", [D, T], F32)
    C.push()
    emit_final(C, n5, n5g, lng, lngg, lnb, lnbg, s1, s1g, x1, x1g, out, outg)
    C.pop(); C.finish(); stack.__exit__(None, None, None)
    return nc, ["outT"]


NODE_INPUT_NAMES = {
    c: sorted({"xn1T" if n_i == -1 else "n%dT" % n_i for (_, n_i) in NODE_SPEC[c]["edges"]})
    for c in range(6)
}

PHASES = [
    ("ln0", build_ln0, ["xT"], ["xn0T"]),
    ("l0attn", build_l0attn, ["xn0T", "w_q", "w_k", "w_v", "b_qkv0"], ["s0T"]),
    ("l0ffn", build_l0ffn, ["xn0T", "xT", "s0T", "router_wT", "router_b", "gate_wT",
                            "up_wT", "down_wT", "sgate_wT", "sup_wT", "sdown_wT"], ["x1T"]),
    ("ln1", build_ln1, ["x1T"], ["xn1T"]),
    ("l1qkv", build_l1qkv, ["xn1T", "wqkv1T", "b_qkv1"], ["qkv1T"]),
    ("l1mix", build_l1mix, ["qkv1T", "qkv_all"], ["s1T"]),
] + [
    ("node%d" % c, (lambda T, c=c: build_node(T, c)),
     NODE_INPUT_NAMES[c] + ["l1f_We", "l1f_be", "l1f_Wq", "l1f_Wk", "l1f_Wv",
                            "l1f_Wo", "l1f_ew", "l1f_actw"], ["n%dT" % c])
    for c in range(6)
] + [
    ("final", build_final, ["n5T", "l1f_lng", "l1f_lnb", "s1T", "x1T"], ["outT"]),
]


def build_fused(T):
    nc, stack, C = _new(T)
    r = C.r
    x, xg = _ein(C, "xT", [D, T], F32)
    wq, wqg = _ein(C, "w_q", [D, D], F32)
    wk, wkg = _ein(C, "w_k", [D, D], F32)
    wv, wvg = _ein(C, "w_v", [D, D], F32)
    b0, b0g = _ein(C, "b_qkv0", [128, 24], F32)
    rw, rwg = _ein(C, "router_wT", [D, 64], F32)
    rb, rbg = _ein(C, "router_b", [1, 64], F32)
    gw, gwg = _ein(C, "gate_wT", [32, D, 128], F32)
    uw, uwg = _ein(C, "up_wT", [32, D, 128], F32)
    dw, dwg = _ein(C, "down_wT", [32, 128, D], F32)
    sga, sgag = _ein(C, "sgate_wT", [D, 512], F32)
    sua, suag = _ein(C, "sup_wT", [D, 512], F32)
    sda, sdag = _ein(C, "sdown_wT", [512, D], F32)
    wqkv1, wqkv1g = _ein(C, "wqkv1T", [D, 3072], F32)
    b1, b1g = _ein(C, "b_qkv1", [128, 24], F32)
    we, weg = _ein(C, "l1f_We", [20, D, D], F32)
    be, beg = _ein(C, "l1f_be", [20, 128, 8], F32)
    wq6, wq6g = _ein(C, "l1f_Wq", [6, D, D], F32)
    wk6, wk6g = _ein(C, "l1f_Wk", [6, D, D], F32)
    wv6, wv6g = _ein(C, "l1f_Wv", [6, D, D], F32)
    wo6, wo6g = _ein(C, "l1f_Wo", [6, D, D], F32)
    ew, ewg_ = _ein(C, "l1f_ew", [1, 18], F32)
    aw, awg = _ein(C, "l1f_actw", [1, 6], F32)
    lng, lngg = _ein(C, "l1f_lng", [128, 8], F32)
    lnb, lnbg = _ein(C, "l1f_lnb", [128, 8], F32)
    out, outg = _eout(C, "outT", [D, T], F32)
    xn0, xn0g = C.dram("i_xn0T", [D, T], BF16)
    s0, s0g = C.dram("i_s0T", [D, T], F32)
    x1, x1g = C.dram("i_x1T", [D, T], F32)
    xn1, xn1g = C.dram("i_xn1T", [D, T], BF16)
    qkv_own, qkv_own_g = C.dram("i_qkvT", [3072, T], BF16)
    qkv_all, qkv_all_g = C.dram("i_qkv_all", [8 * 3072, T], BF16)
    s1, s1g = C.dram("i_s1T", [D, T], F32)
    node_hbm = {-1: (xn1, xn1g)}
    for c in range(6):
        node_hbm[c] = C.dram("i_n%dT" % c, [D, T], BF16)

    C.push(); emit_ln(C, x, xg, xn0, xn0g, eps=1e-5, tag="ln0"); C.pop()
    C.push(); emit_l0attn(C, xn0, xn0g, s0, s0g, wq, wqg, wk, wkg, wv, wvg, b0, b0g); C.pop()
    C.push()
    emit_l0ffn(C, xn0, xn0g, x, xg, s0, s0g, x1, x1g, rw, rwg, rb, rbg,
               gw, gwg, uw, uwg, dw, dwg, sga, sgag, sua, suag, sda, sdag)
    C.pop()
    C.push(); emit_ln(C, x1, x1g, xn1, xn1g, eps=1e-5, tag="ln1"); C.pop()
    C.push(); emit_l1qkv(C, xn1, xn1g, wqkv1, wqkv1g, b1, b1g, qkv_own, qkv_own_g); C.pop()
    r.op("gpsimd", lambda e: e.collective_compute(
        "AllGather", ALU.bypass, [list(range(8))], [qkv_own], [qkv_all]),
        reads=[qkv_own_g], writes=[qkv_all_g], dma=True)
    all3d = qkv_all.rearrange("(t r) c -> t r c", t=8)
    C.push(); emit_l1mix(C, qkv_own, qkv_own_g, all3d, qkv_all_g, s1, s1g); C.pop()
    for c in range(6):
        C.push()
        ewcols, ewcg = emit_l1f_prep(C, ew, ewg_, aw, awg)
        emit_l1ffn_node(C, c, node_hbm, we, weg, be, beg, wq6, wq6g, wk6, wk6g,
                        wv6, wv6g, wo6, wo6g, ewcols, ewcg,
                        node_hbm[c][0], node_hbm[c][1])
        C.pop()
    C.push()
    emit_final(C, node_hbm[5][0], node_hbm[5][1], lng, lngg, lnb, lnbg,
               s1, s1g, x1, x1g, out, outg)
    C.pop()
    C.finish()
    stack.__exit__(None, None, None)
    return nc


def build_fused_a(T):
    nc, stack, C = _new(T)
    x, xg = _ein(C, "xT", [D, T], F32)
    wq, wqg = _ein(C, "w_q", [D, D], F32)
    wk, wkg = _ein(C, "w_k", [D, D], F32)
    wv, wvg = _ein(C, "w_v", [D, D], F32)
    b0, b0g = _ein(C, "b_qkv0", [128, 24], F32)
    rw, rwg = _ein(C, "router_wT", [D, 64], F32)
    rb, rbg = _ein(C, "router_b", [1, 64], F32)
    gw, gwg = _ein(C, "gate_wT", [32, D, 128], F32)
    uw, uwg = _ein(C, "up_wT", [32, D, 128], F32)
    dw, dwg = _ein(C, "down_wT", [32, 128, D], F32)
    sga, sgag = _ein(C, "sgate_wT", [D, 512], F32)
    sua, suag = _ein(C, "sup_wT", [D, 512], F32)
    sda, sdag = _ein(C, "sdown_wT", [512, D], F32)
    wqkv1, wqkv1g = _ein(C, "wqkv1T", [D, 3072], F32)
    b1, b1g = _ein(C, "b_qkv1", [128, 24], F32)
    x1, x1g = _eout(C, "x1T", [D, T], F32)
    xn1, xn1g = _eout(C, "xn1T", [D, T], BF16)
    qkv, qkvg = _eout(C, "qkv1T", [3072, T], BF16)
    xn0, xn0g = C.dram("i_xn0T", [D, T], BF16)
    s0, s0g = C.dram("i_s0T", [D, T], F32)
    C.push(); emit_ln(C, x, xg, xn0, xn0g, eps=1e-5, tag="ln0"); C.pop()
    C.push(); emit_l0attn(C, xn0, xn0g, s0, s0g, wq, wqg, wk, wkg, wv, wvg, b0, b0g); C.pop()
    C.push()
    emit_l0ffn(C, xn0, xn0g, x, xg, s0, s0g, x1, x1g, rw, rwg, rb, rbg,
               gw, gwg, uw, uwg, dw, dwg, sga, sgag, sua, suag, sda, sdag)
    C.pop()
    C.push(); emit_ln(C, x1, x1g, xn1, xn1g, eps=1e-5, tag="ln1"); C.pop()
    C.push(); emit_l1qkv(C, xn1, xn1g, wqkv1, wqkv1g, b1, b1g, qkv, qkvg); C.pop()
    C.finish()
    stack.__exit__(None, None, None)
    return nc


def build_fused_b(T):
    nc, stack, C = _new(T)
    qkv, qkvg = _ein(C, "qkv1T", [3072, T], BF16)
    alla, allg = _ein(C, "qkv_all", [8, 3072, T], BF16)
    xn1, xn1g = _ein(C, "xn1T", [D, T], BF16)
    x1, x1g = _ein(C, "x1T", [D, T], F32)
    we, weg = _ein(C, "l1f_We", [20, D, D], F32)
    be, beg = _ein(C, "l1f_be", [20, 128, 8], F32)
    wq6, wq6g = _ein(C, "l1f_Wq", [6, D, D], F32)
    wk6, wk6g = _ein(C, "l1f_Wk", [6, D, D], F32)
    wv6, wv6g = _ein(C, "l1f_Wv", [6, D, D], F32)
    wo6, wo6g = _ein(C, "l1f_Wo", [6, D, D], F32)
    ew, ewg_ = _ein(C, "l1f_ew", [1, 18], F32)
    aw, awg = _ein(C, "l1f_actw", [1, 6], F32)
    lng, lngg = _ein(C, "l1f_lng", [128, 8], F32)
    lnb, lnbg = _ein(C, "l1f_lnb", [128, 8], F32)
    out, outg = _eout(C, "outT", [D, T], F32)
    s1, s1g = C.dram("i_s1T", [D, T], F32)
    node_hbm = {-1: (xn1, xn1g)}
    for c in range(6):
        node_hbm[c] = C.dram("i_n%dT" % c, [D, T], BF16)
    C.push(); emit_l1mix(C, qkv, qkvg, alla, allg, s1, s1g); C.pop()
    for c in range(6):
        C.push()
        ewcols, ewcg = emit_l1f_prep(C, ew, ewg_, aw, awg)
        emit_l1ffn_node(C, c, node_hbm, we, weg, be, beg, wq6, wq6g, wk6, wk6g,
                        wv6, wv6g, wo6, wo6g, ewcols, ewcg,
                        node_hbm[c][0], node_hbm[c][1])
        C.pop()
    C.push()
    emit_final(C, node_hbm[5][0], node_hbm[5][1], lng, lngg, lnb, lnbg,
               s1, s1g, x1, x1g, out, outg)
    C.pop()
    C.finish()
    stack.__exit__(None, None, None)
    return nc


def kernel(**inputs):
    T = TFULL
    ncore = 8
    consts = host_consts(T)
    per_core = [host_inputs(i, inputs, T) for i in range(ncore)]
    nca = build_fused_a(T)
    in_maps = [dict(consts, **pc) for pc in per_core]
    res = run_bass_kernel_spmd(nca, in_maps, core_ids=list(range(ncore)))
    qkv_all = np.stack([np.asarray(res.results[i]["qkv1T"]) for i in range(ncore)])
    ncb = build_fused_b(T)
    in_maps_b = []
    for i in range(ncore):
        m = dict(consts)
        m.update(per_core[i])
        m["qkv1T"] = qkv_all[i]
        m["qkv_all"] = qkv_all
        m["xn1T"] = np.asarray(res.results[i]["xn1T"])
        m["x1T"] = np.asarray(res.results[i]["x1T"])
        in_maps_b.append(m)
    res_b = run_bass_kernel_spmd(ncb, in_maps_b, core_ids=list(range(ncore)))
    out = np.stack([np.ascontiguousarray(np.asarray(res_b.results[i]["outT"]).T)
                    for i in range(ncore)])
    return out


# version 10
# speedup vs baseline: 1.2162x; 1.2102x over previous
import numpy as np
import ml_dtypes
from contextlib import ExitStack

import concourse.bass as bass
import concourse.mybir as mybir
from concourse.bass_utils import run_bass_kernel_spmd

F32 = mybir.dt.float32
BF16 = mybir.dt.bfloat16
ALU = mybir.AluOpType
AF = mybir.ActivationFunctionType
AX = mybir.AxisListType

NEG = -1.0e30
D = 1024
DC = D // 128
TFULL = 4096

ENGS = ("sync", "vector", "scalar", "gpsimd", "tensor")
DMA_LANES = {"sync": ("qs0", "qs1", "qs2", "qs3"),
             "gpsimd": ("qg0", "qg1", "qg2", "qg3"),
             "vector": ("qv0",), "scalar": ("qc0",)}
SEM_NAMES = ("qs0", "qs1", "qs2", "qs3", "qg0", "qg1", "qg2", "qg3",
             "qv0", "qc0", "cv", "cs", "cg", "ct")
CMP_SEM = {"vector": "cv", "scalar": "cs", "gpsimd": "cg", "tensor": "ct"}


class G:
    __slots__ = ("w", "r", "name")

    def __init__(self, name=""):
        self.w = None
        self.r = []
        self.name = name


class Rec:
    def __init__(self):
        self.ops = {e: [] for e in ENGS}
        self.cnt = {s: 0 for s in SEM_NAMES}
        self.waited = {e: {s: 0 for s in SEM_NAMES} for e in ENGS}
        self.lane_i = {e: 0 for e in DMA_LANES}
        self.sems = None
        self.nops = 0
        self.fence = []

    def _wait(self, eng, tok, cur_stream, enforce_same_queue=True):
        if tok is None:
            return
        sem, val, stream = tok
        if stream == cur_stream:
            if stream.startswith("c:") or not enforce_same_queue:
                return
        if self.waited[eng][sem] >= val:
            return
        self.waited[eng][sem] = val

        def fn(e, sem=sem, val=val):
            e.wait_ge(self.sems[sem], val)
        self.ops[eng].append(fn)

    PARANOID_COMPUTE = False

    def op(self, eng, fn, reads=(), writes=(), dma=False):
        if dma:
            lanes = DMA_LANES[eng]
            sem = lanes[self.lane_i[eng] % len(lanes)]
            self.lane_i[eng] += 1
            amt = 16
            stream = "d:" + eng
            prev = self.cnt[sem]
            if prev > 0 and self.waited[eng][sem] < prev:
                self.waited[eng][sem] = prev
                self.ops[eng].append(
                    lambda e, sem=sem, prev=prev: e.wait_ge(self.sems[sem], prev))
        else:
            sem = CMP_SEM[eng]
            amt = 1
            stream = "c:" + eng
        if Rec.PARANOID_COMPUTE and not dma:
            for s2 in SEM_NAMES:
                val = self.cnt[s2]
                if val > 0 and self.waited[eng][s2] < val:
                    self.waited[eng][s2] = val
                    self.ops[eng].append(
                        lambda e, s2=s2, val=val: e.wait_ge(self.sems[s2], val))
        for b in reads:
            self._wait(eng, b.w, stream)
        for b in writes:
            if b.w is None and not b.r:
                for t in self.fence:
                    self._wait(eng, t, stream)
            self._wait(eng, b.w, stream, enforce_same_queue=False)
            for t in b.r:
                self._wait(eng, t, stream)
        self.cnt[sem] += amt
        tok = (sem, self.cnt[sem], stream)

        def wrapped(e, fn=fn, sem=sem, amt=amt):
            fn(e).then_inc(self.sems[sem], amt)
        self.ops[eng].append(wrapped)
        self.nops += 1
        for b in writes:
            b.w = tok
            b.r = []
        for b in reads:
            b.r.append(tok)
        return tok

    PARANOID_DMA = False

    def dma(self, eng, out, in_, reads=(), writes=(), **kw):
        if Rec.PARANOID_DMA:
            s = DMA_SEM[eng]
            val = self.cnt[s]
            if val > 0 and self.waited[eng][s] < val:
                self.waited[eng][s] = val
                self.ops[eng].append(
                    lambda e, s=s, val=val: e.wait_ge(self.sems[s], val))
        return self.op(eng, lambda e, out=out, in_=in_, kw=kw: e.dma_start(out=out, in_=in_, **kw),
                       reads=reads, writes=writes, dma=True)

    def barrier_token(self):
        return [(s, self.cnt[s], "x") for s in SEM_NAMES if self.cnt[s] > 0]

    def final_drain(self, eng="sync"):
        for s in SEM_NAMES:
            if self.cnt[s] > 0:
                val = self.cnt[s]
                if self.waited[eng][s] >= val:
                    continue
                self.waited[eng][s] = val

                def fn(e, s=s, val=val):
                    e.wait_ge(self.sems[s], val)
                self.ops[eng].append(fn)

    def build(self, nc, stack):
        self.sems = {s: stack.enter_context(nc.semaphore(s)) for s in SEM_NAMES}
        block = stack.enter_context(nc.Block())
        for eng in ENGS:
            ops = self.ops[eng]
            if not ops:
                continue

            def replay(e, ops=ops):
                for fn in ops:
                    fn(e)
            getattr(block, eng)(replay)


class Ctx:

    def __init__(self, nc, stack, T=TFULL):
        self.nc = nc
        self.stack = stack
        self.r = Rec()
        self.T = T
        self.NTG = T // 512
        self.NQT = T // 128
        self._sb = {}
        self._g = {}
        self._scopes = []
        self.pb = stack.enter_context(nc.psum_tensor("psum_all", [128, 4096], F32))
        self.consts = None
        self.hbm = {}

    def push(self):
        self._scopes.append((ExitStack(), []))

    def pop(self):
        st, names = self._scopes.pop()
        for n in names:
            del self._sb[n]
            del self._g["sb:" + n]
        st.close()
        self.r.fence = self.r.barrier_token()

    _uniq = 0

    def sb(self, name, shape, dtype):
        if name not in self._sb:
            if self._scopes:
                st, names = self._scopes[-1]
                names.append(name)
            else:
                st = self.stack
            Ctx._uniq += 1
            t = st.enter_context(self.nc.sbuf_tensor(
                "%s__%d" % (name, Ctx._uniq), shape, dtype))
            self._sb[name] = t
            self._g["sb:" + name] = G("sb:" + name)
        return self._sb[name]

    def g(self, name):
        return self._g["sb:" + name]

    def bankg(self, i):
        k = "pb:%d" % i
        if k not in self._g:
            self._g[k] = G(k)
        return self._g[k]

    def bank(self, i, n=1):
        return self.pb[:, 512 * i:512 * (i + n)]

    def dram(self, name, shape, dtype, kind="Internal"):
        if name not in self.hbm:
            ap = self.nc.dram_tensor(name, shape, dtype, kind=kind).ap()
            self.hbm[name] = (ap, G("hbm:" + name))
        return self.hbm[name]

    def finish(self):
        self.r.final_drain("sync")
        self.r.build(self.nc, self.stack)



def _pm(vec):
    v = np.asarray(vec)
    n = v.shape[-1] // 128
    return np.ascontiguousarray(v.reshape(n, 128).T)


def host_consts(T=TFULL):
    c = {}
    c["ones_col_f"] = np.ones((128, 1), np.float32)
    c["ones_row_f"] = np.ones((1, 128), np.float32)
    c["ones_row_b"] = np.ones((1, 128), ml_dtypes.bfloat16)
    c["ones_col_b"] = np.ones((128, 1), ml_dtypes.bfloat16)
    f = np.arange(512)[None, :]
    p = np.arange(128)[:, None]
    masks = [np.where(f <= o + p, 0.0, NEG).astype(np.float32) for o in (0, 128, 256, 384)]
    c["cmask"] = np.stack(masks)
    qt = np.arange(T // 128)[None, :]
    c["countdown"] = ((T - 1) - (qt * 128 + p)).astype(np.float32)
    seg = np.zeros((128, 2), np.float32)
    seg[:64, 0] = 1.0
    seg[64:, 1] = 1.0
    c["seg_lhsT"] = seg.astype(ml_dtypes.bfloat16)
    c["ind_lhsT"] = np.ascontiguousarray(seg.T).astype(ml_dtypes.bfloat16)
    c["ind_f"] = np.ascontiguousarray(seg.T)
    c["ident_f"] = np.eye(128, dtype=np.float32)
    kv = np.zeros((128, 8), np.float32)
    kv[:, 0] = 1e-12
    c["kvals"] = kv
    return c



def host_inputs(core, inputs, T=TFULL):
    g = lambda k: np.asarray(inputs[k])
    m = {}
    x = g("L0_attn_X")[core]
    m["xT"] = np.ascontiguousarray(x.T).astype(np.float32)
    m["w_q"] = g("L0_attn_Wq").astype(np.float32)
    m["w_k"] = g("L0_attn_Wk").astype(np.float32)
    m["w_v"] = g("L0_attn_Wv").astype(np.float32)
    m["b_qkv0"] = np.concatenate(
        [_pm(g("L0_attn_bq")), _pm(g("L0_attn_bk")), _pm(g("L0_attn_bv"))],
        axis=1).astype(np.float32)
    rwt = np.zeros((1024, 64), np.float32)
    rwt[:, :32] = g("L0_ffn_router_w").T
    m["router_wT"] = rwt
    rbp = np.zeros((1, 64), np.float32)
    rbp[0, :32] = g("L0_ffn_router_bias")
    m["router_b"] = rbp
    m["gate_wT"] = np.ascontiguousarray(g("L0_ffn_gate_w").transpose(0, 2, 1)).astype(np.float32)
    m["up_wT"] = np.ascontiguousarray(g("L0_ffn_up_w").transpose(0, 2, 1)).astype(np.float32)
    m["down_wT"] = np.ascontiguousarray(g("L0_ffn_down_w").transpose(0, 2, 1)).astype(np.float32)
    m["sgate_wT"] = np.ascontiguousarray(g("L0_ffn_sgate_w").T).astype(np.float32)
    m["sup_wT"] = np.ascontiguousarray(g("L0_ffn_sup_w").T).astype(np.float32)
    m["sdown_wT"] = np.ascontiguousarray(g("L0_ffn_sdown_w").T).astype(np.float32)
    m["wqkv1T"] = np.ascontiguousarray(g("L1_attn_W_qkv").T).astype(np.float32)
    m["b_qkv1"] = _pm(g("L1_attn_b_qkv")).astype(np.float32)
    m["l1f_We"] = g("L1_ffn_We").astype(np.float32)
    m["l1f_be"] = np.ascontiguousarray(
        g("L1_ffn_be").reshape(20, 8, 128).transpose(0, 2, 1)).astype(np.float32)
    m["l1f_Wq"] = g("L1_ffn_Wq").astype(np.float32)
    m["l1f_Wk"] = g("L1_ffn_Wk").astype(np.float32)
    m["l1f_Wv"] = g("L1_ffn_Wv").astype(np.float32)
    m["l1f_Wo"] = g("L1_ffn_Wo").astype(np.float32)
    m["l1f_lng"] = _pm(g("L1_ffn_ln_g")).astype(np.float32)
    m["l1f_lnb"] = _pm(g("L1_ffn_ln_b")).astype(np.float32)
    m["l1f_ew"] = np.ascontiguousarray(g("L1_ffn_ew").reshape(1, 18)).astype(np.float32)
    m["l1f_actw"] = np.ascontiguousarray(g("L1_ffn_act_w")[None, :]).astype(np.float32)
    return m


CONST_SPECS = {
    "ones_col_f": ([128, 1], F32),
    "ones_row_f": ([1, 128], F32),
    "ones_row_b": ([1, 128], BF16),
    "ones_col_b": ([128, 1], BF16),
    "cmask": ([4, 128, 512], F32),
    "countdown": (None, F32),
    "seg_lhsT": ([128, 2], BF16),
    "ind_lhsT": ([2, 128], BF16),
    "ind_f": ([2, 128], F32),
    "ident_f": ([128, 128], F32),
    "kvals": ([128, 8], F32),
}


def declare_consts(C, only=None):
    nc, r = C.nc, C.r
    c = {}
    for name, (shape, dt) in CONST_SPECS.items():
        if only is not None and name not in only:
            continue
        if name == "countdown":
            shape = [128, C.T // 128]
        ap = nc.dram_tensor(name, shape, dt, kind="ExternalInput").ap()
        if name == "cmask":
            t = C.sb("k_cmask", [128, 4 * 512], F32)
            for gi in range(4):
                r.dma("sync", t[:, gi * 512:(gi + 1) * 512], ap[gi], writes=[C.g("k_cmask")])
        else:
            t = C.sb("k_" + name, shape, dt)
            r.dma("sync", t[:], ap, writes=[C.g("k_" + name)])
        c[name] = t
    C.consts = c
    return c



def emit_ln(C, x_hbm, xg, xn_hbm, xng, eps, gamma_beta=None, tag="ln",
            in_dtype=F32, out_dtype=BF16):
    r = C.r
    xb = C.sb(tag + "_x", [128, DC * 512], F32)
    if in_dtype != F32:
        xbr = C.sb(tag + "_xr", [128, DC * 512], in_dtype)
        gxbr = C.g(tag + "_xr")
    sq = C.sb(tag + "_sq", [128, DC * 512], F32)
    stats = C.sb(tag + "_st", [1, 1024], F32)
    ab = C.sb(tag + "_ab", [128, 1024], F32)
    xnb = C.sb(tag + "_xn", [128, DC * 512], out_dtype)
    gxb, gsq, gst, gab, gxn = (C.g(tag + "_x"), C.g(tag + "_sq"), C.g(tag + "_st"),
                               C.g(tag + "_ab"), C.g(tag + "_xn"))
    if gamma_beta is not None:
        gb = C.sb(tag + "_gb", [128, 16], F32)
        ggb = C.g(tag + "_gb")
        r.dma("sync", gb[:, 0:8], gamma_beta[0], writes=[ggb])
        r.dma("sync", gb[:, 8:16], gamma_beta[1], writes=[ggb])
    ones_f = C.consts["ones_col_f"]
    ones_row_f = C.consts["ones_row_f"]
    konesg = C.g("k_ones_col_f")
    inv_d = 1.0 / D
    for tg in range(C.NTG):
        if in_dtype == F32:
            for ic in range(DC):
                r.dma("sync", xb[:, ic * 512:(ic + 1) * 512],
                      x_hbm[ic * 128:(ic + 1) * 128, tg * 512:(tg + 1) * 512],
                      reads=[xg], writes=[gxb])
        else:
            for ic in range(DC):
                r.dma("sync", xbr[:, ic * 512:(ic + 1) * 512],
                      x_hbm[ic * 128:(ic + 1) * 128, tg * 512:(tg + 1) * 512],
                      reads=[xg], writes=[gxbr])
            for ic in range(DC):
                sl = slice(ic * 512, (ic + 1) * 512)
                r.op("vector", lambda e, sl=sl: e.tensor_copy(out=xb[:, sl], in_=xbr[:, sl]),
                     reads=[gxbr], writes=[gxb])
        for ic in range(DC):
            sl = slice(ic * 512, (ic + 1) * 512)
            r.op("vector", lambda e, sl=sl: e.tensor_mul(out=sq[:, sl], in0=xb[:, sl], in1=xb[:, sl]),
                 reads=[gxb], writes=[gsq])
        for ic in range(DC):
            sl = slice(ic * 512, (ic + 1) * 512)
            r.op("tensor", lambda e, sl=sl, ic=ic: e.matmul(
                C.bank(0)[0:1, :], ones_f[:], xb[:, sl], start=(ic == 0), stop=(ic == DC - 1)),
                reads=[gxb, konesg], writes=[C.bankg(0)])
        for ic in range(DC):
            sl = slice(ic * 512, (ic + 1) * 512)
            r.op("tensor", lambda e, sl=sl, ic=ic: e.matmul(
                C.bank(1)[0:1, :], ones_f[:], sq[:, sl], start=(ic == 0), stop=(ic == DC - 1)),
                reads=[gsq, konesg], writes=[C.bankg(1)])
        r.op("vector", lambda e: e.tensor_scalar_mul(out=stats[0:1, 0:512], in0=C.bank(0)[0:1, :], scalar1=inv_d),
             reads=[C.bankg(0)], writes=[gst])
        r.op("vector", lambda e: e.tensor_scalar_mul(out=stats[0:1, 512:1024], in0=C.bank(1)[0:1, :], scalar1=inv_d),
             reads=[C.bankg(1)], writes=[gst])
        r.op("vector", lambda e: e.tensor_mul(out=sq[0:1, 0:512], in0=stats[0:1, 0:512], in1=stats[0:1, 0:512]),
             reads=[gst], writes=[gsq])
        r.op("vector", lambda e: e.tensor_sub(out=stats[0:1, 512:1024], in0=stats[0:1, 512:1024], in1=sq[0:1, 0:512]),
             reads=[gsq], writes=[gst])
        r.op("vector", lambda e: e.tensor_scalar_add(out=stats[0:1, 512:1024], in0=stats[0:1, 512:1024], scalar1=float(eps)),
             reads=[gst], writes=[gst])
        r.op("scalar", lambda e: e.activation(out=stats[0:1, 512:1024], in_=stats[0:1, 512:1024], func=AF.Ln),
             reads=[gst], writes=[gst])
        r.op("scalar", lambda e: e.activation(out=stats[0:1, 512:1024], in_=stats[0:1, 512:1024], func=AF.Exp, scale=-0.5),
             reads=[gst], writes=[gst])
        r.op("tensor", lambda e: e.matmul(C.bank(2), ones_row_f[:], stats[0:1, 0:512], start=True, stop=True),
             reads=[gst, C.g("k_ones_row_f")], writes=[C.bankg(2)])
        r.op("tensor", lambda e: e.matmul(C.bank(3), ones_row_f[:], stats[0:1, 512:1024], start=True, stop=True),
             reads=[gst, C.g("k_ones_row_f")], writes=[C.bankg(3)])
        r.op("vector", lambda e: e.tensor_copy(out=ab[:, 0:512], in_=C.bank(2)),
             reads=[C.bankg(2)], writes=[gab])
        r.op("vector", lambda e: e.tensor_copy(out=ab[:, 512:1024], in_=C.bank(3)),
             reads=[C.bankg(3)], writes=[gab])
        for ic in range(DC):
            sl = slice(ic * 512, (ic + 1) * 512)
            r.op("vector", lambda e, sl=sl: e.tensor_sub(out=sq[:, sl], in0=xb[:, sl], in1=ab[:, 0:512]),
                 reads=[gxb, gab], writes=[gsq])
            if gamma_beta is None:
                r.op("vector", lambda e, sl=sl: e.tensor_mul(out=xnb[:, sl], in0=sq[:, sl], in1=ab[:, 512:1024]),
                     reads=[gsq, gab], writes=[gxn])
            else:
                r.op("vector", lambda e, sl=sl: e.tensor_mul(out=sq[:, sl], in0=sq[:, sl], in1=ab[:, 512:1024]),
                     reads=[gsq, gab], writes=[gsq])
                r.op("vector", lambda e, sl=sl, ic=ic: e.tensor_scalar(
                    out=xnb[:, sl], in0=sq[:, sl], scalar1=gb[:, ic:ic + 1],
                    scalar2=gb[:, 8 + ic:9 + ic], op0=ALU.mult, op1=ALU.add),
                    reads=[gsq, ggb], writes=[gxn])
        for ic in range(DC):
            sl = slice(ic * 512, (ic + 1) * 512)
            r.dma("sync", xn_hbm[ic * 128:(ic + 1) * 128, tg * 512:(tg + 1) * 512], xnb[:, sl],
                  reads=[gxn], writes=[xng])
    return


def load_slab(C, name, hbm_ap, hg, nchunk, T, dtype=BF16, queue="sync"):
    t = C.sb(name, [128, nchunk * T], dtype)
    g = C.g(name)
    for ic in range(nchunk):
        C.r.dma(queue, t[:, ic * T:(ic + 1) * T], hbm_ap[ic * 128:(ic + 1) * 128, :],
                reads=[hg], writes=[g])
    return t, g


def load_w(C, name, w_hbm, wg, in_chunks, out_cols):
    t = C.sb(name, [128, in_chunks * out_cols], BF16)
    g = C.g(name)
    for ic in range(in_chunks):
        C.r.dma("gpsimd", t[:, ic * out_cols:(ic + 1) * out_cols],
                w_hbm[ic * 128:(ic + 1) * 128, :], reads=[wg], writes=[g])
    return t, g


def wsl(C, wname, in_chunks, out_cols, ic, oc):
    return C._sb[wname][:, ic * out_cols + oc * 128: ic * out_cols + (oc + 1) * 128]


def rbufs(C, base, n, shape, dtype):
    return [(C.sb("%s%d" % (base, j), shape, dtype), C.g("%s%d" % (base, j)))
            for j in range(n)]


def evac(C, i, bank, bankg, out_sb, out_g, bias=None, biasg=None,
         scale=None, scaleg=None):
    r = C.r
    extra = [g for g in (biasg, scaleg) if g is not None]
    if i % 2 == 0:
        if bias is not None:
            r.op("vector", lambda e, o=out_sb, b=bank, s=bias: e.tensor_scalar_add(
                out=o, in0=b, scalar1=s), reads=[bankg] + extra, writes=[out_g])
        elif scale is not None:
            r.op("vector", lambda e, o=out_sb, b=bank, s=scale: e.tensor_scalar_mul(
                out=o, in0=b, scalar1=s), reads=[bankg] + extra, writes=[out_g])
        else:
            r.op("vector", lambda e, o=out_sb, b=bank: e.tensor_copy(out=o, in_=b),
                 reads=[bankg], writes=[out_g])
    else:
        if bias is not None:
            r.op("scalar", lambda e, o=out_sb, b=bank, s=bias: e.activation(
                out=o, in_=b, func=AF.Identity, bias=s),
                reads=[bankg] + extra, writes=[out_g])
        elif scale is not None:
            r.op("scalar", lambda e, o=out_sb, b=bank, s=scale: e.activation(
                out=o, in_=b, func=AF.Identity, scale=s),
                reads=[bankg] + extra, writes=[out_g])
        else:
            r.op("scalar", lambda e, o=out_sb, b=bank: e.copy(out=o, in_=b),
                 reads=[bankg], writes=[out_g])


def emit_proj(C, slab_name, w_name, in_chunks, out_cols, epi, banks=(0, 1), tgs=None):
    r = C.r
    slab = C._sb[slab_name]
    sg = C.g(slab_name)
    wg = C.g(w_name)
    T = C.T
    if tgs is None:
        tgs = range(C.NTG)
    nb = len(banks)
    i = 0
    for oc in range(out_cols // 128):
        for tg in tgs:
            b = banks[i % nb]
            i += 1
            for ic in range(in_chunks):
                lh = wsl(C, w_name, in_chunks, out_cols, ic, oc)
                rh = slab[:, ic * T + tg * 512: ic * T + tg * 512 + 512]
                r.op("tensor", lambda e, lh=lh, rh=rh, ic=ic, b=b: e.matmul(
                    C.bank(b), lh, rh,
                    start=(ic == 0), stop=(ic == in_chunks - 1)),
                    reads=[sg, wg], writes=[C.bankg(b)])
            epi(oc, tg, C.bank(b), C.bankg(b))


def emit_l0attn(C, xn_hbm, xng, s_hbm, sg_out, wq_ap, wqg, wk_ap, wkg, wv_ap, wvg,
                b0_ap, b0g):
    r, T, NTG, NQT = C.r, C.T, C.NTG, C.NQT
    nc = C.nc
    q_hbm, qg = C.dram("sc_q0T", [D, T], BF16)
    bqkv = C.sb("l0a_b", [128, 24], F32)
    r.dma("sync", bqkv[:], b0_ap, reads=[b0g], writes=[C.g("l0a_b")])
    bg = C.g("l0a_b")

    xnslab, xnsg = load_slab(C, "l0a_xn", xn_hbm, xng, DC, T)
    load_w(C, "l0a_w", wq_ap, wqg, DC, D)

    evs = rbufs(C, "l0a_ev", 2, [128, 512], BF16)
    cnt = [0]

    def epi_q(oc, tg, bank, bankg):
        i = cnt[0]; cnt[0] += 1
        tmp, gtmp = evs[i % 2]
        evac(C, i, bank, bankg, tmp[:], gtmp, bias=bqkv[:, oc:oc + 1], biasg=bg)
        r.dma("sync", q_hbm[oc * 128:(oc + 1) * 128, tg * 512:(tg + 1) * 512], tmp[:],
              reads=[gtmp], writes=[qg])
    emit_proj(C, "l0a_xn", "l0a_w", DC, D, epi_q, banks=(0, 1, 2, 3))

    kslab = C.sb("l0a_k", [128, DC * T], BF16)
    kg = C.g("l0a_k")
    load_w(C, "l0a_w", wk_ap, wkg, DC, D)

    def epi_k(oc, tg, bank, bankg):
        i = cnt[0]; cnt[0] += 1
        evac(C, i, bank, bankg,
             kslab[:, oc * T + tg * 512: oc * T + tg * 512 + 512], kg,
             bias=bqkv[:, 8 + oc:9 + oc], biasg=bg)
    emit_proj(C, "l0a_xn", "l0a_w", DC, D, epi_k, banks=(0, 1, 2, 3))

    qt_b = C.sb("l0a_qt", [128, DC * 128], BF16)
    qtg = C.g("l0a_qt")
    srow = C.sb("l0a_srow", [128, T], F32)
    srg = C.g("l0a_srow")
    pexp = C.sb("l0a_p", [128, T], BF16)
    pg = C.g("l0a_p")
    cols = C.sb("l0a_cols", [128, 8], F32)
    colg = C.g("l0a_cols")
    ratio = C.sb("l0a_ratio", [128, NQT], F32)
    ratg = C.g("l0a_ratio")
    cmask = C.consts["cmask"]
    cmg = C.g("k_cmask")
    cd = C.consts["countdown"]
    cdg = C.g("k_countdown")
    scale = 1.0 / 32.0
    for qt in range(NQT):
        ngrp = qt // 4 + 1
        for ic in range(DC):
            r.dma("sync", qt_b[:, ic * 128:(ic + 1) * 128],
                  q_hbm[ic * 128:(ic + 1) * 128, qt * 128:(qt + 1) * 128],
                  reads=[qg], writes=[qtg])
        for kg_i in range(ngrp):
            b = kg_i % 2
            for ic in range(DC):
                r.op("tensor", lambda e, ic=ic, kg_i=kg_i, b=b: e.matmul(
                    C.bank(b), qt_b[:, ic * 128:(ic + 1) * 128],
                    kslab[:, ic * T + kg_i * 512: ic * T + kg_i * 512 + 512],
                    start=(ic == 0), stop=(ic == DC - 1)),
                    reads=[qtg, kg], writes=[C.bankg(b)])
            if kg_i == ngrp - 1:
                r.op("vector", lambda e, kg_i=kg_i, b=b, m=qt % 4: e.tensor_add(
                    out=srow[:, kg_i * 512:(kg_i + 1) * 512], in0=C.bank(b),
                    in1=cmask[:, m * 512:(m + 1) * 512]),
                    reads=[C.bankg(b), cmg], writes=[srg])
            else:
                r.op("vector", lambda e, kg_i=kg_i, b=b: e.tensor_copy(
                    out=srow[:, kg_i * 512:(kg_i + 1) * 512], in_=C.bank(b)),
                    reads=[C.bankg(b)], writes=[srg])
        kw = ngrp * 512
        r.op("vector", lambda e, kw=kw: e.reduce_max(out=cols[:, 0:1], in_=srow[:, 0:kw], axis=AX.X),
             reads=[srg], writes=[colg])
        r.op("scalar", lambda e: e.activation(out=cols[:, 1:2], in_=cols[:, 0:1], func=AF.Relu, scale=scale),
             reads=[colg], writes=[colg])
        r.op("scalar", lambda e: e.mul(out=cols[:, 2:3], in_=cols[:, 1:2], mul=-1.0),
             reads=[colg], writes=[colg])
        r.op("scalar", lambda e, kw=kw: e.activation(
            out=pexp[:, 0:kw], in_=srow[:, 0:kw], func=AF.Exp, bias=cols[:, 2:3],
            scale=scale, accum_out=cols[:, 3:4]),
            reads=[srg, colg], writes=[pg, colg])
        r.op("scalar", lambda e: e.activation(out=cols[:, 4:5], in_=cols[:, 2:3], func=AF.Exp),
             reads=[colg], writes=[colg])
        r.op("scalar", lambda e, qt=qt: e.activation(out=cols[:, 5:6], in_=cd[:, qt:qt + 1],
                                                     func=AF.Identity, scale=cols[:, 4:5]),
             reads=[colg, cdg], writes=[colg])
        r.op("scalar", lambda e: e.activation(out=cols[:, 6:7], in_=cols[:, 3:4],
                                              func=AF.Identity, bias=cols[:, 5:6]),
             reads=[colg], writes=[colg])
        r.op("vector", lambda e: e.reciprocal(out=cols[:, 7:8], in_=cols[:, 6:7]),
             reads=[colg], writes=[colg])
        r.op("scalar", lambda e, qt=qt: e.activation(out=ratio[:, qt:qt + 1], in_=cols[:, 7:8],
                                                     func=AF.Identity, scale=cols[:, 6:7]),
             reads=[colg], writes=[ratg])

    rrows = C.sb("l0a_rr", [NQT, 128], F32)
    rrg = C.g("l0a_rr")
    rflat = C.sb("l0a_rf", [1, T], F32)
    rfg = C.g("l0a_rf")
    ident = C.consts["ident_f"]
    idg = C.g("k_ident_f")
    r.op("tensor", lambda e: e.transpose(C.bank(4)[0:NQT, 0:128], ratio[:], ident[:]),
         reads=[ratg, idg], writes=[C.bankg(4)])
    r.op("vector", lambda e: e.tensor_copy(out=rrows[:], in_=C.bank(4)[0:NQT, 0:128]),
         reads=[C.bankg(4)], writes=[rrg])
    rdram, rdg = C.dram("sc_l0a_r", [NQT, 128], F32)
    r.dma("sync", rdram, rrows[:], reads=[rrg], writes=[rdg])
    r.dma("sync", rflat[:], rdram.rearrange("q p -> (q p)")[None, :], reads=[rdg], writes=[rfg])

    load_w(C, "l0a_w", wv_ap, wvg, DC, D)
    rb = C.sb("l0a_rb", [128, 512], F32)
    rbg = C.g("l0a_rb")
    sevs = rbufs(C, "l0a_sev", 2, [128, 512], F32)
    ones_row_f = C.consts["ones_row_f"]
    org = C.g("k_ones_row_f")
    cnt2 = [0]

    def epi_v(oc, tg, bank, bankg):
        i = cnt2[0]; cnt2[0] += 1
        sev, sevg = sevs[i % 2]
        if oc == 0:
            r.op("tensor", lambda e, tg=tg: e.matmul(
                C.bank(5), ones_row_f[:], rflat[0:1, tg * 512:(tg + 1) * 512],
                start=True, stop=True),
                reads=[rfg, org], writes=[C.bankg(5)])
            r.op("vector", lambda e: e.tensor_copy(out=rb[:], in_=C.bank(5)),
                 reads=[C.bankg(5)], writes=[rbg])
        r.op("vector", lambda e, oc=oc, bank=bank, sev=sev: e.scalar_tensor_tensor(
            out=sev[:], in0=bank, scalar=bqkv[:, 16 + oc:17 + oc], in1=rb[:],
            op0=ALU.add, op1=ALU.mult),
            reads=[bankg, bg, rbg], writes=[sevg])
        r.dma("sync", s_hbm[oc * 128:(oc + 1) * 128, tg * 512:(tg + 1) * 512], sev[:],
              reads=[sevg], writes=[sg_out])
    emit_proj(C, "l0a_xn", "l0a_w", DC, D, epi_v, banks=(6, 7))

def emit_l0ffn(C, xn_hbm, xng, x_hbm, xg, s_hbm, sg_in, x1_hbm, x1g,
               rw_ap, rwg, rb_ap, rbg_in, gw_ap, gwg, uw_ap, uwg, dw_ap, dwg,
               sga_ap, sgag, sua_ap, suag, sda_ap, sdag):
    r, T, NTG = C.r, C.T, C.NTG
    E = 32
    xnslab, xnsg = load_slab(C, "l0f_xn", xn_hbm, xng, DC, T)
    rwsl, rwsg = load_w(C, "l0f_rw", rw_ap, rwg, DC, 64)
    rbrow = C.sb("l0f_rb", [1, 64], F32)
    r.dma("sync", rbrow[:], rb_ap, reads=[rbg_in], writes=[C.g("l0f_rb")])
    rbb = C.sb("l0f_rbb", [128, 64], F32)
    r.op("tensor", lambda e: e.matmul(C.bank(0)[:, 0:64], C.consts["ones_row_f"][:], rbrow[:],
                                      start=True, stop=True),
         reads=[C.g("l0f_rb"), C.g("k_ones_row_f")], writes=[C.bankg(0)])
    r.op("vector", lambda e: e.tensor_copy(out=rbb[:], in_=C.bank(0)[:, 0:64]),
         reads=[C.bankg(0)], writes=[C.g("l0f_rbb")])

    z = C.sb("l0f_z", [128, 64], F32)
    zw = C.sb("l0f_zw", [128, 64], F32)
    t32 = C.sb("l0f_t32", [128, 64], F32)
    wtsf = C.sb("l0f_wts", [128, 64], F32)
    cols = C.sb("l0f_c", [128, 8], F32)
    wrows = C.sb("l0f_wr", [32, T], BF16)
    gz, gzw, gt32, gwt, gc, gwr = (C.g("l0f_z"), C.g("l0f_zw"), C.g("l0f_t32"),
                                   C.g("l0f_wts"), C.g("l0f_c"), C.g("l0f_wr"))
    ident = C.consts["ident_f"]
    idg = C.g("k_ident_f")
    for tt in range(T // 128):
        for ic in range(DC):
            r.op("tensor", lambda e, ic=ic, tt=tt: e.matmul(
                C.bank(1)[:, 0:64], xnslab[:, ic * T + tt * 128: ic * T + tt * 128 + 128],
                rwsl[:, ic * 64:(ic + 1) * 64], start=(ic == 0), stop=(ic == DC - 1)),
                reads=[xnsg, rwsg], writes=[C.bankg(1)])
        r.op("vector", lambda e: e.tensor_add(out=z[:], in0=C.bank(1)[:, 0:64], in1=rbb[:]),
             reads=[C.bankg(1), C.g("l0f_rbb")], writes=[gz])
        r.op("vector", lambda e: e.reduce_max(out=cols[:, 0:1], in_=z[:, 0:E], axis=AX.X),
             reads=[gz], writes=[gc])
        r.op("scalar", lambda e: e.mul(out=cols[:, 1:2], in_=cols[:, 0:1], mul=-1.0),
             reads=[gc], writes=[gc])
        r.op("scalar", lambda e: e.activation(out=z[:, 0:E], in_=z[:, 0:E], func=AF.Exp, bias=cols[:, 1:2]),
             reads=[gz, gc], writes=[gz])
        r.op("vector", lambda e: e.tensor_copy(out=zw[:], in_=z[:]), reads=[gz], writes=[gzw])
        for it in range(4):
            r.op("vector", lambda e: e.reduce_max(out=cols[:, 2:3], in_=zw[:, 0:E], axis=AX.X),
                 reads=[gzw], writes=[gc])
            r.op("scalar", lambda e: e.copy(out=cols[:, 3:4], in_=cols[:, 2:3]),
                 reads=[gc], writes=[gc])
            if it < 3:
                r.op("vector", lambda e: e.tensor_single_scalar(
                    out=t32[:], in_=zw[:], scalar=cols[:, 3:4], op=ALU.is_ge),
                    reads=[gzw, gc], writes=[gt32])
                r.op("vector", lambda e: e.scalar_tensor_tensor(
                    out=zw[:], in0=t32[:], scalar=NEG, in1=zw[:], op0=ALU.mult, op1=ALU.add),
                    reads=[gt32, gzw], writes=[gzw])
        r.op("vector", lambda e: e.scalar_tensor_tensor(
            out=t32[:], in0=z[:], scalar=cols[:, 3:4], in1=z[:], op0=ALU.is_ge,
            op1=ALU.mult, accum_out=cols[:, 4:5]),
            reads=[gz, gc], writes=[gt32, gc])
        r.op("scalar", lambda e: e.copy(out=cols[:, 5:6], in_=cols[:, 4:5]),
             reads=[gc], writes=[gc])
        r.op("vector", lambda e: e.reciprocal(out=cols[:, 6:7], in_=cols[:, 5:6]),
             reads=[gc], writes=[gc])
        r.op("scalar", lambda e: e.copy(out=cols[:, 7:8], in_=cols[:, 6:7]),
             reads=[gc], writes=[gc])
        r.op("vector", lambda e: e.tensor_scalar_mul(out=wtsf[:], in0=t32[:], scalar1=cols[:, 7:8]),
             reads=[gt32, gc], writes=[gwt])
        r.op("tensor", lambda e: e.transpose(C.bank(2)[0:32, 0:128], wtsf[:, 0:32], ident[:]),
             reads=[gwt, idg], writes=[C.bankg(2)])
        r.op("vector", lambda e, tt=tt: e.tensor_copy(out=wrows[0:32, tt * 128:(tt + 1) * 128],
                                                      in_=C.bank(2)[0:32, 0:128]),
             reads=[C.bankg(2)], writes=[gwr])
    wr_dram, gwrd = C.dram("sc_l0f_wr", [32, T], BF16)
    r.dma("sync", wr_dram, wrows[:], reads=[gwr], writes=[gwrd])
    wfbs = rbufs(C, "l0f_wf", 2, [1, 512], BF16)

    gwes = rbufs(C, "l0f_gwe", 2, [128, DC * 128], BF16)
    uwes = rbufs(C, "l0f_uwe", 2, [128, DC * 128], BF16)
    dwes = rbufs(C, "l0f_dwe", 2, [128, E * 128], BF16)
    sgw, _ = load_w(C, "l0f_sgw", sga_ap, sgag, DC, 512)
    suw, _ = load_w(C, "l0f_suw", sua_ap, suag, DC, 512)
    sdw, _ = load_w(C, "l0f_sdw", sda_ap, sdag, 4, D)
    gsgw, gsuw, gsdw = C.g("l0f_sgw"), C.g("l0f_suw"), C.g("l0f_sdw")
    sgbs = rbufs(C, "l0f_sgb", 2, [128, 512], BF16)
    subs = rbufs(C, "l0f_sub", 2, [128, 512], BF16)
    zsh = C.sb("l0f_zsh", [128, 4 * 512], BF16)
    gzsh = C.g("l0f_zsh")
    hslab = C.sb("l0f_h", [128, E * 512], BF16)
    gh = C.g("l0f_h")
    wbs = rbufs(C, "l0f_wb", 2, [128, 512], BF16)
    sg_ts = rbufs(C, "l0f_sg", 2, [128, 512], BF16)
    su_ts = rbufs(C, "l0f_su", 2, [128, 512], BF16)
    xchs = rbufs(C, "l0f_xch", 2, [128, 512], F32)
    schs = rbufs(C, "l0f_sch", 2, [128, 512], F32)
    oevs = rbufs(C, "l0f_oev", 2, [128, 512], F32)
    onesb = C.consts["ones_row_b"]
    gob = C.g("k_ones_row_b")
    gw3 = gw_ap.rearrange("e (ic p) o -> e p ic o", p=128)
    uw3 = uw_ap.rearrange("e (ic p) o -> e p ic o", p=128)

    for tg in range(NTG):
        tsl = slice(tg * 512, (tg + 1) * 512)
        for hc in range(4):
            b = hc % 2
            sgb, gsgb = sgbs[b]
            sub, gsub = subs[b]
            for ic in range(DC):
                r.op("tensor", lambda e, ic=ic, hc=hc, tg=tg, b=b: e.matmul(
                    C.bank(b), sgw[:, ic * 512 + hc * 128: ic * 512 + (hc + 1) * 128],
                    xnslab[:, ic * T + tg * 512: ic * T + tg * 512 + 512],
                    start=(ic == 0), stop=(ic == DC - 1)),
                    reads=[xnsg, gsgw], writes=[C.bankg(b)])
            for ic in range(DC):
                r.op("tensor", lambda e, ic=ic, hc=hc, tg=tg, b=b: e.matmul(
                    C.bank(6 + b), suw[:, ic * 512 + hc * 128: ic * 512 + (hc + 1) * 128],
                    xnslab[:, ic * T + tg * 512: ic * T + tg * 512 + 512],
                    start=(ic == 0), stop=(ic == DC - 1)),
                    reads=[xnsg, gsuw], writes=[C.bankg(6 + b)])
            r.op("scalar", lambda e, b=b, sgb=sgb: e.activation(out=sgb[:], in_=C.bank(b), func=AF.Silu),
                 reads=[C.bankg(b)], writes=[gsgb])
            r.op("vector", lambda e, b=b, sub=sub: e.tensor_copy(out=sub[:], in_=C.bank(6 + b)),
                 reads=[C.bankg(6 + b)], writes=[gsub])
            r.op("vector", lambda e, hc=hc, sgb=sgb, sub=sub: e.tensor_mul(
                out=zsh[:, hc * 512:(hc + 1) * 512], in0=sgb[:], in1=sub[:]),
                reads=[gsgb, gsub], writes=[gzsh])
        for e_i in range(E):
            b = e_i % 2
            gwe, ggw = gwes[b]
            uwe, guw = uwes[b]
            sg_t, gsg = sg_ts[b]
            su_t, gsu = su_ts[b]
            wb, gwb = wbs[b]
            wfb, gwf = wfbs[b]
            r.dma("sync", wfb[:], wr_dram[e_i:e_i + 1, tsl], reads=[gwrd], writes=[gwf])
            r.dma("gpsimd", gwe[:].rearrange("p (ic o) -> p ic o", ic=DC),
                  gw3[e_i], reads=[gwg], writes=[ggw])
            r.dma("gpsimd", uwe[:].rearrange("p (ic o) -> p ic o", ic=DC),
                  uw3[e_i], reads=[uwg], writes=[guw])
            for ic in range(DC):
                r.op("tensor", lambda e, ic=ic, tg=tg, b=b, gwe=gwe: e.matmul(
                    C.bank(2 + b), gwe[:, ic * 128:(ic + 1) * 128],
                    xnslab[:, ic * T + tg * 512: ic * T + tg * 512 + 512],
                    start=(ic == 0), stop=(ic == DC - 1)),
                    reads=[xnsg, ggw], writes=[C.bankg(2 + b)])
            for ic in range(DC):
                r.op("tensor", lambda e, ic=ic, tg=tg, b=b, uwe=uwe: e.matmul(
                    C.bank(6 + b), uwe[:, ic * 128:(ic + 1) * 128],
                    xnslab[:, ic * T + tg * 512: ic * T + tg * 512 + 512],
                    start=(ic == 0), stop=(ic == DC - 1)),
                    reads=[xnsg, guw], writes=[C.bankg(6 + b)])
            r.op("tensor", lambda e, wfb=wfb: e.matmul(
                C.bank(4), onesb[:], wfb[:],
                start=True, stop=True),
                reads=[gwf, gob], writes=[C.bankg(4)])
            r.op("scalar", lambda e, b=b, sg_t=sg_t: e.activation(out=sg_t[:], in_=C.bank(2 + b), func=AF.Silu),
                 reads=[C.bankg(2 + b)], writes=[gsg])
            r.op("vector", lambda e, b=b, su_t=su_t: e.tensor_copy(out=su_t[:], in_=C.bank(6 + b)),
                 reads=[C.bankg(6 + b)], writes=[gsu])
            r.op("vector", lambda e, wb=wb: e.tensor_copy(out=wb[:], in_=C.bank(4)),
                 reads=[C.bankg(4)], writes=[gwb])
            r.op("vector", lambda e, sg_t=sg_t, su_t=su_t: e.tensor_mul(out=sg_t[:], in0=sg_t[:], in1=su_t[:]),
                 reads=[gsu], writes=[gsg])
            r.op("vector", lambda e, e_i=e_i, sg_t=sg_t, wb=wb: e.tensor_mul(
                out=hslab[:, e_i * 512:(e_i + 1) * 512], in0=sg_t[:], in1=wb[:]),
                reads=[gsg, gwb], writes=[gh])
        for oc in range(DC):
            p = oc % 2
            dwe, gdw = dwes[p]
            xch, gxc = xchs[p]
            sch, gsc = schs[p]
            out_ev, goev = oevs[p]
            dbank = 5
            r.dma("gpsimd", dwe[:].rearrange("h (e o) -> h e o", e=E),
                  dw_ap[:, :, oc * 128:(oc + 1) * 128].rearrange("e h o -> h e o"),
                  reads=[dwg], writes=[gdw])
            for e_i in range(E):
                r.op("tensor", lambda e, e_i=e_i, dwe=dwe, dbank=dbank: e.matmul(
                    C.bank(dbank), dwe[:, e_i * 128:(e_i + 1) * 128],
                    hslab[:, e_i * 512:(e_i + 1) * 512],
                    start=(e_i == 0), stop=False),
                    reads=[gh, gdw], writes=[C.bankg(dbank)])
            for hc in range(4):
                r.op("tensor", lambda e, hc=hc, oc=oc, dbank=dbank: e.matmul(
                    C.bank(dbank), sdw[:, hc * D + oc * 128: hc * D + (oc + 1) * 128],
                    zsh[:, hc * 512:(hc + 1) * 512],
                    start=False, stop=(hc == 3)),
                    reads=[gzsh, gsdw], writes=[C.bankg(dbank)])
            r.dma("sync", xch[:], x_hbm[oc * 128:(oc + 1) * 128, tsl], reads=[xg], writes=[gxc])
            r.dma("sync", sch[:], s_hbm[oc * 128:(oc + 1) * 128, tsl], reads=[sg_in], writes=[gsc])
            r.op("scalar", lambda e, out_ev=out_ev, dbank=dbank: e.activation(
                out=out_ev[:], in_=C.bank(dbank), func=AF.Silu),
                reads=[C.bankg(dbank)], writes=[goev])
            r.op("vector", lambda e, out_ev=out_ev, sch=sch: e.tensor_mul(out=out_ev[:], in0=out_ev[:], in1=sch[:]),
                 reads=[gsc], writes=[goev])
            r.op("vector", lambda e, out_ev=out_ev, xch=xch: e.tensor_add(out=out_ev[:], in0=out_ev[:], in1=xch[:]),
                 reads=[gxc], writes=[goev])
            r.dma("sync", x1_hbm[oc * 128:(oc + 1) * 128, tsl], out_ev[:],
                  reads=[goev], writes=[x1g])


def emit_l1qkv(C, xn_hbm, xng, w_ap, wg_in, b_ap, bg_in, qkv_hbm, qkvg):
    r, T = C.r, C.T
    load_slab(C, "l1q_xn", xn_hbm, xng, DC, T)
    load_w(C, "l1q_w", w_ap, wg_in, DC, 3072)
    bb = C.sb("l1q_b", [128, 24], F32)
    r.dma("sync", bb[:], b_ap, reads=[bg_in], writes=[C.g("l1q_b")])
    bg = C.g("l1q_b")
    evs = rbufs(C, "l1q_ev", 4, [128, 512], BF16)
    cnt = [0]

    def epi(oc, tg, bank, bankg):
        i = cnt[0]; cnt[0] += 1
        ev, gev = evs[i % 4]
        evac(C, i, bank, bankg, ev[:], gev, bias=bb[:, oc:oc + 1], biasg=bg)
        r.dma("sync", qkv_hbm[oc * 128:(oc + 1) * 128, tg * 512:(tg + 1) * 512], ev[:],
              reads=[gev], writes=[qkvg])
    emit_proj(C, "l1q_xn", "l1q_w", DC, 3072, epi, banks=(0, 1, 2, 3))


def emit_l1mix(C, qkv_hbm, qkvg, all_hbm, allg, s1_hbm, s1g):
    r, T = C.r, C.T
    q2 = C.sb("l1m_q", [128, T], BF16)
    k2s = rbufs(C, "l1m_k", 2, [128, 8 * 512], BF16)
    v2s = rbufs(C, "l1m_v", 2, [128, 8 * 512], BF16)
    Et = C.sb("l1m_e", [128, 512], BF16)
    scb = C.sb("l1m_sc", [2, 512], F32)
    pb = C.sb("l1m_pb", [128, 512], F32)
    czs = rbufs(C, "l1m_cz", 2, [128, 512], F32)
    pv = C.sb("l1m_pv", [128, 512], F32)
    gq, gE, gsc, gpb, gpv = (C.g("l1m_q"), C.g("l1m_e"), C.g("l1m_sc"),
                             C.g("l1m_pb"), C.g("l1m_pv"))
    seg = C.consts["seg_lhsT"]
    gseg = C.g("k_seg_lhsT")
    indf = C.consts["ind_f"]
    gind = C.g("k_ind_f")
    cnorm = (1.0 / 8.0) * (8.0 ** -0.5)
    for hp in range(8):
        base = 384 * hp
        r.dma("sync", q2[0:64, :], qkv_hbm[base:base + 64, :], reads=[qkvg], writes=[gq])
        r.dma("sync", q2[64:128, :], qkv_hbm[base + 192:base + 256, :], reads=[qkvg], writes=[gq])
        for tg in range(C.NTG):
            tsl = slice(tg * 512, (tg + 1) * 512)
            kb, gk = k2s[tg % 2]
            vb, gv = v2s[tg % 2]
            cz, gcz = czs[tg % 2]
            r.dma("sync", kb[0:64, :].rearrange("p (t c) -> p t c", t=8),
                  all_hbm[:, base + 64:base + 128, tsl].rearrange("t p c -> p t c"),
                  reads=[allg], writes=[gk])
            r.dma("sync", kb[64:128, :].rearrange("p (t c) -> p t c", t=8),
                  all_hbm[:, base + 256:base + 320, tsl].rearrange("t p c -> p t c"),
                  reads=[allg], writes=[gk])
            r.dma("sync", vb[0:64, :].rearrange("p (t c) -> p t c", t=8),
                  all_hbm[:, base + 128:base + 192, tsl].rearrange("t p c -> p t c"),
                  reads=[allg], writes=[gv])
            r.dma("sync", vb[64:128, :].rearrange("p (t c) -> p t c", t=8),
                  all_hbm[:, base + 320:base + 384, tsl].rearrange("t p c -> p t c"),
                  reads=[allg], writes=[gv])
            for t in range(8):
                ksl = slice(t * 512, (t + 1) * 512)
                r.op("vector", lambda e, tsl=tsl, kb=kb, ksl=ksl: e.tensor_mul(
                    out=Et[:], in0=q2[:, tsl], in1=kb[:, ksl]),
                    reads=[gq, gk], writes=[gE])
                r.op("tensor", lambda e: e.matmul(
                    C.bank(0)[0:2, :], seg[:], Et[:], start=True, stop=True),
                    reads=[gE, gseg], writes=[C.bankg(0)])
                if t == 0:
                    r.op("vector", lambda e: e.tensor_scalar_mul(out=scb[:], in0=C.bank(0)[0:2, :], scalar1=cnorm),
                         reads=[C.bankg(0)], writes=[gsc])
                else:
                    r.op("vector", lambda e: e.scalar_tensor_tensor(
                        out=scb[:], in0=C.bank(0)[0:2, :], scalar=cnorm, in1=scb[:],
                        op0=ALU.mult, op1=ALU.add),
                        reads=[C.bankg(0)], writes=[gsc])
                r.op("tensor", lambda e: e.matmul(C.bank(1), indf[:], scb[:], start=True, stop=True),
                     reads=[gsc, gind], writes=[C.bankg(1)])
                r.op("vector", lambda e: e.tensor_copy(out=pb[:], in_=C.bank(1)),
                     reads=[C.bankg(1)], writes=[gpb])
                if t == 0:
                    r.op("vector", lambda e, vb=vb, ksl=ksl, cz=cz: e.tensor_mul(
                        out=cz[:], in0=pb[:], in1=vb[:, ksl]),
                        reads=[gpb, gv], writes=[gcz])
                else:
                    r.op("vector", lambda e, vb=vb, ksl=ksl: e.tensor_mul(
                        out=pv[:], in0=pb[:], in1=vb[:, ksl]),
                        reads=[gpb, gv], writes=[gpv])
                    r.op("vector", lambda e, cz=cz: e.tensor_add(out=cz[:], in0=cz[:], in1=pv[:]),
                         reads=[gpv], writes=[gcz])
            r.dma("sync", s1_hbm[hp * 128:(hp + 1) * 128, tsl], cz[:],
                  reads=[gcz], writes=[s1g])


NODE_SPEC = {
    0: {"edges": [(0, -1)], "q": 0, "k": 0, "v": 0},
    1: {"edges": [(2, 0), (1, -1)], "q": 2, "k": 1, "v": 2},
    2: {"edges": [(5, 1), (3, -1), (4, 0)], "q": 5, "k": 3, "v": 4},
    3: {"edges": [(9, 2), (6, -1), (8, 1)], "q": 9, "k": 6, "v": 8},
    4: {"edges": [(14, 3), (10, -1), (12, 1)], "q": 14, "k": 10, "v": 12},
    5: {"edges": [(19, 4), (15, 0), (17, 2)], "q": 19, "k": 15, "v": 17},
}
LAM = 4.5


def emit_l1f_prep(C, ew_ap, ewg, actw_ap, actwg):
    r = C.r
    row = C.sb("l1f_row", [1, 64], F32)
    gr = C.g("l1f_row")
    r.op("vector", lambda e: e.memset(row[:], 0.0), writes=[gr])
    r.dma("sync", row[0:1, 0:18], ew_ap, reads=[ewg], writes=[gr])
    r.dma("sync", row[0:1, 18:24], actw_ap, reads=[actwg], writes=[gr])
    cols = C.sb("l1f_sc", [128, 64], F32)
    gc = C.g("l1f_sc")
    r.op("tensor", lambda e: e.matmul(C.bank(7)[:, 0:64], C.consts["ones_row_f"][:], row[:],
                                      start=True, stop=True),
         reads=[gr, C.g("k_ones_row_f")], writes=[C.bankg(7)])
    r.op("vector", lambda e: e.tensor_copy(out=cols[:], in_=C.bank(7)[:, 0:64]),
         reads=[C.bankg(7)], writes=[gc])
    return cols, gc


def _edge_proj(C, in_hbm, ing, we_ap, weg, e_i, be_ap, beg, out_hbm, outg):
    r, T = C.r, C.T
    load_slab(C, "l1n_act", in_hbm, ing, DC, T)
    load_w(C, "l1n_w", we_ap[e_i], weg, DC, D)
    bb = C.sb("l1n_be", [128, 8], F32)
    r.dma("sync", bb[:], be_ap[e_i], reads=[beg], writes=[C.g("l1n_be")])
    bg = C.g("l1n_be")
    evs = rbufs(C, "l1n_ev", 4, [128, 512], BF16)
    cnt = [0]

    def epi(oc, tg, bank, bankg):
        i = cnt[0]; cnt[0] += 1
        ev, gev = evs[i % 4]
        evac(C, i, bank, bankg, ev[:], gev, bias=bb[:, oc:oc + 1], biasg=bg)
        r.dma("sync", out_hbm[oc * 128:(oc + 1) * 128, tg * 512:(tg + 1) * 512], ev[:],
              reads=[gev], writes=[outg])
    emit_proj(C, "l1n_act", "l1n_w", DC, D, epi, banks=(0, 1, 2, 3))


def _head_proj(C, in_hbm, ing, w_ap, wg_, scol, scolg, out_hbm, outg):
    r, T = C.r, C.T
    load_slab(C, "l1n_act", in_hbm, ing, DC, T)
    load_w(C, "l1n_w", w_ap, wg_, DC, D)
    evs = rbufs(C, "l1n_ev", 4, [128, 512], BF16)
    cnt = [0]

    def epi(oc, tg, bank, bankg):
        i = cnt[0]; cnt[0] += 1
        ev, gev = evs[i % 4]
        evac(C, i, bank, bankg, ev[:], gev, scale=scol, scaleg=scolg)
        r.dma("sync", out_hbm[oc * 128:(oc + 1) * 128, tg * 512:(tg + 1) * 512], ev[:],
              reads=[gev], writes=[outg])
    emit_proj(C, "l1n_act", "l1n_w", DC, D, epi, banks=(0, 1, 2, 3))


def _vh_proj(C, in_hbm, ing, w_ap, wg_, scol, scolg, out_hbm, outg):
    r, T = C.r, C.T
    slab = C._sb["l1n_act"]
    sg_ = C.g("l1n_act")
    load_w(C, "l1n_w", w_ap, wg_, DC, D)
    wg2 = C.g("l1n_w")
    evs = rbufs(C, "l1n_ev", 4, [128, 512], BF16)
    wt = C._sb["l1n_w"]
    i = 0
    for tt in range(T // 128):
        for og in range(2):
            b = i % 4
            for ic in range(DC):
                lh = slab[:, ic * T + tt * 128: ic * T + tt * 128 + 128]
                rh = wt[:, ic * D + og * 512: ic * D + (og + 1) * 512]
                r.op("tensor", lambda e, lh=lh, rh=rh, ic=ic, b=b: e.matmul(
                    C.bank(b), lh, rh,
                    start=(ic == 0), stop=(ic == DC - 1)),
                    reads=[sg_, wg2], writes=[C.bankg(b)])
            ev, gev = evs[i % 4]
            evac(C, i, C.bank(b), C.bankg(b), ev[:], gev, scale=scol, scaleg=scolg)
            r.dma("sync", out_hbm[tt * 128:(tt + 1) * 128, og * 512:(og + 1) * 512], ev[:],
                  reads=[gev], writes=[outg])
            i += 1


def emit_l1ffn_node(C, c, node_hbm, we_ap, weg, be_ap, beg,
                    wq_ap, wqg, wk_ap, wkg, wv_ap, wvg, wo_ap, wog,
                    ewcols, ewg, out_hbm, outg):
    r, T = C.r, C.T
    spec = NODE_SPEC[c]
    C.push()
    sc_p = {}
    for e_i, n_i in spec["edges"]:
        ap, g_in = node_hbm[n_i]
        name = "sc_p%d" % e_i
        p_ap, p_g = C.dram(name, [D, T], BF16)
        _edge_proj(C, ap, g_in, we_ap, weg, e_i, be_ap, beg, p_ap, p_g)
        sc_p[e_i] = (p_ap, p_g)
    qh_ap, qh_g = C.dram("sc_qh", [D, T], BF16)
    kh_ap, kh_g = C.dram("sc_kh", [D, T], BF16)
    vh_ap, vh_g = C.dram("sc_vh", [T, D], BF16)
    pq_ap, pq_g = sc_p[spec["q"]]
    _head_proj(C, pq_ap, pq_g, wq_ap[c], wqg, ewcols[:, 3 * c:3 * c + 1], ewg, qh_ap, qh_g)
    pk_ap, pk_g = sc_p[spec["k"]]
    _head_proj(C, pk_ap, pk_g, wk_ap[c], wkg, ewcols[:, 3 * c + 1:3 * c + 2], ewg, kh_ap, kh_g)
    pv_ap, pv_g = sc_p[spec["v"]]
    load_slab(C, "l1n_act", pv_ap, pv_g, DC, T)
    _vh_proj(C, pv_ap, pv_g, wv_ap[c], wvg, ewcols[:, 3 * c + 2:3 * c + 3], ewg, vh_ap, vh_g)
    C.pop()

    NKC = T // 128
    ctxs = C.sb("l1a_ctx", [128, DC * T], BF16)
    C.push()
    kh_ss = rbufs(C, "l1a_k", 2, [128, T], BF16)
    qh_s = C.sb("l1a_q", [128, 2 * 512], BF16)
    vh_ss = rbufs(C, "l1a_v", 2, [128, NKC * 128], BF16)
    psl = C.sb("l1a_p", [128, NKC * 512], BF16)
    Rw = C.sb("l1a_R", [128, 512], F32)
    c0 = C.sb("l1n_oev", [128, 512], F32)
    sqs = C.sb("l1a_sq", [128, 512], F32)
    strow = C.sb("l1a_st", [1, 8], F32)
    bcol = C.sb("l1a_b", [128, 1], F32)
    rrow = C.sb("l1a_rr", [1, 512], F32)
    rb = C.sb("l1a_rb", [128, 512], F32)
    gq, gp, gR, gc0, gsq, gst, gb, grr, grb, gctx = (
        C.g("l1a_q"), C.g("l1a_p"), C.g("l1a_R"),
        C.g("l1n_oev"), C.g("l1a_sq"), C.g("l1a_st"), C.g("l1a_b"), C.g("l1a_rr"),
        C.g("l1a_rb"), C.g("l1a_ctx"))
    ones_f = C.consts["ones_col_f"]
    ones_row_f = C.consts["ones_row_f"]
    g1c, g1r = C.g("k_ones_col_f"), C.g("k_ones_row_f")
    iscale = 128.0 ** -0.5
    inv_n = 1.0 / (128 * 512)
    for h in range(DC):
        hsl = slice(h * 128, (h + 1) * 128)
        kh_s, gk = kh_ss[h % 2]
        vh_s, gv = vh_ss[h % 2]
        r.dma("sync", kh_s[:], kh_ap[hsl, :], reads=[kh_g], writes=[gk])
        r.dma("sync", qh_s[:, 0:512], qh_ap[hsl, 0:512], reads=[qh_g], writes=[gq])
        r.dma("sync", vh_s[:].rearrange("p (kc d) -> p kc d", kc=NKC),
              vh_ap[:, hsl].rearrange("(kc p) d -> p kc d", p=128),
              reads=[vh_g], writes=[gv])
        for x in range(1):
            r.op("tensor", lambda e, kh_s=kh_s: e.matmul(
                C.bank(5), kh_s[:, 0:128], qh_s[:, 0:512], start=True, stop=True),
                reads=[gk, gq], writes=[C.bankg(5)])
            r.op("vector", lambda e: e.tensor_copy(out=c0[:], in_=C.bank(5)),
                 reads=[C.bankg(5)], writes=[gc0])
            r.op("vector", lambda e: e.tensor_mul(out=sqs[:], in0=c0[:], in1=c0[:]),
                 reads=[gc0], writes=[gsq])
            r.op("tensor", lambda e: e.matmul(C.bank(6)[0:1, :], ones_f[:], c0[:],
                                              start=True, stop=True),
                 reads=[gc0, g1c], writes=[C.bankg(6)])
            r.op("tensor", lambda e: e.matmul(C.bank(7)[0:1, :], ones_f[:], sqs[:],
                                              start=True, stop=True),
                 reads=[gsq, g1c], writes=[C.bankg(7)])
            r.op("vector", lambda e: e.reduce_sum(out=strow[0:1, 0:1], in_=C.bank(6)[0:1, :], axis=AX.X),
                 reads=[C.bankg(6)], writes=[gst])
            r.op("vector", lambda e: e.reduce_sum(out=strow[0:1, 1:2], in_=C.bank(7)[0:1, :], axis=AX.X),
                 reads=[C.bankg(7)], writes=[gst])
            kv = C.consts["kvals"]
            kvg = C.g("k_kvals")
            r.op("scalar", lambda e: e.mul(out=strow[0:1, 2:3], in_=strow[0:1, 0:1], mul=inv_n),
                 reads=[gst], writes=[gst])
            r.op("scalar", lambda e: e.mul(out=strow[0:1, 3:4], in_=strow[0:1, 1:2], mul=inv_n),
                 reads=[gst], writes=[gst])
            r.op("scalar", lambda e: e.activation(out=strow[0:1, 4:5], in_=strow[0:1, 2:3], func=AF.Square),
                 reads=[gst], writes=[gst])
            r.op("scalar", lambda e: e.mul(out=strow[0:1, 4:5], in_=strow[0:1, 4:5], mul=-1.0),
                 reads=[gst], writes=[gst])
            r.op("scalar", lambda e: e.activation(out=strow[0:1, 5:6], in_=strow[0:1, 3:4],
                                                  func=AF.Identity, bias=strow[0:1, 4:5]),
                 reads=[gst], writes=[gst])
            r.op("scalar", lambda e, kv=kv: e.activation(out=strow[0:1, 5:6], in_=strow[0:1, 5:6],
                                                         func=AF.Ln, bias=kv[0:1, 0:1]),
                 reads=[gst, kvg], writes=[gst])
            r.op("scalar", lambda e: e.activation(out=strow[0:1, 5:6], in_=strow[0:1, 5:6],
                                                  func=AF.Exp, scale=0.5),
                 reads=[gst], writes=[gst])
            r.op("scalar", lambda e: e.mul(out=strow[0:1, 6:7], in_=strow[0:1, 5:6], mul=LAM),
                 reads=[gst], writes=[gst])
            r.op("scalar", lambda e: e.activation(out=strow[0:1, 3:4], in_=strow[0:1, 2:3],
                                                  func=AF.Identity, bias=strow[0:1, 6:7]),
                 reads=[gst], writes=[gst])
            r.op("scalar", lambda e: e.mul(out=strow[0:1, 7:8], in_=strow[0:1, 3:4], mul=-iscale),
                 reads=[gst], writes=[gst])
            r.op("tensor", lambda e: e.matmul(C.bank(5)[:, 0:1], ones_row_f[:], strow[0:1, 7:8],
                                              start=True, stop=True),
                 reads=[gst, g1r], writes=[C.bankg(5)])
            r.op("vector", lambda e: e.tensor_copy(out=bcol[:], in_=C.bank(5)[:, 0:1]),
                 reads=[C.bankg(5)], writes=[gb])
        for qg in range(C.NTG):
            qpar = qg % 2
            qbuf = qh_s[:, qpar * 512:(qpar + 1) * 512]
            if qg > 0:
                r.dma("sync", qbuf, qh_ap[hsl, qg * 512:(qg + 1) * 512],
                      reads=[qh_g], writes=[gq])
            for kc in range(NKC):
                b = kc % 2
                r.op("tensor", lambda e, kc=kc, b=b, qbuf=qbuf, kh_s=kh_s: e.matmul(
                    C.bank(b), kh_s[:, kc * 128:(kc + 1) * 128], qbuf,
                    start=True, stop=True),
                    reads=[gk, gq], writes=[C.bankg(b)])
                r.op("scalar", lambda e, kc=kc, b=b: e.activation(
                    out=psl[:, kc * 512:(kc + 1) * 512], in_=C.bank(b), func=AF.Exp,
                    bias=bcol[:, 0:1], scale=iscale),
                    reads=[C.bankg(b), gb], writes=[gp])
                if kc == 0:
                    r.op("vector", lambda e, kc=kc: e.tensor_scalar_mul(
                        out=Rw[:], in0=psl[:, kc * 512:(kc + 1) * 512], scalar1=1.0),
                        reads=[gp], writes=[gR])
                else:
                    r.op("vector", lambda e, kc=kc: e.scalar_tensor_tensor(
                        out=Rw[:], in0=psl[:, kc * 512:(kc + 1) * 512], scalar=1.0,
                        in1=Rw[:], op0=ALU.mult, op1=ALU.add),
                        reads=[gp], writes=[gR])
            for kc in range(NKC):
                r.op("tensor", lambda e, kc=kc, vh_s=vh_s: e.matmul(
                    C.bank(2), vh_s[:, kc * 128:(kc + 1) * 128],
                    psl[:, kc * 512:(kc + 1) * 512],
                    start=(kc == 0), stop=(kc == NKC - 1)),
                    reads=[gv, gp], writes=[C.bankg(2)])
            r.op("tensor", lambda e: e.matmul(C.bank(3)[0:1, :], ones_f[:], Rw[:],
                                              start=True, stop=True),
                 reads=[gR, g1c], writes=[C.bankg(3)])
            r.op("vector", lambda e: e.reciprocal(out=rrow[:], in_=C.bank(3)[0:1, :]),
                 reads=[C.bankg(3)], writes=[grr])
            r.op("tensor", lambda e: e.matmul(C.bank(4), ones_row_f[:], rrow[:],
                                              start=True, stop=True),
                 reads=[grr, g1r], writes=[C.bankg(4)])
            r.op("vector", lambda e: e.tensor_copy(out=rb[:], in_=C.bank(4)),
                 reads=[C.bankg(4)], writes=[grb])
            r.op("vector", lambda e, h=h, qg=qg: e.tensor_mul(
                out=ctxs[:, h * T + qg * 512: h * T + qg * 512 + 512],
                in0=C.bank(2), in1=rb[:]),
                reads=[C.bankg(2), grb], writes=[gctx])

    C.pop()
    C.push()
    load_w(C, "l1n_w", wo_ap[c], wog, DC, D)
    pqts = rbufs(C, "l1n_pqt", 4, [128, 512], BF16)
    oevs = rbufs(C, "l1n_oev", 4, [128, 512], F32)
    nevs = rbufs(C, "l1n_nev", 4, [128, 512], BF16)
    cnt = [0]

    def epi(oc, tg, bank, bankg):
        i = cnt[0]; cnt[0] += 1
        pqt, gpqt = pqts[i % 4]
        oev, goev = oevs[i % 4]
        nev, gnev = nevs[i % 4]
        tsl = slice(tg * 512, (tg + 1) * 512)
        r.dma("sync", pqt[:], pq_ap[oc * 128:(oc + 1) * 128, tsl], reads=[pq_g], writes=[gpqt])
        evac(C, i, bank, bankg, oev[:], goev,
             scale=ewcols[:, 18 + c:19 + c], scaleg=ewg)
        r.op("vector", lambda e, c=c, pqt=pqt, oev=oev, nev=nev: e.scalar_tensor_tensor(
            out=nev[:], in0=pqt[:], scalar=ewcols[:, 3 * c:3 * c + 1], in1=oev[:],
            op0=ALU.mult, op1=ALU.add),
            reads=[gpqt, ewg, goev], writes=[gnev])
        r.dma("sync", out_hbm[oc * 128:(oc + 1) * 128, tsl], nev[:],
              reads=[gnev], writes=[outg])
    emit_proj(C, "l1a_ctx", "l1n_w", DC, D, epi, banks=(0, 1, 2, 3))
    C.pop()


def emit_final(C, n5_hbm, n5g, lng_ap, lngg, lnb_ap, lnbg, s1_hbm, s1g,
               x1_hbm, x1g, out_hbm, outg):
    r, T = C.r, C.T
    ln5_ap, ln5g = C.dram("sc_ln5", [D, T], F32)
    emit_ln(C, n5_hbm, n5g, ln5_ap, ln5g, eps=1e-6, gamma_beta=(lng_ap, lnb_ap),
            tag="ln5", in_dtype=BF16, out_dtype=F32)
    lch = C.sb("fin_l", [128, 512], F32)
    sch = C.sb("fin_s", [128, 512], F32)
    xch = C.sb("fin_x", [128, 512], F32)
    glc, gsc, gxc = C.g("fin_l"), C.g("fin_s"), C.g("fin_x")
    for tg in range(C.NTG):
        tsl = slice(tg * 512, (tg + 1) * 512)
        for oc in range(DC):
            csl = slice(oc * 128, (oc + 1) * 128)
            r.dma("sync", lch[:], ln5_ap[csl, tsl], reads=[ln5g], writes=[glc])
            r.dma("sync", sch[:], s1_hbm[csl, tsl], reads=[s1g], writes=[gsc])
            r.dma("sync", xch[:], x1_hbm[csl, tsl], reads=[x1g], writes=[gxc])
            r.op("scalar", lambda e: e.activation(out=lch[:], in_=lch[:], func=AF.Silu),
                 reads=[glc], writes=[glc])
            r.op("vector", lambda e: e.tensor_mul(out=lch[:], in0=lch[:], in1=sch[:]),
                 reads=[gsc], writes=[glc])
            r.op("vector", lambda e: e.tensor_add(out=lch[:], in0=lch[:], in1=xch[:]),
                 reads=[gxc], writes=[glc])
            r.dma("sync", out_hbm[csl, tsl], lch[:], reads=[glc], writes=[outg])



def _new(T, consts=None):
    nc = bass.Bass("TRN2", target_bir_lowering=False, debug=False,
                   detect_race_conditions=False)
    stack = ExitStack()
    stack.__enter__()
    C = Ctx(nc, stack, T=T)
    declare_consts(C, only=consts)
    return nc, stack, C


def _ein(C, name, shape, dtype):
    ap = C.nc.dram_tensor(name, shape, dtype, kind="ExternalInput").ap()
    return ap, G(name)


def _eout(C, name, shape, dtype):
    ap = C.nc.dram_tensor(name, shape, dtype, kind="ExternalOutput").ap()
    return ap, G(name)


def build_ln0(T):
    nc, stack, C = _new(T)
    x, xg = _ein(C, "xT", [D, T], F32)
    xn, xng = _eout(C, "xn0T", [D, T], BF16)
    C.push(); emit_ln(C, x, xg, xn, xng, eps=1e-5, tag="ln0"); C.pop()
    C.finish(); stack.__exit__(None, None, None)
    return nc, ["xn0T"]


def build_l0attn(T):
    nc, stack, C = _new(T)
    xn, xng = _ein(C, "xn0T", [D, T], BF16)
    wq, wqg = _ein(C, "w_q", [D, D], F32)
    wk, wkg = _ein(C, "w_k", [D, D], F32)
    wv, wvg = _ein(C, "w_v", [D, D], F32)
    b0, b0g = _ein(C, "b_qkv0", [128, 24], F32)
    s, sg_ = _eout(C, "s0T", [D, T], F32)
    C.push(); emit_l0attn(C, xn, xng, s, sg_, wq, wqg, wk, wkg, wv, wvg, b0, b0g); C.pop()
    C.finish(); stack.__exit__(None, None, None)
    return nc, ["s0T"]


def build_l0ffn(T):
    nc, stack, C = _new(T)
    xn, xng = _ein(C, "xn0T", [D, T], BF16)
    x, xg = _ein(C, "xT", [D, T], F32)
    s, sg_ = _ein(C, "s0T", [D, T], F32)
    rw, rwg = _ein(C, "router_wT", [D, 64], F32)
    rb, rbg = _ein(C, "router_b", [1, 64], F32)
    gw, gwg = _ein(C, "gate_wT", [32, D, 128], F32)
    uw, uwg = _ein(C, "up_wT", [32, D, 128], F32)
    dw, dwg = _ein(C, "down_wT", [32, 128, D], F32)
    sga, sgag = _ein(C, "sgate_wT", [D, 512], F32)
    sua, suag = _ein(C, "sup_wT", [D, 512], F32)
    sda, sdag = _ein(C, "sdown_wT", [512, D], F32)
    x1, x1g = _eout(C, "x1T", [D, T], F32)
    C.push()
    emit_l0ffn(C, xn, xng, x, xg, s, sg_, x1, x1g, rw, rwg, rb, rbg,
               gw, gwg, uw, uwg, dw, dwg, sga, sgag, sua, suag, sda, sdag)
    C.pop(); C.finish(); stack.__exit__(None, None, None)
    return nc, ["x1T"]


def build_ln1(T):
    nc, stack, C = _new(T)
    x, xg = _ein(C, "x1T", [D, T], F32)
    xn, xng = _eout(C, "xn1T", [D, T], BF16)
    C.push(); emit_ln(C, x, xg, xn, xng, eps=1e-5, tag="ln1"); C.pop()
    C.finish(); stack.__exit__(None, None, None)
    return nc, ["xn1T"]


def build_l1qkv(T):
    nc, stack, C = _new(T)
    xn, xng = _ein(C, "xn1T", [D, T], BF16)
    w, wg_ = _ein(C, "wqkv1T", [D, 3072], F32)
    b, bg_ = _ein(C, "b_qkv1", [128, 24], F32)
    qkv, qkvg = _eout(C, "qkv1T", [3072, T], BF16)
    C.push(); emit_l1qkv(C, xn, xng, w, wg_, b, bg_, qkv, qkvg); C.pop()
    C.finish(); stack.__exit__(None, None, None)
    return nc, ["qkv1T"]


def build_l1mix(T):
    nc, stack, C = _new(T)
    qkv, qkvg = _ein(C, "qkv1T", [3072, T], BF16)
    alla, allg = _ein(C, "qkv_all", [8, 3072, T], BF16)
    s1, s1g = _eout(C, "s1T", [D, T], F32)
    C.push(); emit_l1mix(C, qkv, qkvg, alla, allg, s1, s1g); C.pop()
    C.finish(); stack.__exit__(None, None, None)
    return nc, ["s1T"]


def build_node(T, c):
    nc, stack, C = _new(T, consts={"ones_col_f", "ones_row_f", "kvals"})
    spec = NODE_SPEC[c]
    needed = sorted({n_i for (_, n_i) in spec["edges"]})
    node_hbm = {}
    for n_i in needed:
        nm = "xn1T" if n_i == -1 else "n%dT" % n_i
        ap, g_ = _ein(C, nm, [D, T], BF16)
        node_hbm[n_i] = (ap, g_)
    we, weg = _ein(C, "l1f_We", [20, D, D], F32)
    be, beg = _ein(C, "l1f_be", [20, 128, 8], F32)
    wq, wqg = _ein(C, "l1f_Wq", [6, D, D], F32)
    wk, wkg = _ein(C, "l1f_Wk", [6, D, D], F32)
    wv, wvg = _ein(C, "l1f_Wv", [6, D, D], F32)
    wo, wog = _ein(C, "l1f_Wo", [6, D, D], F32)
    ew, ewg_ = _ein(C, "l1f_ew", [1, 18], F32)
    aw, awg = _ein(C, "l1f_actw", [1, 6], F32)
    out, outg = _eout(C, "n%dT" % c, [D, T], BF16)
    C.push()
    ewcols, ewg = emit_l1f_prep(C, ew, ewg_, aw, awg)
    emit_l1ffn_node(C, c, node_hbm, we, weg, be, beg, wq, wqg, wk, wkg,
                    wv, wvg, wo, wog, ewcols, ewg, out, outg)
    C.pop(); C.finish(); stack.__exit__(None, None, None)
    return nc, ["n%dT" % c]


def build_final(T):
    nc, stack, C = _new(T)
    n5, n5g = _ein(C, "n5T", [D, T], BF16)
    lng, lngg = _ein(C, "l1f_lng", [128, 8], F32)
    lnb, lnbg = _ein(C, "l1f_lnb", [128, 8], F32)
    s1, s1g = _ein(C, "s1T", [D, T], F32)
    x1, x1g = _ein(C, "x1T", [D, T], F32)
    out, outg = _eout(C, "outT", [D, T], F32)
    C.push()
    emit_final(C, n5, n5g, lng, lngg, lnb, lnbg, s1, s1g, x1, x1g, out, outg)
    C.pop(); C.finish(); stack.__exit__(None, None, None)
    return nc, ["outT"]


NODE_INPUT_NAMES = {
    c: sorted({"xn1T" if n_i == -1 else "n%dT" % n_i for (_, n_i) in NODE_SPEC[c]["edges"]})
    for c in range(6)
}

PHASES = [
    ("ln0", build_ln0, ["xT"], ["xn0T"]),
    ("l0attn", build_l0attn, ["xn0T", "w_q", "w_k", "w_v", "b_qkv0"], ["s0T"]),
    ("l0ffn", build_l0ffn, ["xn0T", "xT", "s0T", "router_wT", "router_b", "gate_wT",
                            "up_wT", "down_wT", "sgate_wT", "sup_wT", "sdown_wT"], ["x1T"]),
    ("ln1", build_ln1, ["x1T"], ["xn1T"]),
    ("l1qkv", build_l1qkv, ["xn1T", "wqkv1T", "b_qkv1"], ["qkv1T"]),
    ("l1mix", build_l1mix, ["qkv1T", "qkv_all"], ["s1T"]),
] + [
    ("node%d" % c, (lambda T, c=c: build_node(T, c)),
     NODE_INPUT_NAMES[c] + ["l1f_We", "l1f_be", "l1f_Wq", "l1f_Wk", "l1f_Wv",
                            "l1f_Wo", "l1f_ew", "l1f_actw"], ["n%dT" % c])
    for c in range(6)
] + [
    ("final", build_final, ["n5T", "l1f_lng", "l1f_lnb", "s1T", "x1T"], ["outT"]),
]


def build_fused(T):
    nc, stack, C = _new(T)
    r = C.r
    x, xg = _ein(C, "xT", [D, T], F32)
    wq, wqg = _ein(C, "w_q", [D, D], F32)
    wk, wkg = _ein(C, "w_k", [D, D], F32)
    wv, wvg = _ein(C, "w_v", [D, D], F32)
    b0, b0g = _ein(C, "b_qkv0", [128, 24], F32)
    rw, rwg = _ein(C, "router_wT", [D, 64], F32)
    rb, rbg = _ein(C, "router_b", [1, 64], F32)
    gw, gwg = _ein(C, "gate_wT", [32, D, 128], F32)
    uw, uwg = _ein(C, "up_wT", [32, D, 128], F32)
    dw, dwg = _ein(C, "down_wT", [32, 128, D], F32)
    sga, sgag = _ein(C, "sgate_wT", [D, 512], F32)
    sua, suag = _ein(C, "sup_wT", [D, 512], F32)
    sda, sdag = _ein(C, "sdown_wT", [512, D], F32)
    wqkv1, wqkv1g = _ein(C, "wqkv1T", [D, 3072], F32)
    b1, b1g = _ein(C, "b_qkv1", [128, 24], F32)
    we, weg = _ein(C, "l1f_We", [20, D, D], F32)
    be, beg = _ein(C, "l1f_be", [20, 128, 8], F32)
    wq6, wq6g = _ein(C, "l1f_Wq", [6, D, D], F32)
    wk6, wk6g = _ein(C, "l1f_Wk", [6, D, D], F32)
    wv6, wv6g = _ein(C, "l1f_Wv", [6, D, D], F32)
    wo6, wo6g = _ein(C, "l1f_Wo", [6, D, D], F32)
    ew, ewg_ = _ein(C, "l1f_ew", [1, 18], F32)
    aw, awg = _ein(C, "l1f_actw", [1, 6], F32)
    lng, lngg = _ein(C, "l1f_lng", [128, 8], F32)
    lnb, lnbg = _ein(C, "l1f_lnb", [128, 8], F32)
    out, outg = _eout(C, "outT", [D, T], F32)
    xn0, xn0g = C.dram("i_xn0T", [D, T], BF16)
    s0, s0g = C.dram("i_s0T", [D, T], F32)
    x1, x1g = C.dram("i_x1T", [D, T], F32)
    xn1, xn1g = C.dram("i_xn1T", [D, T], BF16)
    qkv_own, qkv_own_g = C.dram("i_qkvT", [3072, T], BF16)
    qkv_all, qkv_all_g = C.dram("i_qkv_all", [8 * 3072, T], BF16)
    s1, s1g = C.dram("i_s1T", [D, T], F32)
    node_hbm = {-1: (xn1, xn1g)}
    for c in range(6):
        node_hbm[c] = C.dram("i_n%dT" % c, [D, T], BF16)

    C.push(); emit_ln(C, x, xg, xn0, xn0g, eps=1e-5, tag="ln0"); C.pop()
    C.push(); emit_l0attn(C, xn0, xn0g, s0, s0g, wq, wqg, wk, wkg, wv, wvg, b0, b0g); C.pop()
    C.push()
    emit_l0ffn(C, xn0, xn0g, x, xg, s0, s0g, x1, x1g, rw, rwg, rb, rbg,
               gw, gwg, uw, uwg, dw, dwg, sga, sgag, sua, suag, sda, sdag)
    C.pop()
    C.push(); emit_ln(C, x1, x1g, xn1, xn1g, eps=1e-5, tag="ln1"); C.pop()
    C.push(); emit_l1qkv(C, xn1, xn1g, wqkv1, wqkv1g, b1, b1g, qkv_own, qkv_own_g); C.pop()
    r.op("gpsimd", lambda e: e.collective_compute(
        "AllGather", ALU.bypass, [list(range(8))], [qkv_own], [qkv_all]),
        reads=[qkv_own_g], writes=[qkv_all_g], dma=True)
    all3d = qkv_all.rearrange("(t r) c -> t r c", t=8)
    C.push(); emit_l1mix(C, qkv_own, qkv_own_g, all3d, qkv_all_g, s1, s1g); C.pop()
    for c in range(6):
        C.push()
        ewcols, ewcg = emit_l1f_prep(C, ew, ewg_, aw, awg)
        emit_l1ffn_node(C, c, node_hbm, we, weg, be, beg, wq6, wq6g, wk6, wk6g,
                        wv6, wv6g, wo6, wo6g, ewcols, ewcg,
                        node_hbm[c][0], node_hbm[c][1])
        C.pop()
    C.push()
    emit_final(C, node_hbm[5][0], node_hbm[5][1], lng, lngg, lnb, lnbg,
               s1, s1g, x1, x1g, out, outg)
    C.pop()
    C.finish()
    stack.__exit__(None, None, None)
    return nc


def build_fused_a(T):
    nc, stack, C = _new(T)
    x, xg = _ein(C, "xT", [D, T], F32)
    wq, wqg = _ein(C, "w_q", [D, D], F32)
    wk, wkg = _ein(C, "w_k", [D, D], F32)
    wv, wvg = _ein(C, "w_v", [D, D], F32)
    b0, b0g = _ein(C, "b_qkv0", [128, 24], F32)
    rw, rwg = _ein(C, "router_wT", [D, 64], F32)
    rb, rbg = _ein(C, "router_b", [1, 64], F32)
    gw, gwg = _ein(C, "gate_wT", [32, D, 128], F32)
    uw, uwg = _ein(C, "up_wT", [32, D, 128], F32)
    dw, dwg = _ein(C, "down_wT", [32, 128, D], F32)
    sga, sgag = _ein(C, "sgate_wT", [D, 512], F32)
    sua, suag = _ein(C, "sup_wT", [D, 512], F32)
    sda, sdag = _ein(C, "sdown_wT", [512, D], F32)
    wqkv1, wqkv1g = _ein(C, "wqkv1T", [D, 3072], F32)
    b1, b1g = _ein(C, "b_qkv1", [128, 24], F32)
    x1, x1g = _eout(C, "x1T", [D, T], F32)
    xn1, xn1g = _eout(C, "xn1T", [D, T], BF16)
    qkv, qkvg = _eout(C, "qkv1T", [3072, T], BF16)
    xn0, xn0g = C.dram("i_xn0T", [D, T], BF16)
    s0, s0g = C.dram("i_s0T", [D, T], F32)
    C.push(); emit_ln(C, x, xg, xn0, xn0g, eps=1e-5, tag="ln0"); C.pop()
    C.push(); emit_l0attn(C, xn0, xn0g, s0, s0g, wq, wqg, wk, wkg, wv, wvg, b0, b0g); C.pop()
    C.push()
    emit_l0ffn(C, xn0, xn0g, x, xg, s0, s0g, x1, x1g, rw, rwg, rb, rbg,
               gw, gwg, uw, uwg, dw, dwg, sga, sgag, sua, suag, sda, sdag)
    C.pop()
    C.push(); emit_ln(C, x1, x1g, xn1, xn1g, eps=1e-5, tag="ln1"); C.pop()
    C.push(); emit_l1qkv(C, xn1, xn1g, wqkv1, wqkv1g, b1, b1g, qkv, qkvg); C.pop()
    C.finish()
    stack.__exit__(None, None, None)
    return nc


def build_fused_b(T):
    nc, stack, C = _new(T)
    qkv, qkvg = _ein(C, "qkv1T", [3072, T], BF16)
    alla, allg = _ein(C, "qkv_all", [8, 3072, T], BF16)
    xn1, xn1g = _ein(C, "xn1T", [D, T], BF16)
    x1, x1g = _ein(C, "x1T", [D, T], F32)
    we, weg = _ein(C, "l1f_We", [20, D, D], F32)
    be, beg = _ein(C, "l1f_be", [20, 128, 8], F32)
    wq6, wq6g = _ein(C, "l1f_Wq", [6, D, D], F32)
    wk6, wk6g = _ein(C, "l1f_Wk", [6, D, D], F32)
    wv6, wv6g = _ein(C, "l1f_Wv", [6, D, D], F32)
    wo6, wo6g = _ein(C, "l1f_Wo", [6, D, D], F32)
    ew, ewg_ = _ein(C, "l1f_ew", [1, 18], F32)
    aw, awg = _ein(C, "l1f_actw", [1, 6], F32)
    lng, lngg = _ein(C, "l1f_lng", [128, 8], F32)
    lnb, lnbg = _ein(C, "l1f_lnb", [128, 8], F32)
    out, outg = _eout(C, "outT", [D, T], F32)
    s1, s1g = C.dram("i_s1T", [D, T], F32)
    node_hbm = {-1: (xn1, xn1g)}
    for c in range(6):
        node_hbm[c] = C.dram("i_n%dT" % c, [D, T], BF16)
    C.push(); emit_l1mix(C, qkv, qkvg, alla, allg, s1, s1g); C.pop()
    for c in range(6):
        C.push()
        ewcols, ewcg = emit_l1f_prep(C, ew, ewg_, aw, awg)
        emit_l1ffn_node(C, c, node_hbm, we, weg, be, beg, wq6, wq6g, wk6, wk6g,
                        wv6, wv6g, wo6, wo6g, ewcols, ewcg,
                        node_hbm[c][0], node_hbm[c][1])
        C.pop()
    C.push()
    emit_final(C, node_hbm[5][0], node_hbm[5][1], lng, lngg, lnb, lnbg,
               s1, s1g, x1, x1g, out, outg)
    C.pop()
    C.finish()
    stack.__exit__(None, None, None)
    return nc


def kernel(**inputs):
    T = TFULL
    ncore = 8
    consts = host_consts(T)
    per_core = [host_inputs(i, inputs, T) for i in range(ncore)]
    nca = build_fused_a(T)
    in_maps = [dict(consts, **pc) for pc in per_core]
    res = run_bass_kernel_spmd(nca, in_maps, core_ids=list(range(ncore)))
    qkv_all = np.stack([np.asarray(res.results[i]["qkv1T"]) for i in range(ncore)])
    ncb = build_fused_b(T)
    in_maps_b = []
    for i in range(ncore):
        m = dict(consts)
        m.update(per_core[i])
        m["qkv1T"] = qkv_all[i]
        m["qkv_all"] = qkv_all
        m["xn1T"] = np.asarray(res.results[i]["xn1T"])
        m["x1T"] = np.asarray(res.results[i]["x1T"])
        in_maps_b.append(m)
    res_b = run_bass_kernel_spmd(ncb, in_maps_b, core_ids=list(range(ncore)))
    out = np.stack([np.ascontiguousarray(np.asarray(res_b.results[i]["outT"]).T)
                    for i in range(ncore)])
    return out


# version 15 (final)
# speedup vs baseline: 1.6844x; 1.3850x over previous
import numpy as np
import ml_dtypes
from contextlib import ExitStack

import concourse.bass as bass
import concourse.mybir as mybir
from concourse.bass_utils import run_bass_kernel_spmd

F32 = mybir.dt.float32
BF16 = mybir.dt.bfloat16
ALU = mybir.AluOpType
AF = mybir.ActivationFunctionType
AX = mybir.AxisListType

NEG = -1.0e30
D = 1024
DC = D // 128
TFULL = 4096

ENGS = ("sync", "vector", "scalar", "gpsimd", "tensor")
DMA_LANES = {"sync": ("qs0", "qs1", "qs2", "qs3"),
             "gpsimd": ("qg0", "qg1", "qg2", "qg3"),
             "vector": ("qv0",), "scalar": ("qc0",)}
SEM_NAMES = ("qs0", "qs1", "qs2", "qs3", "qg0", "qg1", "qg2", "qg3",
             "qv0", "qc0", "cv", "cs", "cg", "ct")
CMP_SEM = {"vector": "cv", "scalar": "cs", "gpsimd": "cg", "tensor": "ct"}


class G:
    __slots__ = ("w", "r", "name")

    def __init__(self, name=""):
        self.w = None
        self.r = []
        self.name = name


class Rec:
    def __init__(self):
        self.ops = {e: [] for e in ENGS}
        self.cnt = {s: 0 for s in SEM_NAMES}
        self.waited = {e: {s: 0 for s in SEM_NAMES} for e in ENGS}
        self.lane_i = {e: 0 for e in DMA_LANES}
        self.sems = None
        self.nops = 0
        self.fence = []

    def _wait(self, eng, tok, cur_stream, enforce_same_queue=True):
        if tok is None:
            return
        sem, val, stream = tok
        if stream == cur_stream:
            if stream.startswith("c:") or not enforce_same_queue:
                return
        if self.waited[eng][sem] >= val:
            return
        self.waited[eng][sem] = val

        def fn(e, sem=sem, val=val):
            e.wait_ge(self.sems[sem], val)
        self.ops[eng].append(fn)

    PARANOID_COMPUTE = False

    def op(self, eng, fn, reads=(), writes=(), dma=False):
        if dma:
            lanes = DMA_LANES[eng]
            sem = lanes[self.lane_i[eng] % len(lanes)]
            self.lane_i[eng] += 1
            amt = 16
            stream = "d:" + eng
            prev = self.cnt[sem]
            if prev > 0 and self.waited[eng][sem] < prev:
                self.waited[eng][sem] = prev
                self.ops[eng].append(
                    lambda e, sem=sem, prev=prev: e.wait_ge(self.sems[sem], prev))
        else:
            sem = CMP_SEM[eng]
            amt = 1
            stream = "c:" + eng
        if Rec.PARANOID_COMPUTE and not dma:
            for s2 in SEM_NAMES:
                val = self.cnt[s2]
                if val > 0 and self.waited[eng][s2] < val:
                    self.waited[eng][s2] = val
                    self.ops[eng].append(
                        lambda e, s2=s2, val=val: e.wait_ge(self.sems[s2], val))
        for b in reads:
            self._wait(eng, b.w, stream)
        for b in writes:
            if b.w is None and not b.r:
                for t in self.fence:
                    self._wait(eng, t, stream)
            self._wait(eng, b.w, stream, enforce_same_queue=False)
            for t in b.r:
                self._wait(eng, t, stream)
        self.cnt[sem] += amt
        tok = (sem, self.cnt[sem], stream)

        def wrapped(e, fn=fn, sem=sem, amt=amt):
            fn(e).then_inc(self.sems[sem], amt)
        self.ops[eng].append(wrapped)
        self.nops += 1
        for b in writes:
            b.w = tok
            b.r = []
        for b in reads:
            b.r.append(tok)
        return tok

    PARANOID_DMA = False

    def dma(self, eng, out, in_, reads=(), writes=(), **kw):
        if Rec.PARANOID_DMA:
            s = DMA_SEM[eng]
            val = self.cnt[s]
            if val > 0 and self.waited[eng][s] < val:
                self.waited[eng][s] = val
                self.ops[eng].append(
                    lambda e, s=s, val=val: e.wait_ge(self.sems[s], val))
        return self.op(eng, lambda e, out=out, in_=in_, kw=kw: e.dma_start(out=out, in_=in_, **kw),
                       reads=reads, writes=writes, dma=True)

    def barrier_token(self):
        return [(s, self.cnt[s], "x") for s in SEM_NAMES if self.cnt[s] > 0]

    def final_drain(self, eng="sync"):
        for s in SEM_NAMES:
            if self.cnt[s] > 0:
                val = self.cnt[s]
                if self.waited[eng][s] >= val:
                    continue
                self.waited[eng][s] = val

                def fn(e, s=s, val=val):
                    e.wait_ge(self.sems[s], val)
                self.ops[eng].append(fn)

    def build(self, nc, stack):
        self.sems = {s: stack.enter_context(nc.semaphore(s)) for s in SEM_NAMES}
        block = stack.enter_context(nc.Block())
        for eng in ENGS:
            ops = self.ops[eng]
            if not ops:
                continue

            def replay(e, ops=ops):
                for fn in ops:
                    fn(e)
            getattr(block, eng)(replay)


class Ctx:

    def __init__(self, nc, stack, T=TFULL):
        self.nc = nc
        self.stack = stack
        self.r = Rec()
        self.T = T
        self.NTG = T // 512
        self.NQT = T // 128
        self._sb = {}
        self._g = {}
        self._scopes = []
        self.pb = stack.enter_context(nc.psum_tensor("psum_all", [128, 4096], F32))
        self.consts = None
        self.hbm = {}

    def push(self):
        self._scopes.append((ExitStack(), []))

    def pop(self):
        st, names = self._scopes.pop()
        for n in names:
            del self._sb[n]
            del self._g["sb:" + n]
        st.close()
        self.r.fence = self.r.barrier_token()

    _uniq = 0

    def sb(self, name, shape, dtype):
        if name not in self._sb:
            if self._scopes:
                st, names = self._scopes[-1]
                names.append(name)
            else:
                st = self.stack
            Ctx._uniq += 1
            t = st.enter_context(self.nc.sbuf_tensor(
                "%s__%d" % (name, Ctx._uniq), shape, dtype))
            self._sb[name] = t
            self._g["sb:" + name] = G("sb:" + name)
        return self._sb[name]

    def g(self, name):
        return self._g["sb:" + name]

    def bankg(self, i):
        k = "pb:%d" % i
        if k not in self._g:
            self._g[k] = G(k)
        return self._g[k]

    def bank(self, i, n=1):
        return self.pb[:, 512 * i:512 * (i + n)]

    def dram(self, name, shape, dtype, kind="Internal", **kw):
        if name not in self.hbm:
            ap = self.nc.dram_tensor(name, shape, dtype, kind=kind, **kw).ap()
            self.hbm[name] = (ap, G("hbm:" + name))
        return self.hbm[name]

    def finish(self):
        self.r.final_drain("sync")
        self.r.build(self.nc, self.stack)



def _pm(vec):
    v = np.asarray(vec)
    n = v.shape[-1] // 128
    return np.ascontiguousarray(v.reshape(n, 128).T)


def host_consts(T=TFULL):
    c = {}
    c["ones_col_f"] = np.ones((128, 1), np.float32)
    c["ones_row_f"] = np.ones((1, 128), np.float32)
    c["ones_row_b"] = np.ones((1, 128), ml_dtypes.bfloat16)
    c["ones_col_b"] = np.ones((128, 1), ml_dtypes.bfloat16)
    f = np.arange(512)[None, :]
    p = np.arange(128)[:, None]
    masks = [np.where(f <= o + p, 0.0, NEG).astype(np.float32) for o in (0, 128, 256, 384)]
    c["cmask"] = np.stack(masks)
    qt = np.arange(T // 128)[None, :]
    c["countdown"] = ((T - 1) - (qt * 128 + p)).astype(np.float32)
    seg = np.zeros((128, 2), np.float32)
    seg[:64, 0] = 1.0
    seg[64:, 1] = 1.0
    c["seg_lhsT"] = seg.astype(ml_dtypes.bfloat16)
    c["ind_lhsT"] = np.ascontiguousarray(seg.T).astype(ml_dtypes.bfloat16)
    c["ind_f"] = np.ascontiguousarray(seg.T)
    c["ident_f"] = np.eye(128, dtype=np.float32)
    kv = np.zeros((128, 8), np.float32)
    kv[:, 0] = 1e-12
    c["kvals"] = kv
    return c



def host_inputs(core, inputs, T=TFULL):
    g = lambda k: np.asarray(inputs[k])
    m = {}
    x = g("L0_attn_X")[core]
    m["xT"] = np.ascontiguousarray(x.T).astype(np.float32)
    m["w_q"] = g("L0_attn_Wq").astype(np.float32)
    m["w_k"] = g("L0_attn_Wk").astype(np.float32)
    m["w_v"] = g("L0_attn_Wv").astype(np.float32)
    m["b_qkv0"] = np.concatenate(
        [_pm(g("L0_attn_bq")), _pm(g("L0_attn_bk")), _pm(g("L0_attn_bv"))],
        axis=1).astype(np.float32)
    rwt = np.zeros((1024, 64), np.float32)
    rwt[:, :32] = g("L0_ffn_router_w").T
    m["router_wT"] = rwt
    rbp = np.zeros((1, 64), np.float32)
    rbp[0, :32] = g("L0_ffn_router_bias")
    m["router_b"] = rbp
    m["gate_wT"] = np.ascontiguousarray(g("L0_ffn_gate_w").transpose(0, 2, 1)).astype(np.float32)
    m["up_wT"] = np.ascontiguousarray(g("L0_ffn_up_w").transpose(0, 2, 1)).astype(np.float32)
    m["down_wT"] = np.ascontiguousarray(g("L0_ffn_down_w").transpose(0, 2, 1)).astype(np.float32)
    m["sgate_wT"] = np.ascontiguousarray(g("L0_ffn_sgate_w").T).astype(np.float32)
    m["sup_wT"] = np.ascontiguousarray(g("L0_ffn_sup_w").T).astype(np.float32)
    m["sdown_wT"] = np.ascontiguousarray(g("L0_ffn_sdown_w").T).astype(np.float32)
    m["wqkv1T"] = np.ascontiguousarray(g("L1_attn_W_qkv").T).astype(np.float32)
    m["b_qkv1"] = _pm(g("L1_attn_b_qkv")).astype(np.float32)
    m["l1f_We"] = g("L1_ffn_We").astype(np.float32)
    m["l1f_be"] = np.ascontiguousarray(
        g("L1_ffn_be").reshape(20, 8, 128).transpose(0, 2, 1)).astype(np.float32)
    m["l1f_Wq"] = g("L1_ffn_Wq").astype(np.float32)
    m["l1f_Wk"] = g("L1_ffn_Wk").astype(np.float32)
    m["l1f_Wv"] = g("L1_ffn_Wv").astype(np.float32)
    m["l1f_Wo"] = g("L1_ffn_Wo").astype(np.float32)
    m["l1f_lng"] = _pm(g("L1_ffn_ln_g")).astype(np.float32)
    m["l1f_lnb"] = _pm(g("L1_ffn_ln_b")).astype(np.float32)
    m["l1f_ew"] = np.ascontiguousarray(g("L1_ffn_ew").reshape(1, 18)).astype(np.float32)
    m["l1f_actw"] = np.ascontiguousarray(g("L1_ffn_act_w")[None, :]).astype(np.float32)
    return m


CONST_SPECS = {
    "ones_col_f": ([128, 1], F32),
    "ones_row_f": ([1, 128], F32),
    "ones_row_b": ([1, 128], BF16),
    "ones_col_b": ([128, 1], BF16),
    "cmask": ([4, 128, 512], F32),
    "countdown": (None, F32),
    "seg_lhsT": ([128, 2], BF16),
    "ind_lhsT": ([2, 128], BF16),
    "ind_f": ([2, 128], F32),
    "ident_f": ([128, 128], F32),
    "kvals": ([128, 8], F32),
}


def declare_consts(C, only=None):
    nc, r = C.nc, C.r
    c = {}
    for name, (shape, dt) in CONST_SPECS.items():
        if only is not None and name not in only:
            continue
        if name == "countdown":
            shape = [128, C.T // 128]
        ap = nc.dram_tensor(name, shape, dt, kind="ExternalInput").ap()
        if name == "cmask":
            t = C.sb("k_cmask", [128, 4 * 512], F32)
            for gi in range(4):
                r.dma("sync", t[:, gi * 512:(gi + 1) * 512], ap[gi], writes=[C.g("k_cmask")])
        else:
            t = C.sb("k_" + name, shape, dt)
            r.dma("sync", t[:], ap, writes=[C.g("k_" + name)])
        c[name] = t
    C.consts = c
    return c



def emit_ln(C, x_hbm, xg, xn_hbm, xng, eps, gamma_beta=None, tag="ln",
            in_dtype=F32, out_dtype=BF16):
    r = C.r
    xb = C.sb(tag + "_x", [128, DC * 512], F32)
    if in_dtype != F32:
        xbr = C.sb(tag + "_xr", [128, DC * 512], in_dtype)
        gxbr = C.g(tag + "_xr")
    sq = C.sb(tag + "_sq", [128, DC * 512], F32)
    stats = C.sb(tag + "_st", [1, 1024], F32)
    ab = C.sb(tag + "_ab", [128, 1024], F32)
    xnb = C.sb(tag + "_xn", [128, DC * 512], out_dtype)
    gxb, gsq, gst, gab, gxn = (C.g(tag + "_x"), C.g(tag + "_sq"), C.g(tag + "_st"),
                               C.g(tag + "_ab"), C.g(tag + "_xn"))
    if gamma_beta is not None:
        gb = C.sb(tag + "_gb", [128, 16], F32)
        ggb = C.g(tag + "_gb")
        r.dma("sync", gb[:, 0:8], gamma_beta[0], writes=[ggb])
        r.dma("sync", gb[:, 8:16], gamma_beta[1], writes=[ggb])
    ones_f = C.consts["ones_col_f"]
    ones_row_f = C.consts["ones_row_f"]
    konesg = C.g("k_ones_col_f")
    inv_d = 1.0 / D
    for tg in range(C.NTG):
        if in_dtype == F32:
            for ic in range(DC):
                r.dma("sync", xb[:, ic * 512:(ic + 1) * 512],
                      x_hbm[ic * 128:(ic + 1) * 128, tg * 512:(tg + 1) * 512],
                      reads=[xg], writes=[gxb])
        else:
            for ic in range(DC):
                r.dma("sync", xbr[:, ic * 512:(ic + 1) * 512],
                      x_hbm[ic * 128:(ic + 1) * 128, tg * 512:(tg + 1) * 512],
                      reads=[xg], writes=[gxbr])
            for ic in range(DC):
                sl = slice(ic * 512, (ic + 1) * 512)
                r.op("vector", lambda e, sl=sl: e.tensor_copy(out=xb[:, sl], in_=xbr[:, sl]),
                     reads=[gxbr], writes=[gxb])
        for ic in range(DC):
            sl = slice(ic * 512, (ic + 1) * 512)
            r.op("vector", lambda e, sl=sl: e.tensor_mul(out=sq[:, sl], in0=xb[:, sl], in1=xb[:, sl]),
                 reads=[gxb], writes=[gsq])
        for ic in range(DC):
            sl = slice(ic * 512, (ic + 1) * 512)
            r.op("tensor", lambda e, sl=sl, ic=ic: e.matmul(
                C.bank(0)[0:1, :], ones_f[:], xb[:, sl], start=(ic == 0), stop=(ic == DC - 1)),
                reads=[gxb, konesg], writes=[C.bankg(0)])
        for ic in range(DC):
            sl = slice(ic * 512, (ic + 1) * 512)
            r.op("tensor", lambda e, sl=sl, ic=ic: e.matmul(
                C.bank(1)[0:1, :], ones_f[:], sq[:, sl], start=(ic == 0), stop=(ic == DC - 1)),
                reads=[gsq, konesg], writes=[C.bankg(1)])
        r.op("vector", lambda e: e.tensor_scalar_mul(out=stats[0:1, 0:512], in0=C.bank(0)[0:1, :], scalar1=inv_d),
             reads=[C.bankg(0)], writes=[gst])
        r.op("vector", lambda e: e.tensor_scalar_mul(out=stats[0:1, 512:1024], in0=C.bank(1)[0:1, :], scalar1=inv_d),
             reads=[C.bankg(1)], writes=[gst])
        r.op("vector", lambda e: e.tensor_mul(out=sq[0:1, 0:512], in0=stats[0:1, 0:512], in1=stats[0:1, 0:512]),
             reads=[gst], writes=[gsq])
        r.op("vector", lambda e: e.tensor_sub(out=stats[0:1, 512:1024], in0=stats[0:1, 512:1024], in1=sq[0:1, 0:512]),
             reads=[gsq], writes=[gst])
        r.op("vector", lambda e: e.tensor_scalar_add(out=stats[0:1, 512:1024], in0=stats[0:1, 512:1024], scalar1=float(eps)),
             reads=[gst], writes=[gst])
        r.op("scalar", lambda e: e.activation(out=stats[0:1, 512:1024], in_=stats[0:1, 512:1024], func=AF.Ln),
             reads=[gst], writes=[gst])
        r.op("scalar", lambda e: e.activation(out=stats[0:1, 512:1024], in_=stats[0:1, 512:1024], func=AF.Exp, scale=-0.5),
             reads=[gst], writes=[gst])
        r.op("tensor", lambda e: e.matmul(C.bank(2), ones_row_f[:], stats[0:1, 0:512], start=True, stop=True),
             reads=[gst, C.g("k_ones_row_f")], writes=[C.bankg(2)])
        r.op("tensor", lambda e: e.matmul(C.bank(3), ones_row_f[:], stats[0:1, 512:1024], start=True, stop=True),
             reads=[gst, C.g("k_ones_row_f")], writes=[C.bankg(3)])
        r.op("vector", lambda e: e.tensor_copy(out=ab[:, 0:512], in_=C.bank(2)),
             reads=[C.bankg(2)], writes=[gab])
        r.op("vector", lambda e: e.tensor_copy(out=ab[:, 512:1024], in_=C.bank(3)),
             reads=[C.bankg(3)], writes=[gab])
        for ic in range(DC):
            sl = slice(ic * 512, (ic + 1) * 512)
            r.op("vector", lambda e, sl=sl: e.tensor_sub(out=sq[:, sl], in0=xb[:, sl], in1=ab[:, 0:512]),
                 reads=[gxb, gab], writes=[gsq])
            if gamma_beta is None:
                r.op("vector", lambda e, sl=sl: e.tensor_mul(out=xnb[:, sl], in0=sq[:, sl], in1=ab[:, 512:1024]),
                     reads=[gsq, gab], writes=[gxn])
            else:
                r.op("vector", lambda e, sl=sl: e.tensor_mul(out=sq[:, sl], in0=sq[:, sl], in1=ab[:, 512:1024]),
                     reads=[gsq, gab], writes=[gsq])
                r.op("vector", lambda e, sl=sl, ic=ic: e.tensor_scalar(
                    out=xnb[:, sl], in0=sq[:, sl], scalar1=gb[:, ic:ic + 1],
                    scalar2=gb[:, 8 + ic:9 + ic], op0=ALU.mult, op1=ALU.add),
                    reads=[gsq, ggb], writes=[gxn])
        for ic in range(DC):
            sl = slice(ic * 512, (ic + 1) * 512)
            r.dma("sync", xn_hbm[ic * 128:(ic + 1) * 128, tg * 512:(tg + 1) * 512], xnb[:, sl],
                  reads=[gxn], writes=[xng])
    return


def load_slab(C, name, hbm_ap, hg, nchunk, T, dtype=BF16, queue="sync"):
    t = C.sb(name, [128, nchunk * T], dtype)
    g = C.g(name)
    for ic in range(nchunk):
        C.r.dma(queue, t[:, ic * T:(ic + 1) * T], hbm_ap[ic * 128:(ic + 1) * 128, :],
                reads=[hg], writes=[g])
    return t, g


def load_w(C, name, w_hbm, wg, in_chunks, out_cols):
    t = C.sb(name, [128, in_chunks * out_cols], BF16)
    g = C.g(name)
    for ic in range(in_chunks):
        C.r.dma("gpsimd", t[:, ic * out_cols:(ic + 1) * out_cols],
                w_hbm[ic * 128:(ic + 1) * 128, :], reads=[wg], writes=[g])
    return t, g


def wsl(C, wname, in_chunks, out_cols, ic, oc):
    return C._sb[wname][:, ic * out_cols + oc * 128: ic * out_cols + (oc + 1) * 128]


def rbufs(C, base, n, shape, dtype):
    return [(C.sb("%s%d" % (base, j), shape, dtype), C.g("%s%d" % (base, j)))
            for j in range(n)]


def evac(C, i, bank, bankg, out_sb, out_g, bias=None, biasg=None,
         scale=None, scaleg=None):
    r = C.r
    extra = [g for g in (biasg, scaleg) if g is not None]
    if i % 2 == 0:
        if bias is not None:
            r.op("vector", lambda e, o=out_sb, b=bank, s=bias: e.tensor_scalar_add(
                out=o, in0=b, scalar1=s), reads=[bankg] + extra, writes=[out_g])
        elif scale is not None:
            r.op("vector", lambda e, o=out_sb, b=bank, s=scale: e.tensor_scalar_mul(
                out=o, in0=b, scalar1=s), reads=[bankg] + extra, writes=[out_g])
        else:
            r.op("vector", lambda e, o=out_sb, b=bank: e.tensor_copy(out=o, in_=b),
                 reads=[bankg], writes=[out_g])
    else:
        if bias is not None:
            r.op("scalar", lambda e, o=out_sb, b=bank, s=bias: e.activation(
                out=o, in_=b, func=AF.Identity, bias=s),
                reads=[bankg] + extra, writes=[out_g])
        elif scale is not None:
            r.op("scalar", lambda e, o=out_sb, b=bank, s=scale: e.activation(
                out=o, in_=b, func=AF.Identity, scale=s),
                reads=[bankg] + extra, writes=[out_g])
        else:
            r.op("scalar", lambda e, o=out_sb, b=bank: e.copy(out=o, in_=b),
                 reads=[bankg], writes=[out_g])


def emit_proj(C, slab_name, w_name, in_chunks, out_cols, epi, banks=(0, 1), tgs=None):
    r = C.r
    slab = C._sb[slab_name]
    sg = C.g(slab_name)
    wg = C.g(w_name)
    T = C.T
    if tgs is None:
        tgs = range(C.NTG)
    nb = len(banks)
    i = 0
    for oc in range(out_cols // 128):
        for tg in tgs:
            b = banks[i % nb]
            i += 1
            for ic in range(in_chunks):
                lh = wsl(C, w_name, in_chunks, out_cols, ic, oc)
                rh = slab[:, ic * T + tg * 512: ic * T + tg * 512 + 512]
                r.op("tensor", lambda e, lh=lh, rh=rh, ic=ic, b=b: e.matmul(
                    C.bank(b), lh, rh,
                    start=(ic == 0), stop=(ic == in_chunks - 1)),
                    reads=[sg, wg], writes=[C.bankg(b)])
            epi(oc, tg, C.bank(b), C.bankg(b))


def emit_l0attn(C, xn_hbm, xng, s_hbm, sg_out, wq_ap, wqg, wk_ap, wkg, wv_ap, wvg,
                b0_ap, b0g):
    r, T, NTG, NQT = C.r, C.T, C.NTG, C.NQT
    nc = C.nc
    q_hbm, qg = C.dram("sc_q0T", [D, T], BF16)
    bqkv = C.sb("l0a_b", [128, 24], F32)
    r.dma("sync", bqkv[:], b0_ap, reads=[b0g], writes=[C.g("l0a_b")])
    bg = C.g("l0a_b")

    xnslab, xnsg = load_slab(C, "l0a_xn", xn_hbm, xng, DC, T)
    load_w(C, "l0a_w", wq_ap, wqg, DC, D)

    evs = rbufs(C, "l0a_ev", 2, [128, 512], BF16)
    cnt = [0]

    def epi_q(oc, tg, bank, bankg):
        i = cnt[0]; cnt[0] += 1
        tmp, gtmp = evs[i % 2]
        evac(C, i, bank, bankg, tmp[:], gtmp, bias=bqkv[:, oc:oc + 1], biasg=bg)
        r.dma("sync", q_hbm[oc * 128:(oc + 1) * 128, tg * 512:(tg + 1) * 512], tmp[:],
              reads=[gtmp], writes=[qg])
    emit_proj(C, "l0a_xn", "l0a_w", DC, D, epi_q, banks=(0, 1, 2, 3))

    kslab = C.sb("l0a_k", [128, DC * T], BF16)
    kg = C.g("l0a_k")
    load_w(C, "l0a_w", wk_ap, wkg, DC, D)

    def epi_k(oc, tg, bank, bankg):
        i = cnt[0]; cnt[0] += 1
        evac(C, i, bank, bankg,
             kslab[:, oc * T + tg * 512: oc * T + tg * 512 + 512], kg,
             bias=bqkv[:, 8 + oc:9 + oc], biasg=bg)
    emit_proj(C, "l0a_xn", "l0a_w", DC, D, epi_k, banks=(0, 1, 2, 3))

    qt_b = C.sb("l0a_qt", [128, DC * 128], BF16)
    qtg = C.g("l0a_qt")
    srow = C.sb("l0a_srow", [128, T], F32)
    srg = C.g("l0a_srow")
    pexp = C.sb("l0a_p", [128, T], BF16)
    pg = C.g("l0a_p")
    cols = C.sb("l0a_cols", [128, 8], F32)
    colg = C.g("l0a_cols")
    ratio = C.sb("l0a_ratio", [128, NQT], F32)
    ratg = C.g("l0a_ratio")
    cmask = C.consts["cmask"]
    cmg = C.g("k_cmask")
    cd = C.consts["countdown"]
    cdg = C.g("k_countdown")
    scale = 1.0 / 32.0
    for qt in range(NQT):
        ngrp = qt // 4 + 1
        for ic in range(DC):
            r.dma("sync", qt_b[:, ic * 128:(ic + 1) * 128],
                  q_hbm[ic * 128:(ic + 1) * 128, qt * 128:(qt + 1) * 128],
                  reads=[qg], writes=[qtg])
        for kg_i in range(ngrp):
            b = kg_i % 2
            for ic in range(DC):
                r.op("tensor", lambda e, ic=ic, kg_i=kg_i, b=b: e.matmul(
                    C.bank(b), qt_b[:, ic * 128:(ic + 1) * 128],
                    kslab[:, ic * T + kg_i * 512: ic * T + kg_i * 512 + 512],
                    start=(ic == 0), stop=(ic == DC - 1)),
                    reads=[qtg, kg], writes=[C.bankg(b)])
            if kg_i == ngrp - 1:
                r.op("vector", lambda e, kg_i=kg_i, b=b, m=qt % 4: e.tensor_add(
                    out=srow[:, kg_i * 512:(kg_i + 1) * 512], in0=C.bank(b),
                    in1=cmask[:, m * 512:(m + 1) * 512]),
                    reads=[C.bankg(b), cmg], writes=[srg])
            else:
                r.op("vector", lambda e, kg_i=kg_i, b=b: e.tensor_copy(
                    out=srow[:, kg_i * 512:(kg_i + 1) * 512], in_=C.bank(b)),
                    reads=[C.bankg(b)], writes=[srg])
        kw = ngrp * 512
        r.op("vector", lambda e, kw=kw: e.reduce_max(out=cols[:, 0:1], in_=srow[:, 0:kw], axis=AX.X),
             reads=[srg], writes=[colg])
        r.op("scalar", lambda e: e.activation(out=cols[:, 1:2], in_=cols[:, 0:1], func=AF.Relu, scale=scale),
             reads=[colg], writes=[colg])
        r.op("scalar", lambda e: e.mul(out=cols[:, 2:3], in_=cols[:, 1:2], mul=-1.0),
             reads=[colg], writes=[colg])
        r.op("scalar", lambda e, kw=kw: e.activation(
            out=pexp[:, 0:kw], in_=srow[:, 0:kw], func=AF.Exp, bias=cols[:, 2:3],
            scale=scale, accum_out=cols[:, 3:4]),
            reads=[srg, colg], writes=[pg, colg])
        r.op("scalar", lambda e: e.activation(out=cols[:, 4:5], in_=cols[:, 2:3], func=AF.Exp),
             reads=[colg], writes=[colg])
        r.op("scalar", lambda e, qt=qt: e.activation(out=cols[:, 5:6], in_=cd[:, qt:qt + 1],
                                                     func=AF.Identity, scale=cols[:, 4:5]),
             reads=[colg, cdg], writes=[colg])
        r.op("scalar", lambda e: e.activation(out=cols[:, 6:7], in_=cols[:, 3:4],
                                              func=AF.Identity, bias=cols[:, 5:6]),
             reads=[colg], writes=[colg])
        r.op("vector", lambda e: e.reciprocal(out=cols[:, 7:8], in_=cols[:, 6:7]),
             reads=[colg], writes=[colg])
        r.op("scalar", lambda e, qt=qt: e.activation(out=ratio[:, qt:qt + 1], in_=cols[:, 7:8],
                                                     func=AF.Identity, scale=cols[:, 6:7]),
             reads=[colg], writes=[ratg])

    rrows = C.sb("l0a_rr", [NQT, 128], F32)
    rrg = C.g("l0a_rr")
    rflat = C.sb("l0a_rf", [1, T], F32)
    rfg = C.g("l0a_rf")
    ident = C.consts["ident_f"]
    idg = C.g("k_ident_f")
    r.op("tensor", lambda e: e.transpose(C.bank(4)[0:NQT, 0:128], ratio[:], ident[:]),
         reads=[ratg, idg], writes=[C.bankg(4)])
    r.op("vector", lambda e: e.tensor_copy(out=rrows[:], in_=C.bank(4)[0:NQT, 0:128]),
         reads=[C.bankg(4)], writes=[rrg])
    rdram, rdg = C.dram("sc_l0a_r", [NQT, 128], F32)
    r.dma("sync", rdram, rrows[:], reads=[rrg], writes=[rdg])
    r.dma("sync", rflat[:], rdram.rearrange("q p -> (q p)")[None, :], reads=[rdg], writes=[rfg])

    load_w(C, "l0a_w", wv_ap, wvg, DC, D)
    rb = C.sb("l0a_rb", [128, 512], F32)
    rbg = C.g("l0a_rb")
    sevs = rbufs(C, "l0a_sev", 2, [128, 512], F32)
    ones_row_f = C.consts["ones_row_f"]
    org = C.g("k_ones_row_f")
    cnt2 = [0]

    def epi_v(oc, tg, bank, bankg):
        i = cnt2[0]; cnt2[0] += 1
        sev, sevg = sevs[i % 2]
        if oc == 0:
            r.op("tensor", lambda e, tg=tg: e.matmul(
                C.bank(5), ones_row_f[:], rflat[0:1, tg * 512:(tg + 1) * 512],
                start=True, stop=True),
                reads=[rfg, org], writes=[C.bankg(5)])
            r.op("vector", lambda e: e.tensor_copy(out=rb[:], in_=C.bank(5)),
                 reads=[C.bankg(5)], writes=[rbg])
        r.op("vector", lambda e, oc=oc, bank=bank, sev=sev: e.scalar_tensor_tensor(
            out=sev[:], in0=bank, scalar=bqkv[:, 16 + oc:17 + oc], in1=rb[:],
            op0=ALU.add, op1=ALU.mult),
            reads=[bankg, bg, rbg], writes=[sevg])
        r.dma("sync", s_hbm[oc * 128:(oc + 1) * 128, tg * 512:(tg + 1) * 512], sev[:],
              reads=[sevg], writes=[sg_out])
    emit_proj(C, "l0a_xn", "l0a_w", DC, D, epi_v, banks=(6, 7))

def emit_l0ffn(C, xn_hbm, xng, x_hbm, xg, s_hbm, sg_in, x1_hbm, x1g,
               rw_ap, rwg, rb_ap, rbg_in, gw_ap, gwg, uw_ap, uwg, dw_ap, dwg,
               sga_ap, sgag, sua_ap, suag, sda_ap, sdag):
    r, T, NTG = C.r, C.T, C.NTG
    E = 32
    xnslab, xnsg = load_slab(C, "l0f_xn", xn_hbm, xng, DC, T)
    rwsl, rwsg = load_w(C, "l0f_rw", rw_ap, rwg, DC, 64)
    rbrow = C.sb("l0f_rb", [1, 64], F32)
    r.dma("sync", rbrow[:], rb_ap, reads=[rbg_in], writes=[C.g("l0f_rb")])
    rbb = C.sb("l0f_rbb", [128, 64], F32)
    r.op("tensor", lambda e: e.matmul(C.bank(0)[:, 0:64], C.consts["ones_row_f"][:], rbrow[:],
                                      start=True, stop=True),
         reads=[C.g("l0f_rb"), C.g("k_ones_row_f")], writes=[C.bankg(0)])
    r.op("vector", lambda e: e.tensor_copy(out=rbb[:], in_=C.bank(0)[:, 0:64]),
         reads=[C.bankg(0)], writes=[C.g("l0f_rbb")])

    z = C.sb("l0f_z", [128, 64], F32)
    zw = C.sb("l0f_zw", [128, 64], F32)
    t32 = C.sb("l0f_t32", [128, 64], F32)
    wtsf = C.sb("l0f_wts", [128, 64], F32)
    cols = C.sb("l0f_c", [128, 8], F32)
    wrows = C.sb("l0f_wr", [32, T], BF16)
    gz, gzw, gt32, gwt, gc, gwr = (C.g("l0f_z"), C.g("l0f_zw"), C.g("l0f_t32"),
                                   C.g("l0f_wts"), C.g("l0f_c"), C.g("l0f_wr"))
    ident = C.consts["ident_f"]
    idg = C.g("k_ident_f")
    for tt in range(T // 128):
        for ic in range(DC):
            r.op("tensor", lambda e, ic=ic, tt=tt: e.matmul(
                C.bank(1)[:, 0:64], xnslab[:, ic * T + tt * 128: ic * T + tt * 128 + 128],
                rwsl[:, ic * 64:(ic + 1) * 64], start=(ic == 0), stop=(ic == DC - 1)),
                reads=[xnsg, rwsg], writes=[C.bankg(1)])
        r.op("vector", lambda e: e.tensor_add(out=z[:], in0=C.bank(1)[:, 0:64], in1=rbb[:]),
             reads=[C.bankg(1), C.g("l0f_rbb")], writes=[gz])
        r.op("vector", lambda e: e.reduce_max(out=cols[:, 0:1], in_=z[:, 0:E], axis=AX.X),
             reads=[gz], writes=[gc])
        r.op("scalar", lambda e: e.mul(out=cols[:, 1:2], in_=cols[:, 0:1], mul=-1.0),
             reads=[gc], writes=[gc])
        r.op("scalar", lambda e: e.activation(out=z[:, 0:E], in_=z[:, 0:E], func=AF.Exp, bias=cols[:, 1:2]),
             reads=[gz, gc], writes=[gz])
        r.op("vector", lambda e: e.tensor_copy(out=zw[:], in_=z[:]), reads=[gz], writes=[gzw])
        for it in range(4):
            r.op("vector", lambda e: e.reduce_max(out=cols[:, 2:3], in_=zw[:, 0:E], axis=AX.X),
                 reads=[gzw], writes=[gc])
            r.op("scalar", lambda e: e.copy(out=cols[:, 3:4], in_=cols[:, 2:3]),
                 reads=[gc], writes=[gc])
            if it < 3:
                r.op("vector", lambda e: e.tensor_single_scalar(
                    out=t32[:], in_=zw[:], scalar=cols[:, 3:4], op=ALU.is_ge),
                    reads=[gzw, gc], writes=[gt32])
                r.op("vector", lambda e: e.scalar_tensor_tensor(
                    out=zw[:], in0=t32[:], scalar=NEG, in1=zw[:], op0=ALU.mult, op1=ALU.add),
                    reads=[gt32, gzw], writes=[gzw])
        r.op("vector", lambda e: e.scalar_tensor_tensor(
            out=t32[:], in0=z[:], scalar=cols[:, 3:4], in1=z[:], op0=ALU.is_ge,
            op1=ALU.mult, accum_out=cols[:, 4:5]),
            reads=[gz, gc], writes=[gt32, gc])
        r.op("scalar", lambda e: e.copy(out=cols[:, 5:6], in_=cols[:, 4:5]),
             reads=[gc], writes=[gc])
        r.op("vector", lambda e: e.reciprocal(out=cols[:, 6:7], in_=cols[:, 5:6]),
             reads=[gc], writes=[gc])
        r.op("scalar", lambda e: e.copy(out=cols[:, 7:8], in_=cols[:, 6:7]),
             reads=[gc], writes=[gc])
        r.op("vector", lambda e: e.tensor_scalar_mul(out=wtsf[:], in0=t32[:], scalar1=cols[:, 7:8]),
             reads=[gt32, gc], writes=[gwt])
        r.op("tensor", lambda e: e.transpose(C.bank(2)[0:32, 0:128], wtsf[:, 0:32], ident[:]),
             reads=[gwt, idg], writes=[C.bankg(2)])
        r.op("vector", lambda e, tt=tt: e.tensor_copy(out=wrows[0:32, tt * 128:(tt + 1) * 128],
                                                      in_=C.bank(2)[0:32, 0:128]),
             reads=[C.bankg(2)], writes=[gwr])
    wr_dram, gwrd = C.dram("sc_l0f_wr", [32, T], BF16)
    r.dma("sync", wr_dram, wrows[:], reads=[gwr], writes=[gwrd])
    wfbs = rbufs(C, "l0f_wf", 2, [1, 512], BF16)

    gwes = rbufs(C, "l0f_gwe", 2, [128, DC * 128], BF16)
    uwes = rbufs(C, "l0f_uwe", 2, [128, DC * 128], BF16)
    dwes = rbufs(C, "l0f_dwe", 2, [128, E * 128], BF16)
    sgw, _ = load_w(C, "l0f_sgw", sga_ap, sgag, DC, 512)
    suw, _ = load_w(C, "l0f_suw", sua_ap, suag, DC, 512)
    sdw, _ = load_w(C, "l0f_sdw", sda_ap, sdag, 4, D)
    gsgw, gsuw, gsdw = C.g("l0f_sgw"), C.g("l0f_suw"), C.g("l0f_sdw")
    sgbs = rbufs(C, "l0f_sgb", 2, [128, 512], BF16)
    subs = rbufs(C, "l0f_sub", 2, [128, 512], BF16)
    zsh = C.sb("l0f_zsh", [128, 4 * 512], BF16)
    gzsh = C.g("l0f_zsh")
    hslab = C.sb("l0f_h", [128, E * 512], BF16)
    gh = C.g("l0f_h")
    wbs = rbufs(C, "l0f_wb", 2, [128, 512], BF16)
    sg_ts = rbufs(C, "l0f_sg", 2, [128, 512], BF16)
    su_ts = rbufs(C, "l0f_su", 2, [128, 512], BF16)
    xchs = rbufs(C, "l0f_xch", 2, [128, 512], F32)
    schs = rbufs(C, "l0f_sch", 2, [128, 512], F32)
    oevs = rbufs(C, "l0f_oev", 2, [128, 512], F32)
    onesb = C.consts["ones_row_b"]
    gob = C.g("k_ones_row_b")
    gw3 = gw_ap.rearrange("e (ic p) o -> e p ic o", p=128)
    uw3 = uw_ap.rearrange("e (ic p) o -> e p ic o", p=128)

    for tg in range(NTG):
        tsl = slice(tg * 512, (tg + 1) * 512)
        for hc in range(4):
            b = hc % 2
            sgb, gsgb = sgbs[b]
            sub, gsub = subs[b]
            for ic in range(DC):
                r.op("tensor", lambda e, ic=ic, hc=hc, tg=tg, b=b: e.matmul(
                    C.bank(b), sgw[:, ic * 512 + hc * 128: ic * 512 + (hc + 1) * 128],
                    xnslab[:, ic * T + tg * 512: ic * T + tg * 512 + 512],
                    start=(ic == 0), stop=(ic == DC - 1)),
                    reads=[xnsg, gsgw], writes=[C.bankg(b)])
            for ic in range(DC):
                r.op("tensor", lambda e, ic=ic, hc=hc, tg=tg, b=b: e.matmul(
                    C.bank(6 + b), suw[:, ic * 512 + hc * 128: ic * 512 + (hc + 1) * 128],
                    xnslab[:, ic * T + tg * 512: ic * T + tg * 512 + 512],
                    start=(ic == 0), stop=(ic == DC - 1)),
                    reads=[xnsg, gsuw], writes=[C.bankg(6 + b)])
            r.op("scalar", lambda e, b=b, sgb=sgb: e.activation(out=sgb[:], in_=C.bank(b), func=AF.Silu),
                 reads=[C.bankg(b)], writes=[gsgb])
            r.op("vector", lambda e, b=b, sub=sub: e.tensor_copy(out=sub[:], in_=C.bank(6 + b)),
                 reads=[C.bankg(6 + b)], writes=[gsub])
            r.op("vector", lambda e, hc=hc, sgb=sgb, sub=sub: e.tensor_mul(
                out=zsh[:, hc * 512:(hc + 1) * 512], in0=sgb[:], in1=sub[:]),
                reads=[gsgb, gsub], writes=[gzsh])
        for e_i in range(E):
            b = e_i % 2
            gwe, ggw = gwes[b]
            uwe, guw = uwes[b]
            sg_t, gsg = sg_ts[b]
            su_t, gsu = su_ts[b]
            wb, gwb = wbs[b]
            wfb, gwf = wfbs[b]
            r.dma("sync", wfb[:], wr_dram[e_i:e_i + 1, tsl], reads=[gwrd], writes=[gwf])
            r.dma("gpsimd", gwe[:].rearrange("p (ic o) -> p ic o", ic=DC),
                  gw3[e_i], reads=[gwg], writes=[ggw])
            r.dma("gpsimd", uwe[:].rearrange("p (ic o) -> p ic o", ic=DC),
                  uw3[e_i], reads=[uwg], writes=[guw])
            for ic in range(DC):
                r.op("tensor", lambda e, ic=ic, tg=tg, b=b, gwe=gwe: e.matmul(
                    C.bank(2 + b), gwe[:, ic * 128:(ic + 1) * 128],
                    xnslab[:, ic * T + tg * 512: ic * T + tg * 512 + 512],
                    start=(ic == 0), stop=(ic == DC - 1)),
                    reads=[xnsg, ggw], writes=[C.bankg(2 + b)])
            for ic in range(DC):
                r.op("tensor", lambda e, ic=ic, tg=tg, b=b, uwe=uwe: e.matmul(
                    C.bank(6 + b), uwe[:, ic * 128:(ic + 1) * 128],
                    xnslab[:, ic * T + tg * 512: ic * T + tg * 512 + 512],
                    start=(ic == 0), stop=(ic == DC - 1)),
                    reads=[xnsg, guw], writes=[C.bankg(6 + b)])
            r.op("tensor", lambda e, wfb=wfb: e.matmul(
                C.bank(4), onesb[:], wfb[:],
                start=True, stop=True),
                reads=[gwf, gob], writes=[C.bankg(4)])
            r.op("scalar", lambda e, b=b, sg_t=sg_t: e.activation(out=sg_t[:], in_=C.bank(2 + b), func=AF.Silu),
                 reads=[C.bankg(2 + b)], writes=[gsg])
            r.op("vector", lambda e, b=b, su_t=su_t: e.tensor_copy(out=su_t[:], in_=C.bank(6 + b)),
                 reads=[C.bankg(6 + b)], writes=[gsu])
            r.op("vector", lambda e, wb=wb: e.tensor_copy(out=wb[:], in_=C.bank(4)),
                 reads=[C.bankg(4)], writes=[gwb])
            r.op("vector", lambda e, sg_t=sg_t, su_t=su_t: e.tensor_mul(out=sg_t[:], in0=sg_t[:], in1=su_t[:]),
                 reads=[gsu], writes=[gsg])
            r.op("vector", lambda e, e_i=e_i, sg_t=sg_t, wb=wb: e.tensor_mul(
                out=hslab[:, e_i * 512:(e_i + 1) * 512], in0=sg_t[:], in1=wb[:]),
                reads=[gsg, gwb], writes=[gh])
        for oc in range(DC):
            p = oc % 2
            dwe, gdw = dwes[p]
            xch, gxc = xchs[p]
            sch, gsc = schs[p]
            out_ev, goev = oevs[p]
            dbank = 5
            r.dma("gpsimd", dwe[:].rearrange("h (e o) -> h e o", e=E),
                  dw_ap[:, :, oc * 128:(oc + 1) * 128].rearrange("e h o -> h e o"),
                  reads=[dwg], writes=[gdw])
            for e_i in range(E):
                r.op("tensor", lambda e, e_i=e_i, dwe=dwe, dbank=dbank: e.matmul(
                    C.bank(dbank), dwe[:, e_i * 128:(e_i + 1) * 128],
                    hslab[:, e_i * 512:(e_i + 1) * 512],
                    start=(e_i == 0), stop=False),
                    reads=[gh, gdw], writes=[C.bankg(dbank)])
            for hc in range(4):
                r.op("tensor", lambda e, hc=hc, oc=oc, dbank=dbank: e.matmul(
                    C.bank(dbank), sdw[:, hc * D + oc * 128: hc * D + (oc + 1) * 128],
                    zsh[:, hc * 512:(hc + 1) * 512],
                    start=False, stop=(hc == 3)),
                    reads=[gzsh, gsdw], writes=[C.bankg(dbank)])
            r.dma("sync", xch[:], x_hbm[oc * 128:(oc + 1) * 128, tsl], reads=[xg], writes=[gxc])
            r.dma("sync", sch[:], s_hbm[oc * 128:(oc + 1) * 128, tsl], reads=[sg_in], writes=[gsc])
            r.op("scalar", lambda e, out_ev=out_ev, dbank=dbank: e.activation(
                out=out_ev[:], in_=C.bank(dbank), func=AF.Silu),
                reads=[C.bankg(dbank)], writes=[goev])
            r.op("vector", lambda e, out_ev=out_ev, sch=sch: e.tensor_mul(out=out_ev[:], in0=out_ev[:], in1=sch[:]),
                 reads=[gsc], writes=[goev])
            r.op("vector", lambda e, out_ev=out_ev, xch=xch: e.tensor_add(out=out_ev[:], in0=out_ev[:], in1=xch[:]),
                 reads=[gxc], writes=[goev])
            r.dma("sync", x1_hbm[oc * 128:(oc + 1) * 128, tsl], out_ev[:],
                  reads=[goev], writes=[x1g])


def emit_l1qkv(C, xn_hbm, xng, w_ap, wg_in, b_ap, bg_in, qkv_hbm, qkvg):
    r, T = C.r, C.T
    load_slab(C, "l1q_xn", xn_hbm, xng, DC, T)
    load_w(C, "l1q_w", w_ap, wg_in, DC, 3072)
    bb = C.sb("l1q_b", [128, 24], F32)
    r.dma("sync", bb[:], b_ap, reads=[bg_in], writes=[C.g("l1q_b")])
    bg = C.g("l1q_b")
    evs = rbufs(C, "l1q_ev", 4, [128, 512], BF16)
    cnt = [0]

    def epi(oc, tg, bank, bankg):
        i = cnt[0]; cnt[0] += 1
        ev, gev = evs[i % 4]
        evac(C, i, bank, bankg, ev[:], gev, bias=bb[:, oc:oc + 1], biasg=bg)
        r.dma("sync", qkv_hbm[oc * 128:(oc + 1) * 128, tg * 512:(tg + 1) * 512], ev[:],
              reads=[gev], writes=[qkvg])
    emit_proj(C, "l1q_xn", "l1q_w", DC, 3072, epi, banks=(0, 1, 2, 3))


def emit_l1mix(C, qkv_hbm, qkvg, all_hbm, allg, s1_hbm, s1g):
    r, T = C.r, C.T
    q2 = C.sb("l1m_q", [128, T], BF16)
    k2s = rbufs(C, "l1m_k", 2, [128, 8 * 512], BF16)
    v2s = rbufs(C, "l1m_v", 2, [128, 8 * 512], BF16)
    Etall = C.sb("l1m_e", [128, 8 * 512], BF16)
    egs = [G("l1m_e.%d" % t) for t in range(8)]
    scall = C.sb("l1m_sc", [2, 8 * 512], F32)
    scgs = [G("l1m_sc.%d" % t) for t in range(8)]
    pbs = rbufs(C, "l1m_pb", 2, [128, 512], F32)
    czs = rbufs(C, "l1m_cz", 2, [128, 512], F32)
    pvs = rbufs(C, "l1m_pv", 2, [128, 512], F32)
    gq = C.g("l1m_q")
    seg = C.consts["seg_lhsT"]
    gseg = C.g("k_seg_lhsT")
    indf = C.consts["ind_f"]
    gind = C.g("k_ind_f")
    cnorm = (1.0 / 8.0) * (8.0 ** -0.5)
    for hp in range(8):
        base = 384 * hp
        r.dma("sync", q2[0:64, :], qkv_hbm[base:base + 64, :], reads=[qkvg], writes=[gq])
        r.dma("sync", q2[64:128, :], qkv_hbm[base + 192:base + 256, :], reads=[qkvg], writes=[gq])
        for tg in range(C.NTG):
            tsl = slice(tg * 512, (tg + 1) * 512)
            kb, gk = k2s[tg % 2]
            vb, gv = v2s[tg % 2]
            cz, gcz = czs[tg % 2]
            r.dma("sync", kb[0:64, :].rearrange("p (t c) -> p t c", t=8),
                  all_hbm[:, base + 64:base + 128, tsl].rearrange("t p c -> p t c"),
                  reads=[allg], writes=[gk])
            r.dma("sync", kb[64:128, :].rearrange("p (t c) -> p t c", t=8),
                  all_hbm[:, base + 256:base + 320, tsl].rearrange("t p c -> p t c"),
                  reads=[allg], writes=[gk])
            r.dma("sync", vb[0:64, :].rearrange("p (t c) -> p t c", t=8),
                  all_hbm[:, base + 128:base + 192, tsl].rearrange("t p c -> p t c"),
                  reads=[allg], writes=[gv])
            r.dma("sync", vb[64:128, :].rearrange("p (t c) -> p t c", t=8),
                  all_hbm[:, base + 320:base + 384, tsl].rearrange("t p c -> p t c"),
                  reads=[allg], writes=[gv])
            for t in range(8):
                ksl = slice(t * 512, (t + 1) * 512)
                r.op("vector", lambda e, tsl=tsl, kb=kb, ksl=ksl: e.tensor_mul(
                    out=Etall[:, ksl], in0=q2[:, tsl], in1=kb[:, ksl]),
                    reads=[gq, gk], writes=[egs[t]])
                b = t % 4
                r.op("tensor", lambda e, ksl=ksl, b=b: e.matmul(
                    C.bank(b)[0:2, :], seg[:], Etall[:, ksl], start=True, stop=True),
                    reads=[egs[t], gseg], writes=[C.bankg(b)])
                r.op("scalar", lambda e, ksl=ksl, b=b: e.activation(
                    out=scall[0:2, ksl], in_=C.bank(b)[0:2, :], func=AF.Identity,
                    scale=cnorm),
                    reads=[C.bankg(b)], writes=[scgs[t]])
            for t in range(1, 8):
                r.op("vector", lambda e, t=t: e.tensor_add(
                    out=scall[0:2, t * 512:(t + 1) * 512],
                    in0=scall[0:2, t * 512:(t + 1) * 512],
                    in1=scall[0:2, (t - 1) * 512:t * 512]),
                    reads=[scgs[t - 1]], writes=[scgs[t]])
            for t in range(8):
                ksl = slice(t * 512, (t + 1) * 512)
                bb_ = 4 + t % 2
                pb, gpb = pbs[t % 2]
                pv, gpv = pvs[t % 2]
                r.op("tensor", lambda e, ksl=ksl, bb_=bb_: e.matmul(
                    C.bank(bb_), indf[:], scall[0:2, ksl], start=True, stop=True),
                    reads=[scgs[t], gind], writes=[C.bankg(bb_)])
                r.op("vector", lambda e, bb_=bb_, pb=pb: e.tensor_copy(out=pb[:], in_=C.bank(bb_)),
                     reads=[C.bankg(bb_)], writes=[gpb])
                if t == 0:
                    r.op("vector", lambda e, vb=vb, ksl=ksl, cz=cz, pb=pb: e.tensor_mul(
                        out=cz[:], in0=pb[:], in1=vb[:, ksl]),
                        reads=[gpb, gv], writes=[gcz])
                else:
                    r.op("vector", lambda e, vb=vb, ksl=ksl, pv=pv, pb=pb: e.tensor_mul(
                        out=pv[:], in0=pb[:], in1=vb[:, ksl]),
                        reads=[gpb, gv], writes=[gpv])
                    r.op("vector", lambda e, cz=cz, pv=pv: e.tensor_add(out=cz[:], in0=cz[:], in1=pv[:]),
                         reads=[gpv], writes=[gcz])
            r.dma("sync", s1_hbm[hp * 128:(hp + 1) * 128, tsl], cz[:],
                  reads=[gcz], writes=[s1g])


NODE_SPEC = {
    0: {"edges": [(0, -1)], "q": 0, "k": 0, "v": 0},
    1: {"edges": [(2, 0), (1, -1)], "q": 2, "k": 1, "v": 2},
    2: {"edges": [(5, 1), (3, -1), (4, 0)], "q": 5, "k": 3, "v": 4},
    3: {"edges": [(9, 2), (6, -1), (8, 1)], "q": 9, "k": 6, "v": 8},
    4: {"edges": [(14, 3), (10, -1), (12, 1)], "q": 14, "k": 10, "v": 12},
    5: {"edges": [(19, 4), (15, 0), (17, 2)], "q": 19, "k": 15, "v": 17},
}
LAM = 4.5


def emit_l1f_prep(C, ew_ap, ewg, actw_ap, actwg):
    r = C.r
    row = C.sb("l1f_row", [1, 64], F32)
    gr = C.g("l1f_row")
    r.op("vector", lambda e: e.memset(row[:], 0.0), writes=[gr])
    r.dma("sync", row[0:1, 0:18], ew_ap, reads=[ewg], writes=[gr])
    r.dma("sync", row[0:1, 18:24], actw_ap, reads=[actwg], writes=[gr])
    cols = C.sb("l1f_sc", [128, 64], F32)
    gc = C.g("l1f_sc")
    r.op("tensor", lambda e: e.matmul(C.bank(7)[:, 0:64], C.consts["ones_row_f"][:], row[:],
                                      start=True, stop=True),
         reads=[gr, C.g("k_ones_row_f")], writes=[C.bankg(7)])
    r.op("vector", lambda e: e.tensor_copy(out=cols[:], in_=C.bank(7)[:, 0:64]),
         reads=[C.bankg(7)], writes=[gc])
    return cols, gc


def _edge_proj(C, in_hbm, ing, we_ap, weg, e_i, be_ap, beg, out_hbm, outg):
    r, T = C.r, C.T
    load_slab(C, "l1n_act", in_hbm, ing, DC, T)
    load_w(C, "l1n_w", we_ap[e_i], weg, DC, D)
    bb = C.sb("l1n_be", [128, 8], F32)
    r.dma("sync", bb[:], be_ap[e_i], reads=[beg], writes=[C.g("l1n_be")])
    bg = C.g("l1n_be")
    evs = rbufs(C, "l1n_ev", 4, [128, 512], BF16)
    cnt = [0]

    def epi(oc, tg, bank, bankg):
        i = cnt[0]; cnt[0] += 1
        ev, gev = evs[i % 4]
        evac(C, i, bank, bankg, ev[:], gev, bias=bb[:, oc:oc + 1], biasg=bg)
        r.dma("sync", out_hbm[oc * 128:(oc + 1) * 128, tg * 512:(tg + 1) * 512], ev[:],
              reads=[gev], writes=[outg])
    emit_proj(C, "l1n_act", "l1n_w", DC, D, epi, banks=(0, 1, 2, 3))


def _head_proj(C, in_hbm, ing, w_ap, wg_, scol, scolg, out_hbm, outg):
    r, T = C.r, C.T
    load_slab(C, "l1n_act", in_hbm, ing, DC, T)
    load_w(C, "l1n_w", w_ap, wg_, DC, D)
    evs = rbufs(C, "l1n_ev", 4, [128, 512], BF16)
    cnt = [0]

    def epi(oc, tg, bank, bankg):
        i = cnt[0]; cnt[0] += 1
        ev, gev = evs[i % 4]
        evac(C, i, bank, bankg, ev[:], gev, scale=scol, scaleg=scolg)
        r.dma("sync", out_hbm[oc * 128:(oc + 1) * 128, tg * 512:(tg + 1) * 512], ev[:],
              reads=[gev], writes=[outg])
    emit_proj(C, "l1n_act", "l1n_w", DC, D, epi, banks=(0, 1, 2, 3))


def _vh_proj(C, in_hbm, ing, w_ap, wg_, scol, scolg, out_hbm, outg):
    r, T = C.r, C.T
    slab = C._sb["l1n_act"]
    sg_ = C.g("l1n_act")
    load_w(C, "l1n_w", w_ap, wg_, DC, D)
    wg2 = C.g("l1n_w")
    evs = rbufs(C, "l1n_ev", 4, [128, 512], BF16)
    wt = C._sb["l1n_w"]
    i = 0
    for tt in range(T // 128):
        for og in range(2):
            b = i % 4
            for ic in range(DC):
                lh = slab[:, ic * T + tt * 128: ic * T + tt * 128 + 128]
                rh = wt[:, ic * D + og * 512: ic * D + (og + 1) * 512]
                r.op("tensor", lambda e, lh=lh, rh=rh, ic=ic, b=b: e.matmul(
                    C.bank(b), lh, rh,
                    start=(ic == 0), stop=(ic == DC - 1)),
                    reads=[sg_, wg2], writes=[C.bankg(b)])
            ev, gev = evs[i % 4]
            evac(C, i, C.bank(b), C.bankg(b), ev[:], gev, scale=scol, scaleg=scolg)
            r.dma("sync", out_hbm[tt * 128:(tt + 1) * 128, og * 512:(og + 1) * 512], ev[:],
                  reads=[gev], writes=[outg])
            i += 1


def emit_l1ffn_node(C, c, node_hbm, we_ap, weg, be_ap, beg,
                    wq_ap, wqg, wk_ap, wkg, wv_ap, wvg, wo_ap, wog,
                    ewcols, ewg, out_hbm, outg):
    r, T = C.r, C.T
    spec = NODE_SPEC[c]
    C.push()
    sc_p = {}
    for e_i, n_i in spec["edges"]:
        ap, g_in = node_hbm[n_i]
        name = "sc_p%d" % e_i
        p_ap, p_g = C.dram(name, [D, T], BF16)
        _edge_proj(C, ap, g_in, we_ap, weg, e_i, be_ap, beg, p_ap, p_g)
        sc_p[e_i] = (p_ap, p_g)
    qh_ap, qh_g = C.dram("sc_qh", [D, T], BF16)
    kh_ap, kh_g = C.dram("sc_kh", [D, T], BF16)
    vh_ap, vh_g = C.dram("sc_vh", [T, D], BF16)
    pq_ap, pq_g = sc_p[spec["q"]]
    _head_proj(C, pq_ap, pq_g, wq_ap[c], wqg, ewcols[:, 3 * c:3 * c + 1], ewg, qh_ap, qh_g)
    pk_ap, pk_g = sc_p[spec["k"]]
    _head_proj(C, pk_ap, pk_g, wk_ap[c], wkg, ewcols[:, 3 * c + 1:3 * c + 2], ewg, kh_ap, kh_g)
    pv_ap, pv_g = sc_p[spec["v"]]
    load_slab(C, "l1n_act", pv_ap, pv_g, DC, T)
    _vh_proj(C, pv_ap, pv_g, wv_ap[c], wvg, ewcols[:, 3 * c + 2:3 * c + 3], ewg, vh_ap, vh_g)
    C.pop()

    NKC = T // 128
    ctxs = C.sb("l1a_ctx", [128, DC * T], BF16)
    C.push()
    kh_ss = rbufs(C, "l1a_k", 2, [128, T], BF16)
    qh_s = C.sb("l1a_q", [128, 2 * 512], BF16)
    vh_ss = rbufs(C, "l1a_v", 2, [128, NKC * 128], BF16)
    psl = C.sb("l1a_p", [128, NKC * 512], BF16)
    pgs = [G("l1a_p.%d" % i) for i in range(NKC)]
    Rws = rbufs(C, "l1a_R", 2, [128, 1024], F32)
    c0 = C.sb("l1n_oev", [128, 512], F32)
    sqs = C.sb("l1a_sq", [128, 512], F32)
    strow = C.sb("l1a_st", [1, 8], F32)
    bcol = C.sb("l1a_b", [128, 1], F32)
    rrows = rbufs(C, "l1a_rr", 2, [1, 512], F32)
    rbs = rbufs(C, "l1a_rb", 2, [128, 512], F32)
    gq, gp, gc0, gsq, gst, gb, gctx = (
        C.g("l1a_q"), C.g("l1a_p"),
        C.g("l1n_oev"), C.g("l1a_sq"), C.g("l1a_st"), C.g("l1a_b"),
        C.g("l1a_ctx"))
    ones_f = C.consts["ones_col_f"]
    ones_row_f = C.consts["ones_row_f"]
    g1c, g1r = C.g("k_ones_col_f"), C.g("k_ones_row_f")
    iscale = 128.0 ** -0.5
    inv_n = 1.0 / (128 * 512)
    for h in range(DC):
        hsl = slice(h * 128, (h + 1) * 128)
        kh_s, gk = kh_ss[h % 2]
        vh_s, gv = vh_ss[h % 2]
        r.dma("sync", kh_s[:], kh_ap[hsl, :], reads=[kh_g], writes=[gk])
        r.dma("sync", qh_s[:, 0:512], qh_ap[hsl, 0:512], reads=[qh_g], writes=[gq])
        r.dma("sync", vh_s[:].rearrange("p (kc d) -> p kc d", kc=NKC),
              vh_ap[:, hsl].rearrange("(kc p) d -> p kc d", p=128),
              reads=[vh_g], writes=[gv])
        for x in range(1):
            r.op("tensor", lambda e, kh_s=kh_s: e.matmul(
                C.bank(5), kh_s[:, 0:128], qh_s[:, 0:512], start=True, stop=True),
                reads=[gk, gq], writes=[C.bankg(5)])
            r.op("vector", lambda e: e.tensor_copy(out=c0[:], in_=C.bank(5)),
                 reads=[C.bankg(5)], writes=[gc0])
            r.op("vector", lambda e: e.tensor_mul(out=sqs[:], in0=c0[:], in1=c0[:]),
                 reads=[gc0], writes=[gsq])
            r.op("tensor", lambda e: e.matmul(C.bank(6)[0:1, :], ones_f[:], c0[:],
                                              start=True, stop=True),
                 reads=[gc0, g1c], writes=[C.bankg(6)])
            r.op("tensor", lambda e: e.matmul(C.bank(7)[0:1, :], ones_f[:], sqs[:],
                                              start=True, stop=True),
                 reads=[gsq, g1c], writes=[C.bankg(7)])
            r.op("vector", lambda e: e.reduce_sum(out=strow[0:1, 0:1], in_=C.bank(6)[0:1, :], axis=AX.X),
                 reads=[C.bankg(6)], writes=[gst])
            r.op("vector", lambda e: e.reduce_sum(out=strow[0:1, 1:2], in_=C.bank(7)[0:1, :], axis=AX.X),
                 reads=[C.bankg(7)], writes=[gst])
            kv = C.consts["kvals"]
            kvg = C.g("k_kvals")
            r.op("scalar", lambda e: e.mul(out=strow[0:1, 2:3], in_=strow[0:1, 0:1], mul=inv_n),
                 reads=[gst], writes=[gst])
            r.op("scalar", lambda e: e.mul(out=strow[0:1, 3:4], in_=strow[0:1, 1:2], mul=inv_n),
                 reads=[gst], writes=[gst])
            r.op("scalar", lambda e: e.activation(out=strow[0:1, 4:5], in_=strow[0:1, 2:3], func=AF.Square),
                 reads=[gst], writes=[gst])
            r.op("scalar", lambda e: e.mul(out=strow[0:1, 4:5], in_=strow[0:1, 4:5], mul=-1.0),
                 reads=[gst], writes=[gst])
            r.op("scalar", lambda e: e.activation(out=strow[0:1, 5:6], in_=strow[0:1, 3:4],
                                                  func=AF.Identity, bias=strow[0:1, 4:5]),
                 reads=[gst], writes=[gst])
            r.op("scalar", lambda e, kv=kv: e.activation(out=strow[0:1, 5:6], in_=strow[0:1, 5:6],
                                                         func=AF.Ln, bias=kv[0:1, 0:1]),
                 reads=[gst, kvg], writes=[gst])
            r.op("scalar", lambda e: e.activation(out=strow[0:1, 5:6], in_=strow[0:1, 5:6],
                                                  func=AF.Exp, scale=0.5),
                 reads=[gst], writes=[gst])
            r.op("scalar", lambda e: e.mul(out=strow[0:1, 6:7], in_=strow[0:1, 5:6], mul=LAM),
                 reads=[gst], writes=[gst])
            r.op("scalar", lambda e: e.activation(out=strow[0:1, 3:4], in_=strow[0:1, 2:3],
                                                  func=AF.Identity, bias=strow[0:1, 6:7]),
                 reads=[gst], writes=[gst])
            r.op("scalar", lambda e: e.mul(out=strow[0:1, 7:8], in_=strow[0:1, 3:4], mul=-iscale),
                 reads=[gst], writes=[gst])
            r.op("tensor", lambda e: e.matmul(C.bank(5)[:, 0:1], ones_row_f[:], strow[0:1, 7:8],
                                              start=True, stop=True),
                 reads=[gst, g1r], writes=[C.bankg(5)])
            r.op("vector", lambda e: e.tensor_copy(out=bcol[:], in_=C.bank(5)[:, 0:1]),
                 reads=[C.bankg(5)], writes=[gb])
        pending_tails = []
        for qg in range(C.NTG):
            qpar = qg % 2
            qbuf = qh_s[:, qpar * 512:(qpar + 1) * 512]
            if qg > 0:
                r.dma("sync", qbuf, qh_ap[hsl, qg * 512:(qg + 1) * 512],
                      reads=[qh_g], writes=[gq])
            Rw, gR = Rws[qg % 2]
            rrow, grr = rrows[qg % 2]
            rb, grb = rbs[qg % 2]
            cbank = 2 if qg % 2 == 0 else 5
            for kc in range(0, NKC, 2):
                p0 = 0 if (kc // 2) % 2 == 0 else 6
                for j in range(2):
                    r.op("tensor", lambda e, kc=kc, j=j, p0=p0, qbuf=qbuf, kh_s=kh_s: e.matmul(
                        C.bank(p0 + j), kh_s[:, (kc + j) * 128:(kc + j + 1) * 128], qbuf,
                        start=True, stop=True),
                        reads=[gk, gq], writes=[C.bankg(p0 + j)])
                r.op("scalar", lambda e, kc=kc, p0=p0: e.activation(
                    out=psl[:, kc * 512:(kc + 2) * 512], in_=C.bank(p0, n=2), func=AF.Exp,
                    bias=bcol[:, 0:1], scale=iscale),
                    reads=[C.bankg(p0), C.bankg(p0 + 1), gb],
                    writes=[pgs[kc], pgs[kc + 1]])
                if kc == 0:
                    r.op("vector", lambda e, kc=kc, Rw=Rw: e.tensor_scalar_mul(
                        out=Rw[:], in0=psl[:, kc * 512:(kc + 2) * 512], scalar1=1.0),
                        reads=[pgs[kc], pgs[kc + 1]], writes=[gR])
                else:
                    r.op("vector", lambda e, kc=kc, Rw=Rw: e.scalar_tensor_tensor(
                        out=Rw[:], in0=psl[:, kc * 512:(kc + 2) * 512], scalar=1.0,
                        in1=Rw[:], op0=ALU.mult, op1=ALU.add),
                        reads=[pgs[kc], pgs[kc + 1]], writes=[gR])
            for kc in range(NKC):
                r.op("tensor", lambda e, kc=kc, vh_s=vh_s, cbank=cbank: e.matmul(
                    C.bank(cbank), vh_s[:, kc * 128:(kc + 1) * 128],
                    psl[:, kc * 512:(kc + 1) * 512],
                    start=(kc == 0), stop=(kc == NKC - 1)),
                    reads=[gv, pgs[kc]], writes=[C.bankg(cbank)])
            def tail(h=h, qg=qg, Rw=Rw, gR=gR, rrow=rrow, grr=grr, rb=rb,
                     grb=grb, cbank=cbank):
                r.op("tensor", lambda e: e.matmul(C.bank(3)[0:1, :], ones_f[:], Rw[:, 0:512],
                                                  start=True, stop=False),
                     reads=[gR, g1c], writes=[C.bankg(3)])
                r.op("tensor", lambda e: e.matmul(C.bank(3)[0:1, :], ones_f[:], Rw[:, 512:1024],
                                                  start=False, stop=True),
                     reads=[gR, g1c], writes=[C.bankg(3)])
                r.op("vector", lambda e: e.reciprocal(out=rrow[:], in_=C.bank(3)[0:1, :]),
                     reads=[C.bankg(3)], writes=[grr])
                r.op("tensor", lambda e: e.matmul(C.bank(4), ones_row_f[:], rrow[:],
                                                  start=True, stop=True),
                     reads=[grr, g1r], writes=[C.bankg(4)])
                r.op("vector", lambda e: e.tensor_copy(out=rb[:], in_=C.bank(4)),
                     reads=[C.bankg(4)], writes=[grb])
                r.op("vector", lambda e: e.tensor_mul(
                    out=ctxs[:, h * T + qg * 512: h * T + qg * 512 + 512],
                    in0=C.bank(cbank), in1=rb[:]),
                    reads=[C.bankg(cbank), grb], writes=[gctx])
            pending_tails.append(tail)
            if len(pending_tails) > 1:
                pending_tails.pop(0)()
        for t_ in pending_tails:
            t_()

    C.pop()
    C.push()
    load_w(C, "l1n_w", wo_ap[c], wog, DC, D)
    pqts = rbufs(C, "l1n_pqt", 4, [128, 512], BF16)
    oevs = rbufs(C, "l1n_oev", 4, [128, 512], F32)
    nevs = rbufs(C, "l1n_nev", 4, [128, 512], BF16)
    cnt = [0]

    def epi(oc, tg, bank, bankg):
        i = cnt[0]; cnt[0] += 1
        pqt, gpqt = pqts[i % 4]
        oev, goev = oevs[i % 4]
        nev, gnev = nevs[i % 4]
        tsl = slice(tg * 512, (tg + 1) * 512)
        r.dma("sync", pqt[:], pq_ap[oc * 128:(oc + 1) * 128, tsl], reads=[pq_g], writes=[gpqt])
        evac(C, i, bank, bankg, oev[:], goev,
             scale=ewcols[:, 18 + c:19 + c], scaleg=ewg)
        r.op("vector", lambda e, c=c, pqt=pqt, oev=oev, nev=nev: e.scalar_tensor_tensor(
            out=nev[:], in0=pqt[:], scalar=ewcols[:, 3 * c:3 * c + 1], in1=oev[:],
            op0=ALU.mult, op1=ALU.add),
            reads=[gpqt, ewg, goev], writes=[gnev])
        r.dma("sync", out_hbm[oc * 128:(oc + 1) * 128, tsl], nev[:],
              reads=[gnev], writes=[outg])
    emit_proj(C, "l1a_ctx", "l1n_w", DC, D, epi, banks=(0, 1, 2, 3))
    C.pop()


def emit_final(C, n5_hbm, n5g, lng_ap, lngg, lnb_ap, lnbg, s1_hbm, s1g,
               x1_hbm, x1g, out_hbm, outg):
    r, T = C.r, C.T
    ln5_ap, ln5g = C.dram("sc_ln5", [D, T], F32)
    emit_ln(C, n5_hbm, n5g, ln5_ap, ln5g, eps=1e-6, gamma_beta=(lng_ap, lnb_ap),
            tag="ln5", in_dtype=BF16, out_dtype=F32)
    lch = C.sb("fin_l", [128, 512], F32)
    sch = C.sb("fin_s", [128, 512], F32)
    xch = C.sb("fin_x", [128, 512], F32)
    glc, gsc, gxc = C.g("fin_l"), C.g("fin_s"), C.g("fin_x")
    for tg in range(C.NTG):
        tsl = slice(tg * 512, (tg + 1) * 512)
        for oc in range(DC):
            csl = slice(oc * 128, (oc + 1) * 128)
            r.dma("sync", lch[:], ln5_ap[csl, tsl], reads=[ln5g], writes=[glc])
            r.dma("sync", sch[:], s1_hbm[csl, tsl], reads=[s1g], writes=[gsc])
            r.dma("sync", xch[:], x1_hbm[csl, tsl], reads=[x1g], writes=[gxc])
            r.op("scalar", lambda e: e.activation(out=lch[:], in_=lch[:], func=AF.Silu),
                 reads=[glc], writes=[glc])
            r.op("vector", lambda e: e.tensor_mul(out=lch[:], in0=lch[:], in1=sch[:]),
                 reads=[gsc], writes=[glc])
            r.op("vector", lambda e: e.tensor_add(out=lch[:], in0=lch[:], in1=xch[:]),
                 reads=[gxc], writes=[glc])
            r.dma("sync", out_hbm[csl, tsl], lch[:], reads=[glc], writes=[outg])



def _new(T, consts=None):
    nc = bass.Bass("TRN2", target_bir_lowering=False, debug=False,
                   detect_race_conditions=False)
    stack = ExitStack()
    stack.__enter__()
    C = Ctx(nc, stack, T=T)
    declare_consts(C, only=consts)
    return nc, stack, C


def _ein(C, name, shape, dtype):
    ap = C.nc.dram_tensor(name, shape, dtype, kind="ExternalInput").ap()
    return ap, G(name)


def _eout(C, name, shape, dtype):
    ap = C.nc.dram_tensor(name, shape, dtype, kind="ExternalOutput").ap()
    return ap, G(name)


def build_ln0(T):
    nc, stack, C = _new(T)
    x, xg = _ein(C, "xT", [D, T], F32)
    xn, xng = _eout(C, "xn0T", [D, T], BF16)
    C.push(); emit_ln(C, x, xg, xn, xng, eps=1e-5, tag="ln0"); C.pop()
    C.finish(); stack.__exit__(None, None, None)
    return nc, ["xn0T"]


def build_l0attn(T):
    nc, stack, C = _new(T)
    xn, xng = _ein(C, "xn0T", [D, T], BF16)
    wq, wqg = _ein(C, "w_q", [D, D], F32)
    wk, wkg = _ein(C, "w_k", [D, D], F32)
    wv, wvg = _ein(C, "w_v", [D, D], F32)
    b0, b0g = _ein(C, "b_qkv0", [128, 24], F32)
    s, sg_ = _eout(C, "s0T", [D, T], F32)
    C.push(); emit_l0attn(C, xn, xng, s, sg_, wq, wqg, wk, wkg, wv, wvg, b0, b0g); C.pop()
    C.finish(); stack.__exit__(None, None, None)
    return nc, ["s0T"]


def build_l0ffn(T):
    nc, stack, C = _new(T)
    xn, xng = _ein(C, "xn0T", [D, T], BF16)
    x, xg = _ein(C, "xT", [D, T], F32)
    s, sg_ = _ein(C, "s0T", [D, T], F32)
    rw, rwg = _ein(C, "router_wT", [D, 64], F32)
    rb, rbg = _ein(C, "router_b", [1, 64], F32)
    gw, gwg = _ein(C, "gate_wT", [32, D, 128], F32)
    uw, uwg = _ein(C, "up_wT", [32, D, 128], F32)
    dw, dwg = _ein(C, "down_wT", [32, 128, D], F32)
    sga, sgag = _ein(C, "sgate_wT", [D, 512], F32)
    sua, suag = _ein(C, "sup_wT", [D, 512], F32)
    sda, sdag = _ein(C, "sdown_wT", [512, D], F32)
    x1, x1g = _eout(C, "x1T", [D, T], F32)
    C.push()
    emit_l0ffn(C, xn, xng, x, xg, s, sg_, x1, x1g, rw, rwg, rb, rbg,
               gw, gwg, uw, uwg, dw, dwg, sga, sgag, sua, suag, sda, sdag)
    C.pop(); C.finish(); stack.__exit__(None, None, None)
    return nc, ["x1T"]


def build_ln1(T):
    nc, stack, C = _new(T)
    x, xg = _ein(C, "x1T", [D, T], F32)
    xn, xng = _eout(C, "xn1T", [D, T], BF16)
    C.push(); emit_ln(C, x, xg, xn, xng, eps=1e-5, tag="ln1"); C.pop()
    C.finish(); stack.__exit__(None, None, None)
    return nc, ["xn1T"]


def build_l1qkv(T):
    nc, stack, C = _new(T)
    xn, xng = _ein(C, "xn1T", [D, T], BF16)
    w, wg_ = _ein(C, "wqkv1T", [D, 3072], F32)
    b, bg_ = _ein(C, "b_qkv1", [128, 24], F32)
    qkv, qkvg = _eout(C, "qkv1T", [3072, T], BF16)
    C.push(); emit_l1qkv(C, xn, xng, w, wg_, b, bg_, qkv, qkvg); C.pop()
    C.finish(); stack.__exit__(None, None, None)
    return nc, ["qkv1T"]


def build_l1mix(T):
    nc, stack, C = _new(T)
    qkv, qkvg = _ein(C, "qkv1T", [3072, T], BF16)
    alla, allg = _ein(C, "qkv_all", [8, 3072, T], BF16)
    s1, s1g = _eout(C, "s1T", [D, T], F32)
    C.push(); emit_l1mix(C, qkv, qkvg, alla, allg, s1, s1g); C.pop()
    C.finish(); stack.__exit__(None, None, None)
    return nc, ["s1T"]


def build_node(T, c):
    nc, stack, C = _new(T, consts={"ones_col_f", "ones_row_f", "kvals"})
    spec = NODE_SPEC[c]
    needed = sorted({n_i for (_, n_i) in spec["edges"]})
    node_hbm = {}
    for n_i in needed:
        nm = "xn1T" if n_i == -1 else "n%dT" % n_i
        ap, g_ = _ein(C, nm, [D, T], BF16)
        node_hbm[n_i] = (ap, g_)
    we, weg = _ein(C, "l1f_We", [20, D, D], F32)
    be, beg = _ein(C, "l1f_be", [20, 128, 8], F32)
    wq, wqg = _ein(C, "l1f_Wq", [6, D, D], F32)
    wk, wkg = _ein(C, "l1f_Wk", [6, D, D], F32)
    wv, wvg = _ein(C, "l1f_Wv", [6, D, D], F32)
    wo, wog = _ein(C, "l1f_Wo", [6, D, D], F32)
    ew, ewg_ = _ein(C, "l1f_ew", [1, 18], F32)
    aw, awg = _ein(C, "l1f_actw", [1, 6], F32)
    out, outg = _eout(C, "n%dT" % c, [D, T], BF16)
    C.push()
    ewcols, ewg = emit_l1f_prep(C, ew, ewg_, aw, awg)
    emit_l1ffn_node(C, c, node_hbm, we, weg, be, beg, wq, wqg, wk, wkg,
                    wv, wvg, wo, wog, ewcols, ewg, out, outg)
    C.pop(); C.finish(); stack.__exit__(None, None, None)
    return nc, ["n%dT" % c]


def build_final(T):
    nc, stack, C = _new(T)
    n5, n5g = _ein(C, "n5T", [D, T], BF16)
    lng, lngg = _ein(C, "l1f_lng", [128, 8], F32)
    lnb, lnbg = _ein(C, "l1f_lnb", [128, 8], F32)
    s1, s1g = _ein(C, "s1T", [D, T], F32)
    x1, x1g = _ein(C, "x1T", [D, T], F32)
    out, outg = _eout(C, "outT", [D, T], F32)
    C.push()
    emit_final(C, n5, n5g, lng, lngg, lnb, lnbg, s1, s1g, x1, x1g, out, outg)
    C.pop(); C.finish(); stack.__exit__(None, None, None)
    return nc, ["outT"]


NODE_INPUT_NAMES = {
    c: sorted({"xn1T" if n_i == -1 else "n%dT" % n_i for (_, n_i) in NODE_SPEC[c]["edges"]})
    for c in range(6)
}

PHASES = [
    ("ln0", build_ln0, ["xT"], ["xn0T"]),
    ("l0attn", build_l0attn, ["xn0T", "w_q", "w_k", "w_v", "b_qkv0"], ["s0T"]),
    ("l0ffn", build_l0ffn, ["xn0T", "xT", "s0T", "router_wT", "router_b", "gate_wT",
                            "up_wT", "down_wT", "sgate_wT", "sup_wT", "sdown_wT"], ["x1T"]),
    ("ln1", build_ln1, ["x1T"], ["xn1T"]),
    ("l1qkv", build_l1qkv, ["xn1T", "wqkv1T", "b_qkv1"], ["qkv1T"]),
    ("l1mix", build_l1mix, ["qkv1T", "qkv_all"], ["s1T"]),
] + [
    ("node%d" % c, (lambda T, c=c: build_node(T, c)),
     NODE_INPUT_NAMES[c] + ["l1f_We", "l1f_be", "l1f_Wq", "l1f_Wk", "l1f_Wv",
                            "l1f_Wo", "l1f_ew", "l1f_actw"], ["n%dT" % c])
    for c in range(6)
] + [
    ("final", build_final, ["n5T", "l1f_lng", "l1f_lnb", "s1T", "x1T"], ["outT"]),
]


def build_fused(T):
    nc, stack, C = _new(T)
    r = C.r
    x, xg = _ein(C, "xT", [D, T], F32)
    wq, wqg = _ein(C, "w_q", [D, D], F32)
    wk, wkg = _ein(C, "w_k", [D, D], F32)
    wv, wvg = _ein(C, "w_v", [D, D], F32)
    b0, b0g = _ein(C, "b_qkv0", [128, 24], F32)
    rw, rwg = _ein(C, "router_wT", [D, 64], F32)
    rb, rbg = _ein(C, "router_b", [1, 64], F32)
    gw, gwg = _ein(C, "gate_wT", [32, D, 128], F32)
    uw, uwg = _ein(C, "up_wT", [32, D, 128], F32)
    dw, dwg = _ein(C, "down_wT", [32, 128, D], F32)
    sga, sgag = _ein(C, "sgate_wT", [D, 512], F32)
    sua, suag = _ein(C, "sup_wT", [D, 512], F32)
    sda, sdag = _ein(C, "sdown_wT", [512, D], F32)
    wqkv1, wqkv1g = _ein(C, "wqkv1T", [D, 3072], F32)
    b1, b1g = _ein(C, "b_qkv1", [128, 24], F32)
    we, weg = _ein(C, "l1f_We", [20, D, D], F32)
    be, beg = _ein(C, "l1f_be", [20, 128, 8], F32)
    wq6, wq6g = _ein(C, "l1f_Wq", [6, D, D], F32)
    wk6, wk6g = _ein(C, "l1f_Wk", [6, D, D], F32)
    wv6, wv6g = _ein(C, "l1f_Wv", [6, D, D], F32)
    wo6, wo6g = _ein(C, "l1f_Wo", [6, D, D], F32)
    ew, ewg_ = _ein(C, "l1f_ew", [1, 18], F32)
    aw, awg = _ein(C, "l1f_actw", [1, 6], F32)
    lng, lngg = _ein(C, "l1f_lng", [128, 8], F32)
    lnb, lnbg = _ein(C, "l1f_lnb", [128, 8], F32)
    out, outg = _eout(C, "outT", [D, T], F32)
    xn0, xn0g = C.dram("i_xn0T", [D, T], BF16)
    s0, s0g = C.dram("i_s0T", [D, T], F32)
    x1, x1g = C.dram("i_x1T", [D, T], F32)
    xn1, xn1g = C.dram("i_xn1T", [D, T], BF16)
    qkv_own, qkv_own_g = C.dram("i_qkvT", [3072, T], BF16)
    qkv_all, qkv_all_g = C.dram("i_qkv_all", [8 * 3072, T], BF16)
    s1, s1g = C.dram("i_s1T", [D, T], F32)
    node_hbm = {-1: (xn1, xn1g)}
    for c in range(6):
        node_hbm[c] = C.dram("i_n%dT" % c, [D, T], BF16)

    C.push(); emit_ln(C, x, xg, xn0, xn0g, eps=1e-5, tag="ln0"); C.pop()
    C.push(); emit_l0attn(C, xn0, xn0g, s0, s0g, wq, wqg, wk, wkg, wv, wvg, b0, b0g); C.pop()
    C.push()
    emit_l0ffn(C, xn0, xn0g, x, xg, s0, s0g, x1, x1g, rw, rwg, rb, rbg,
               gw, gwg, uw, uwg, dw, dwg, sga, sgag, sua, suag, sda, sdag)
    C.pop()
    C.push(); emit_ln(C, x1, x1g, xn1, xn1g, eps=1e-5, tag="ln1"); C.pop()
    C.push(); emit_l1qkv(C, xn1, xn1g, wqkv1, wqkv1g, b1, b1g, qkv_own, qkv_own_g); C.pop()
    r.op("gpsimd", lambda e: e.collective_compute(
        "AllGather", ALU.bypass, [list(range(8))], [qkv_own.opt()],
        [qkv_all.opt()]),
        reads=[qkv_own_g], writes=[qkv_all_g], dma=False)
    all3d = qkv_all.rearrange("(t r) c -> t r c", t=8)
    C.push(); emit_l1mix(C, qkv_own, qkv_own_g, all3d, qkv_all_g, s1, s1g); C.pop()
    for c in range(6):
        C.push()
        ewcols, ewcg = emit_l1f_prep(C, ew, ewg_, aw, awg)
        emit_l1ffn_node(C, c, node_hbm, we, weg, be, beg, wq6, wq6g, wk6, wk6g,
                        wv6, wv6g, wo6, wo6g, ewcols, ewcg,
                        node_hbm[c][0], node_hbm[c][1])
        C.pop()
    C.push()
    emit_final(C, node_hbm[5][0], node_hbm[5][1], lng, lngg, lnb, lnbg,
               s1, s1g, x1, x1g, out, outg)
    C.pop()
    C.finish()
    stack.__exit__(None, None, None)
    return nc


def build_fused_a(T):
    nc, stack, C = _new(T)
    x, xg = _ein(C, "xT", [D, T], F32)
    wq, wqg = _ein(C, "w_q", [D, D], F32)
    wk, wkg = _ein(C, "w_k", [D, D], F32)
    wv, wvg = _ein(C, "w_v", [D, D], F32)
    b0, b0g = _ein(C, "b_qkv0", [128, 24], F32)
    rw, rwg = _ein(C, "router_wT", [D, 64], F32)
    rb, rbg = _ein(C, "router_b", [1, 64], F32)
    gw, gwg = _ein(C, "gate_wT", [32, D, 128], F32)
    uw, uwg = _ein(C, "up_wT", [32, D, 128], F32)
    dw, dwg = _ein(C, "down_wT", [32, 128, D], F32)
    sga, sgag = _ein(C, "sgate_wT", [D, 512], F32)
    sua, suag = _ein(C, "sup_wT", [D, 512], F32)
    sda, sdag = _ein(C, "sdown_wT", [512, D], F32)
    wqkv1, wqkv1g = _ein(C, "wqkv1T", [D, 3072], F32)
    b1, b1g = _ein(C, "b_qkv1", [128, 24], F32)
    x1, x1g = _eout(C, "x1T", [D, T], F32)
    xn1, xn1g = _eout(C, "xn1T", [D, T], BF16)
    qkv, qkvg = _eout(C, "qkv1T", [3072, T], BF16)
    xn0, xn0g = C.dram("i_xn0T", [D, T], BF16)
    s0, s0g = C.dram("i_s0T", [D, T], F32)
    C.push(); emit_ln(C, x, xg, xn0, xn0g, eps=1e-5, tag="ln0"); C.pop()
    C.push(); emit_l0attn(C, xn0, xn0g, s0, s0g, wq, wqg, wk, wkg, wv, wvg, b0, b0g); C.pop()
    C.push()
    emit_l0ffn(C, xn0, xn0g, x, xg, s0, s0g, x1, x1g, rw, rwg, rb, rbg,
               gw, gwg, uw, uwg, dw, dwg, sga, sgag, sua, suag, sda, sdag)
    C.pop()
    C.push(); emit_ln(C, x1, x1g, xn1, xn1g, eps=1e-5, tag="ln1"); C.pop()
    C.push(); emit_l1qkv(C, xn1, xn1g, wqkv1, wqkv1g, b1, b1g, qkv, qkvg); C.pop()
    C.finish()
    stack.__exit__(None, None, None)
    return nc


def build_fused_b(T):
    nc, stack, C = _new(T)
    qkv, qkvg = _ein(C, "qkv1T", [3072, T], BF16)
    alla, allg = _ein(C, "qkv_all", [8, 3072, T], BF16)
    xn1, xn1g = _ein(C, "xn1T", [D, T], BF16)
    x1, x1g = _ein(C, "x1T", [D, T], F32)
    we, weg = _ein(C, "l1f_We", [20, D, D], F32)
    be, beg = _ein(C, "l1f_be", [20, 128, 8], F32)
    wq6, wq6g = _ein(C, "l1f_Wq", [6, D, D], F32)
    wk6, wk6g = _ein(C, "l1f_Wk", [6, D, D], F32)
    wv6, wv6g = _ein(C, "l1f_Wv", [6, D, D], F32)
    wo6, wo6g = _ein(C, "l1f_Wo", [6, D, D], F32)
    ew, ewg_ = _ein(C, "l1f_ew", [1, 18], F32)
    aw, awg = _ein(C, "l1f_actw", [1, 6], F32)
    lng, lngg = _ein(C, "l1f_lng", [128, 8], F32)
    lnb, lnbg = _ein(C, "l1f_lnb", [128, 8], F32)
    out, outg = _eout(C, "outT", [D, T], F32)
    s1, s1g = C.dram("i_s1T", [D, T], F32)
    node_hbm = {-1: (xn1, xn1g)}
    for c in range(6):
        node_hbm[c] = C.dram("i_n%dT" % c, [D, T], BF16)
    C.push(); emit_l1mix(C, qkv, qkvg, alla, allg, s1, s1g); C.pop()
    for c in range(6):
        C.push()
        ewcols, ewcg = emit_l1f_prep(C, ew, ewg_, aw, awg)
        emit_l1ffn_node(C, c, node_hbm, we, weg, be, beg, wq6, wq6g, wk6, wk6g,
                        wv6, wv6g, wo6, wo6g, ewcols, ewcg,
                        node_hbm[c][0], node_hbm[c][1])
        C.pop()
    C.push()
    emit_final(C, node_hbm[5][0], node_hbm[5][1], lng, lngg, lnb, lnbg,
               s1, s1g, x1, x1g, out, outg)
    C.pop()
    C.finish()
    stack.__exit__(None, None, None)
    return nc


def kernel(**inputs):
    T = TFULL
    ncore = 8
    consts = host_consts(T)
    nc = build_fused(T)
    in_maps = []
    for i in range(ncore):
        m = dict(consts)
        m.update(host_inputs(i, inputs, T))
        in_maps.append(m)
    res = run_bass_kernel_spmd(nc, in_maps, core_ids=list(range(ncore)))
    out = np.stack([np.ascontiguousarray(np.asarray(res.results[i]["outT"]).T)
                    for i in range(ncore)])
    return out
